# version 06
# baseline (speedup 1.0000x reference)
import contextlib
import math
import numpy as np
import concourse.bass as bass
import concourse.mybir as mybir
from concourse.bass_utils import run_bass_kernel_spmd

F32 = mybir.dt.float32
BF16 = mybir.dt.bfloat16
I32 = mybir.dt.int32
ALU = mybir.AluOpType
ACTF = mybir.ActivationFunctionType
AX = mybir.AxisListType

D_MODEL = 1024
D_INNER = 2048
D_STATE = 16
DT_RANK = 64
HEADS = 8
WINDOW = 512
HID = 2730
HIDP = 2816
QK_SCALE = 8.0
T_TOT = 4096
NEG_BIG = -1e38
DM_T = 8
DI_T = 16
N_HALF = 2
T_HALF = 2048
CH = 512
NCH = T_HALF // CH
TCM = 256
PI = math.pi
TWO_PI_HI = np.float32(2 * PI)
TWO_PI_LO = np.float32(2 * PI - float(np.float32(2 * PI)))
INV_2PI = np.float32(1.0 / (2 * PI))
PI_HALF = np.float32(PI / 2)


class _Sem:
    __slots__ = ("h", "name", "n", "emitted")

    def __init__(self, h, name):
        self.h, self.name, self.n, self.emitted = h, name, 0, 0


class Prog:
    ENGINES = ("sync", "scalar", "vector", "gpsimd", "tensor")

    def __init__(self, nc, stack):
        self.nc, self.stack = nc, stack
        self.prog = {e: [] for e in self.ENGINES}
        self._semid = 0

    def sem(self, name=None):
        name = name or f"s{self._semid}"
        self._semid += 1
        h = self.stack.enter_context(self.nc.semaphore(name))
        return _Sem(h, name)

    def sbuf(self, name, shape, dt):
        return self.stack.enter_context(self.nc.sbuf_tensor(name, shape, dt))

    def psum(self, name, shape, dt=F32):
        return self.stack.enter_context(self.nc.psum_tensor(name, shape, dt))

    SAFE = False
    SAFE_STAGES = set()
    cur_stage = ""

    def emit(self, eng, fn, wait=None, inc=None):
        if (Prog.SAFE or (self.cur_stage in Prog.SAFE_STAGES)) and getattr(self, "allsems", None):
            wait = [(s, s.n) for s in self.allsems]
        if inc is not None:
            sem, k = inc
            sem.n += k
            inc = (sem, sem.n)
        self.prog[eng].append((fn, list(wait or []), inc))

    def wv(self, *sems):
        return [(s, s.n) for s in sems]

    def barrier(self):
        w = [(s, s.n) for s in self.allsems]
        for eng in self.ENGINES:
            self.prog[eng].append(((lambda e: e.nop()), list(w), None))

    def build(self):
        with self.nc.Block() as block:
            for eng_name in self.ENGINES:
                seq = self.prog[eng_name]
                if not seq:
                    continue

                def body(eng, seq=seq):
                    for fn, waits, inc in seq:
                        for s, v in waits:
                            if v > 0:
                                eng.wait_ge(s.h, v)
                        inst = fn(eng)
                        if inc is not None:
                            s, v = inc
                            inst.then_inc(s.h, v - s.emitted)
                            s.emitted = v

                getattr(block, eng_name)(body)



def _colvec(v):
    v = np.asarray(v, np.float32).reshape(-1)
    return np.ascontiguousarray(v.reshape(-1, 128).T)


def _prep_inputs(ix):
    f = np.float32
    pre = {}
    for l in range(2):
        pre[f"wqkv{l}"] = np.ascontiguousarray(ix["L0_ffn_wqkv"][l], f)
        pre[f"wo{l}"] = np.ascontiguousarray(ix["L0_ffn_wo"][l], f)
        w1 = np.asarray(ix["L0_ffn_w1"][l], f)
        w1p = np.zeros((D_MODEL, 2 * HIDP), f)
        w1p[:, :HID] = w1[:, :HID]
        w1p[:, HIDP:HIDP + HID] = w1[:, HID:]
        pre[f"w1{l}"] = w1p
        w2 = np.asarray(ix["L0_ffn_w2"][l], f)
        w2p = np.zeros((HIDP, D_MODEL), f)
        w2p[:HID] = w2
        pre[f"w2{l}"] = w2p
        b1 = np.asarray(ix["L0_ffn_b1"][l], f)
        b1p = np.zeros((2 * HIDP,), f)
        b1p[:HID] = b1[:HID]
        b1p[HIDP:HIDP + HID] = b1[HID:]
        pre[f"b1{l}"] = _colvec(b1p)
        pre[f"b2{l}"] = _colvec(ix["L0_ffn_b2"][l])
        pre[f"lg{l}"] = _colvec(ix["L0_ffn_ln_g"][l])
        pre[f"lb{l}"] = _colvec(ix["L0_ffn_ln_b"][l])
        pre[f"qs{l}"] = np.ascontiguousarray(np.tile(np.asarray(ix["L0_ffn_q_scale"][l], f), 2).reshape(128, 1))
        pre[f"ks{l}"] = np.ascontiguousarray(np.tile(np.asarray(ix["L0_ffn_k_scale"][l], f), 2).reshape(128, 1))
    pre["inT"] = np.ascontiguousarray(np.asarray(ix["L0_attn_in_proj_w"], f).T)
    pre["outT"] = np.ascontiguousarray(np.asarray(ix["L0_attn_out_proj_w"], f).T)
    xpT = np.asarray(ix["L0_attn_x_proj_w"], f).T
    xpp = np.zeros((D_INNER, 112), f)
    xpp[:, 0:80] = xpT[:, 0:80]
    xpp[:, 96:112] = xpT[:, 80:96]
    pre["xpT"] = xpp
    pre["dtT"] = np.ascontiguousarray(np.asarray(ix["L0_attn_dt_proj_w"], f).T)
    cw = np.asarray(ix["L0_attn_conv_w"], f)[:, 0, :]
    pre["convw"] = np.ascontiguousarray(
        np.concatenate([_colvec(cw[:, k]) for k in range(4)], axis=1))
    pre["nw"] = _colvec(ix["L0_attn_norm_weight"])
    pre["cb"] = _colvec(ix["L0_attn_conv_b"])
    pre["dtb"] = _colvec(ix["L0_attn_dt_proj_b"])
    pre["Dvec"] = _colvec(ix["L0_attn_D"])
    al = np.asarray(ix["L0_attn_A_log"], f).reshape(DI_T, 128, D_STATE)
    pre["AlogT"] = np.ascontiguousarray(al.transpose(1, 0, 2).reshape(128, DI_T * D_STATE))
    pre["dimidx"] = np.ascontiguousarray((2.0 * (np.arange(128) % 32)).reshape(128, 1).astype(f))
    pre["ident"] = np.ascontiguousarray(np.eye(128, dtype=f))
    R = np.zeros((128, 128), f)
    for m in range(128):
        if m % 64 < 32:
            R[(m + 32), m] = -1.0
        else:
            R[(m - 32), m] = 1.0
    pre["rotm"] = R
    bm = np.zeros((128, 128), f)
    bm[:64, :64] = 1.0
    bm[64:, 64:] = 1.0
    pre["headmask"] = bm
    sel = np.zeros((128, 16 * 128), f)
    for n in range(16):
        sel[64 + n, n * 128:(n + 1) * 128] = 1.0
        sel[96 + n, n * 128:(n + 1) * 128] = 1.0
    pre["seln"] = sel
    pre["ten_thousand"] = np.full((1, 1), 10000.0, f)
    dimidx = (2.0 * (np.arange(128) % 32)).astype(np.float64)
    l10k = np.log(np.float64(10000.0))
    invf = np.exp(dimidx * np.float64(-1.0 / 64.0) * l10k)
    t_all = np.arange(T_TOT, dtype=np.float64)
    fr = invf[:, None] * t_all[None, :]
    c = np.float64(np.float32(1.0 / (2.0 * math.pi)))
    pre["rotk1"] = np.floor(fr * c + 0.5).astype(f)
    pre["rotk2"] = np.floor(fr * c + 0.75).astype(f)
    return pre


_IN_SHAPES = {
    "hs": [T_TOT, D_MODEL],
    "inT": [D_MODEL, 2 * D_INNER], "outT": [D_INNER, D_MODEL],
    "xpT": [D_INNER, 112], "dtT": [DT_RANK, D_INNER], "convw": [128, 64],
    "nw": [128, 8], "cb": [128, 16], "dtb": [128, 16], "Dvec": [128, 16],
    "AlogT": [128, DI_T * D_STATE], "dimidx": [128, 1], "ident": [128, 128],
    "rotm": [128, 128], "headmask": [128, 128], "seln": [128, 16 * 128],
    "ten_thousand": [1, 1],
    "rotk1": [128, T_TOT], "rotk2": [128, T_TOT],
}
for _l in range(2):
    _IN_SHAPES.update({
        f"wqkv{_l}": [D_MODEL, 1536], f"wo{_l}": [512, D_MODEL],
        f"w1{_l}": [D_MODEL, 2 * HIDP], f"w2{_l}": [HIDP, D_MODEL],
        f"b1{_l}": [128, 44], f"b2{_l}": [128, 8], f"lg{_l}": [128, 8],
        f"lb{_l}": [128, 8], f"qs{_l}": [128, 1], f"ks{_l}": [128, 1],
    })



class Ctx:
    pass


def _reg(cx, byte_off, n_f32):
    assert byte_off % 4 == 0
    c = byte_off // 4
    assert c + n_f32 <= cx.SLAB_COLS, (byte_off, n_f32)
    return cx.slab[:, c:c + n_f32]


def _regb(cx, byte_off, n_bf16):
    assert byte_off % 4 == 0
    c = byte_off // 4
    assert c * 4 + n_bf16 * 2 <= cx.SLAB_COLS * 4
    return cx.slab[:, c:c + (n_bf16 + 1) // 2].bitcast(BF16)[:, :n_bf16]


def setup(p, dram):
    cx = Ctx()
    cx.dram = dram
    cx.ident_f = p.sbuf("ident_f", [128, 128], F32)
    cx.ident_b = p.sbuf("ident_b", [128, 128], BF16)
    cx.rot_b = p.sbuf("rot_b", [128, 128], BF16)
    cx.hmask_b = p.sbuf("hmask_b", [128, 128], BF16)
    cx.triu_b = p.sbuf("triu_b", [128, 128], BF16)
    cx.tril_b = p.sbuf("tril_b", [128, 128], BF16)
    cx.negb_b = p.sbuf("negb_b", [128, 128], BF16)
    cx.ones_b = p.sbuf("ones_b", [128, 1], BF16)
    cx.small = p.sbuf("small", [128, 3072], F32)
    cx.SLAB_COLS = 48592
    cx.slab = p.sbuf("slab", [128, cx.SLAB_COLS], F32)
    cx.PS = p.psum("PS", [128, 3072], F32)
    cx.PSB = p.psum("PSB", [128, 2048], BF16)
    cx.s_ld = p.sem("ld")
    cx.s_st = p.sem("st")
    cx.s_ld2 = p.sem("ld2")
    cx.s_pe = p.sem("pe")
    cx.s_dv = p.sem("dv")
    cx.s_ac = p.sem("ac")
    cx.s_gp = p.sem("gp")
    cx.consts = {}
    cx.n_consts = 0
    p.allsems = [cx.s_ld, cx.s_st, cx.s_ld2, cx.s_pe, cx.s_dv, cx.s_ac, cx.s_gp]
    return cx


SM_B1 = {0: 0, 1: 44}
SM_B2 = {0: 88, 1: 96}
SM_LG = {0: 104, 1: 112}
SM_LB = {0: 120, 1: 128}
SM_QS = {0: 136, 1: 137}
SM_KS = {0: 138, 1: 139}
SM_NW = 140
SM_CB = 148
SM_DTB = 164
SM_DVEC = 180
SM_ANEG = 196
SM_DIMIDX = 452
SM_INVF = 453
SM_LNB = 454
SM_L10K = 455
SM_BASE = 456
SM_MX = 460
SM_NMX = 461
SM_RS = 462
SM_RC = 463
SM_STATE = 464
SM_CONVH = 736
SM_AHALO = 800
SM_ONESR = 840
SM_STAT = 1024
SM_STAT2 = 1536
SM_TMP = 2048


SLAB_BYTES = 48592 * 4

A_F = 0
FF_ABF = 65568
FF_QH = 73856
FF_KF = 90240
FF_VT = 110720
FF_SQB = 131200
FF_TBL = 135296
FF_TSC = 143488
FF_RRS = FF_TSC
FF_Q2 = FF_TSC + 2048
FF_UB = FF_TSC + 4096
FF_WSCR = FF_TSC + 6144
FF_WQ = 151680
FF_WO = 176256
FF_OB = 131200
FF_XCH = FF_QH
FF_XB = FF_VT
FF_PSC = 184448
FF_PT = 187008

FF_SHLN = 65568
FF_W1 = 98336
FF_W2 = 143392
FF_HP = 165920
FF_GLB = 177184
FF_ABF2 = 98336
FF_TMP45 = 106624
FF_SQ45 = 123008

MB_LNM = 65568
MB_SQ = 73760
MB_XCH = 77856
MB_XB = 94240
MB_WIN = 102432
MB_XZS = 167968
MB_LNX = 176160
MB_XIN = 65568
MB_XT = 98560
MB_XTB = 131328
MB_WXP = 147712
M3_DA = 65568
M3_BH = 81952
M3_DT = 98336
M3_UM = 99360
M3_XTG = 100384
M3_ZG = 101408
M3_SZG = 102432
M3_YG = M3_UM
M3_YSB = 103456
M3_WDT = 111648
M3_DTRB = 115744
M3_WOUT = 119840
M3_BBC = 152608
M3_CBC = 168992
MB_XP = 185376

assert FF_PT + 1280 <= SLAB_BYTES
assert FF_GLB + 1024 <= SLAB_BYTES
assert MB_XZS + 8192 <= SLAB_BYTES
assert MB_XP + 8192 <= SLAB_BYTES



def _allsems(cx):
    return (cx.s_ld, cx.s_st, cx.s_ld2, cx.s_pe, cx.s_dv, cx.s_ac, cx.s_gp)


def barrier_wait(p, cx):
    return p.wv(*_allsems(cx))


def dve(p, cx, fn, wait=()):
    p.emit("vector", fn, wait=wait, inc=(cx.s_dv, 1))


def vnop(p, cx):
    p.emit("vector", lambda e: e.nop())


def act(p, cx, fn, wait=()):
    p.emit("scalar", fn, wait=wait, inc=(cx.s_ac, 1))


def pe(p, cx, fn, wait=()):
    p.emit("tensor", fn, wait=wait, inc=(cx.s_pe, 1))


def ldma(p, cx, out, in_, wait=()):
    p.emit("sync", lambda e, o=out, i=in_: e.dma_start(out=o, in_=i), wait=wait, inc=(cx.s_ld, 16))


def gdma(p, cx, out, in_, wait=()):
    p.emit("gpsimd", lambda e, o=out, i=in_: e.dma_start(out=o, in_=i), wait=wait, inc=(cx.s_st, 16))


SM_CONST = 968
SM_CW = 2100


def ccol(p, cx, val):
    val = float(val)
    if val in cx.consts:
        return cx.consts[val]
    idx = SM_CONST + cx.n_consts
    cx.n_consts += 1
    assert idx < 3072
    ap = cx.small[:, idx:idx + 1]
    p.emit("vector", lambda e, a=ap, v=val: e.memset(a, v), inc=(cx.s_dv, 1))
    cx.consts[val] = ap
    return ap


def sm(cx, col, n=1, row0=False):
    t = cx.small[0:1, col:col + n] if row0 else cx.small[:, col:col + n]
    return t



def init_consts(p, cx):
    d = cx.dram
    for v in (1e-5, 1.0, -0.5, -1.0, 0.5):
        ccol(p, cx, v)
    B = lambda: barrier_wait(p, cx)
    ldma(p, cx, cx.ident_f[:, :], d["ident"][:, :])
    gdma(p, cx, cx.ident_b[:, :], d["ident"][:, :])
    gdma(p, cx, cx.rot_b[:, :], d["rotm"][:, :])
    gdma(p, cx, cx.hmask_b[:, :], d["headmask"][:, :])
    for l in range(2):
        ldma(p, cx, cx.small[:, SM_B1[l]:SM_B1[l] + 44], d[f"b1{l}"][:, :])
        ldma(p, cx, cx.small[:, SM_B2[l]:SM_B2[l] + 8], d[f"b2{l}"][:, :])
        ldma(p, cx, cx.small[:, SM_LG[l]:SM_LG[l] + 8], d[f"lg{l}"][:, :])
        ldma(p, cx, cx.small[:, SM_LB[l]:SM_LB[l] + 8], d[f"lb{l}"][:, :])
        ldma(p, cx, cx.small[:, SM_QS[l]:SM_QS[l] + 1], d[f"qs{l}"][:, :])
        ldma(p, cx, cx.small[:, SM_KS[l]:SM_KS[l] + 1], d[f"ks{l}"][:, :])
    ldma(p, cx, cx.small[:, SM_NW:SM_NW + 8], d["nw"][:, :])
    ldma(p, cx, cx.small[:, SM_CB:SM_CB + 16], d["cb"][:, :])
    ldma(p, cx, cx.small[:, SM_DTB:SM_DTB + 16], d["dtb"][:, :])
    ldma(p, cx, cx.small[:, SM_DVEC:SM_DVEC + 16], d["Dvec"][:, :])
    ldma(p, cx, cx.small[:, SM_DIMIDX:SM_DIMIDX + 1], d["dimidx"][:, :])
    ldma(p, cx, cx.small[:, SM_TMP:SM_TMP + 256], d["AlogT"][:, :])
    ldma(p, cx, cx.small[0:1, SM_TMP + 384:SM_TMP + 385], d["ten_thousand"][:, :])
    act(p, cx, lambda e: e.activation(out=sm(cx, SM_TMP, 256), in_=sm(cx, SM_TMP, 256),
                                      func=ACTF.Exp), wait=B())
    dve(p, cx, lambda e: e.tensor_scalar_mul(sm(cx, SM_ANEG, 256), sm(cx, SM_TMP, 256), -1.0),
        wait=B())
    dve(p, cx, lambda e: e.memset(cx.ones_b[:, :], 1.0), wait=B())
    dve(p, cx, lambda e: e.memset(cx.negb_b[:, :], NEG_BIG))
    dve(p, cx, lambda e: e.memset(cx.small[0:1, SM_ONESR:SM_ONESR + 128], 1.0))
    p.emit("gpsimd", lambda e: e.iota(sm(cx, SM_TMP, 128), [[1, 128]], channel_multiplier=0,
                                      allow_small_or_imprecise_dtypes=True), wait=B(), inc=(cx.s_gp, 1))
    p.emit("gpsimd", lambda e: e.iota(sm(cx, SM_TMP + 128, 128), [[0, 128]], channel_multiplier=1,
                                      allow_small_or_imprecise_dtypes=True), inc=(cx.s_gp, 1))
    dve(p, cx, lambda e: e.tensor_tensor(sm(cx, SM_TMP + 256, 128), sm(cx, SM_TMP, 128),
                                         sm(cx, SM_TMP + 128, 128), ALU.is_lt), wait=B())
    dve(p, cx, lambda e: e.tensor_scalar_mul(cx.triu_b[:, :], sm(cx, SM_TMP + 256, 128), NEG_BIG),
        wait=B())
    dve(p, cx, lambda e: e.tensor_tensor(sm(cx, SM_TMP + 256, 128), sm(cx, SM_TMP, 128),
                                         sm(cx, SM_TMP + 128, 128), ALU.is_gt), wait=B())
    dve(p, cx, lambda e: e.tensor_scalar_mul(cx.tril_b[:, :], sm(cx, SM_TMP + 256, 128), NEG_BIG),
        wait=B())
    act(p, cx, lambda e: e.activation(out=cx.small[0:1, SM_TMP + 385:SM_TMP + 386],
                                      in_=cx.small[0:1, SM_TMP + 384:SM_TMP + 385], func=ACTF.Ln),
        wait=B())
    pe(p, cx, lambda e: e.matmul(cx.PS[:, 0:1], cx.small[0:1, SM_ONESR:SM_ONESR + 128],
                                 cx.small[0:1, SM_TMP + 385:SM_TMP + 386], start=True, stop=True),
       wait=B())
    dve(p, cx, lambda e: e.tensor_copy(sm(cx, SM_L10K), cx.PS[:, 0:1]), wait=B())
    vnop(p, cx)
    dve(p, cx, lambda e: e.tensor_scalar_mul(sm(cx, SM_TMP), sm(cx, SM_DIMIDX), -1.0 / 64.0),
        wait=B())
    vnop(p, cx)
    dve(p, cx, lambda e: e.tensor_tensor(sm(cx, SM_TMP + 1, 1), sm(cx, SM_TMP), sm(cx, SM_L10K),
                                         ALU.mult))
    vnop(p, cx)
    act(p, cx, lambda e: e.activation(out=sm(cx, SM_INVF), in_=sm(cx, SM_TMP + 1, 1),
                                      func=ACTF.Exp), wait=B())
    dve(p, cx, lambda e: e.tensor_scalar(sm(cx, SM_BASE), sm(cx, SM_DIMIDX), 25.6, 1.0 / 89.6,
                                         ALU.add, ALU.mult), wait=B())
    act(p, cx, lambda e: e.activation(out=sm(cx, SM_LNB), in_=sm(cx, SM_BASE), func=ACTF.Ln),
        wait=B())
    p.emit("sync", lambda e: e.nop(), wait=B())



def ln_stats(p, cx, xb_tiles, sq_slot_off, shifted=False):
    for g in range(8):
        pe(p, cx, lambda e, g=g: e.matmul(cx.PS[0:1, 2048:2048 + CH], cx.ones_b[:, 0:1],
                                          xb_tiles[g], start=(g == 0), stop=(g == 7)),
           wait=p.wv(cx.s_st, cx.s_dv, cx.s_ld))
        slot = _regb(cx, sq_slot_off + (g % 4) * 1024, CH)
        act(p, cx, lambda e, s=slot, g=g: e.activation(out=s, in_=xb_tiles[g], func=ACTF.Square),
            wait=p.wv(cx.s_pe, cx.s_st, cx.s_ld))
        pe(p, cx, lambda e, s=slot, g=g: e.matmul(cx.PS[0:1, 2560:2560 + CH], cx.ones_b[:, 0:1],
                                                  s, start=(g == 0), stop=(g == 7)),
           wait=p.wv(cx.s_ac))


def ln_rows(p, cx, mamba=False):
    S = 1024
    m_row = cx.small[0:1, S:S + CH]
    r_row = cx.small[0:1, S + 512:S + 512 + CH]
    t1 = cx.small[0:1, S + 1024:S + 1024 + CH]
    w = p.wv(cx.s_pe)
    dve(p, cx, lambda e: e.tensor_scalar_mul(m_row, cx.PS[0:1, 2048:2048 + CH], 1.0 / 1024.0), wait=w)
    dve(p, cx, lambda e: e.tensor_scalar_mul(t1, cx.PS[0:1, 2560:2560 + CH], 1.0 / 1024.0))
    dve(p, cx, lambda e: e.tensor_tensor(r_row, m_row, m_row, ALU.mult))
    dve(p, cx, lambda e: e.tensor_tensor(t1, t1, r_row, ALU.subtract))
    w = p.wv(cx.s_dv)
    act(p, cx, lambda e: e.activation(out=t1, in_=t1, func=ACTF.Ln,
                                      bias=ccol(p, cx, 1e-5)[0:1, :]), wait=w)
    if not mamba:
        act(p, cx, lambda e: e.activation(out=r_row, in_=t1, func=ACTF.Exp,
                                          scale=ccol(p, cx, -0.5)[0:1, :]))
    else:
        r1_row = cx.small[0:1, S + 1536:S + 1536 + CH]
        act(p, cx, lambda e: e.activation(out=r1_row, in_=t1, func=ACTF.Exp,
                                          scale=ccol(p, cx, -0.5)[0:1, :]))
        act(p, cx, lambda e: e.activation(out=r_row, in_=t1, func=ACTF.Exp,
                                          scale=ccol(p, cx, -1.0)[0:1, :]))
        dve(p, cx, lambda e: e.tensor_scalar(r_row, r_row, -1e-5, 1.0, ALU.mult, ALU.add),
            wait=p.wv(cx.s_ac))
        act(p, cx, lambda e: e.activation(out=r_row, in_=r_row, func=ACTF.Ln,
                                          bias=ccol(p, cx, 1e-5)[0:1, :]), wait=p.wv(cx.s_dv))
        dve(p, cx, lambda e: e.tensor_tensor(t1, t1, r_row, ALU.add), wait=p.wv(cx.s_ac))
        act(p, cx, lambda e: e.activation(out=r_row, in_=t1, func=ACTF.Exp,
                                          scale=ccol(p, cx, -0.5)[0:1, :]), wait=p.wv(cx.s_dv))
    w = p.wv(cx.s_ac, cx.s_dv)
    pe(p, cx, lambda e: e.matmul(cx.PS[:, 1024:1024 + CH], cx.small[0:1, SM_ONESR:SM_ONESR + 128],
                                 m_row, start=True, stop=True), wait=w)
    pe(p, cx, lambda e: e.matmul(cx.PS[:, 1536:1536 + CH], cx.small[0:1, SM_ONESR:SM_ONESR + 128],
                                 r_row, start=True, stop=True))
    if mamba:
        r1_row = cx.small[0:1, S + 1536:S + 1536 + CH]
        pe(p, cx, lambda e: e.matmul(cx.PS[:, 0:CH], cx.small[0:1, SM_ONESR:SM_ONESR + 128],
                                     r1_row, start=True, stop=True))


def stage_ln1(p, cx, h, mamba=False, dst="a"):
    d = cx.dram
    p.cur_stage = "ln1"
    for c in range(NCH):
        p.barrier()
        t0 = h * T_HALF + c * CH
        xch = [_reg(cx, (FF_XCH if not mamba else MB_XCH) + g * 2048, CH) for g in range(8)]
        xb = [_regb(cx, (FF_XB if not mamba else MB_XB) + g * 1024, CH) for g in range(8)]
        wld = p.wv(cx.s_dv, cx.s_ac, cx.s_pe, cx.s_st)
        for g in range(8):
            ldma(p, cx, xch[g], d["Xbuf"][g * 128:(g + 1) * 128, t0:t0 + CH], wait=wld)
            gdma(p, cx, xb[g], d["Xbuf"][g * 128:(g + 1) * 128, t0:t0 + CH], wait=wld)
        ln_stats(p, cx, xb, (FF_SQB if not mamba else MB_LNM))
        ln_rows(p, cx, mamba=mamba)
        wmr = p.wv(cx.s_pe)
        for g in range(8):
            dve(p, cx, lambda e, g=g: e.scalar_tensor_tensor(xch[g], cx.PS[:, 1024:1024 + CH],
                                                             -1.0, xch[g], ALU.mult, ALU.add),
                wait=wmr)
            if not mamba:
                dst_ap = cx.slab[:, (A_F // 4) + g * 2049 + c * CH + 1: (A_F // 4) + g * 2049 + c * CH + 1 + CH]
                dve(p, cx, lambda e, g=g, o=dst_ap: e.tensor_tensor(o, xch[g], cx.PS[:, 1536:1536 + CH],
                                                                    ALU.mult))
            else:
                dve(p, cx, lambda e, g=g: e.tensor_tensor(xch[g], xch[g], cx.PS[:, 1536:1536 + CH],
                                                          ALU.mult))
                lnm = _regb(cx, MB_LNM + g * 1024, CH)
                dve(p, cx, lambda e, g=g, o=lnm: e.tensor_scalar_mul(o, xch[g],
                                                                     cx.small[:, SM_NW + g:SM_NW + g + 1]))



def tables_chunk(p, cx, t0):
    FR = _reg(cx, FF_TSC, CH)
    KF_ = _reg(cx, FF_TSC + 2048, CH)
    S1 = _reg(cx, FF_TSC + 4096, CH)
    S2 = _reg(cx, FF_TSC + 6144, CH)
    cs = _reg(cx, FF_TBL, CH)
    sn = _reg(cx, FF_TBL + 2048, CH)
    scq = _reg(cx, FF_TBL + 4096, CH)
    sck = _reg(cx, FF_TBL + 6144, CH)
    w = p.wv(cx.s_dv, cx.s_ac, cx.s_pe)
    p.emit("gpsimd", lambda e: e.iota(FR, [[1, CH]], base=t0, channel_multiplier=0,
                                      allow_small_or_imprecise_dtypes=True), wait=w, inc=(cx.s_gp, 1))
    wg = p.wv(cx.s_gp)
    dve(p, cx, lambda e: e.tensor_scalar_mul(FR, FR, sm(cx, SM_INVF)), wait=wg)
    ldma(p, cx, KF_, cx.dram["rotk1"][:, t0:t0 + CH], wait=p.wv(cx.s_dv, cx.s_ac))
    dve(p, cx, lambda e: e.scalar_tensor_tensor(S1, KF_, -float(TWO_PI_HI), FR, ALU.mult, ALU.add),
        wait=p.wv(cx.s_ld))
    dve(p, cx, lambda e: e.scalar_tensor_tensor(S1, KF_, -float(TWO_PI_LO), S1, ALU.mult, ALU.add))
    act(p, cx, lambda e: e.activation(out=sn, in_=S1, func=ACTF.Sin), wait=p.wv(cx.s_dv))
    ldma(p, cx, KF_, cx.dram["rotk2"][:, t0:t0 + CH], wait=p.wv(cx.s_dv, cx.s_ac))
    dve(p, cx, lambda e: e.scalar_tensor_tensor(S2, KF_, -float(TWO_PI_HI), FR, ALU.mult, ALU.add),
        wait=p.wv(cx.s_ld))
    dve(p, cx, lambda e: e.scalar_tensor_tensor(S2, KF_, -float(TWO_PI_LO), S2, ALU.mult, ALU.add))
    dve(p, cx, lambda e: e.tensor_scalar_add(S2, S2, float(PI_HALF)))
    act(p, cx, lambda e: e.activation(out=cs, in_=S2, func=ACTF.Sin), wait=p.wv(cx.s_dv))
    p.emit("gpsimd", lambda e: e.iota(S1, [[1, CH]], base=t0 - 2048, channel_multiplier=0,
                                      allow_small_or_imprecise_dtypes=True),
           wait=p.wv(cx.s_dv), inc=(cx.s_gp, 1))
    dve(p, cx, lambda e: e.tensor_scalar_mul(S1, S1, 1.0 / 256.0), wait=p.wv(cx.s_gp))
    dve(p, cx, lambda e: e.tensor_scalar_mul(S1, S1, sm(cx, SM_LNB)))
    act(p, cx, lambda e: e.activation(out=scq, in_=S1, func=ACTF.Exp), wait=p.wv(cx.s_dv))
    act(p, cx, lambda e: e.activation(out=sck, in_=S1, func=ACTF.Exp, scale=ccol(p, cx, -1.0)))
    dve(p, cx, lambda e: e.tensor_scalar_mul(scq, scq, 8.0), wait=p.wv(cx.s_ac))
    w = p.wv(cx.s_ac)
    csq = cs
    snq = sn
    csk = _reg(cx, FF_TSC, CH)
    snk = _reg(cx, FF_TSC + 2048, CH)
    dve(p, cx, lambda e: e.tensor_tensor(csk, cs, sck, ALU.mult), wait=w)
    dve(p, cx, lambda e: e.tensor_tensor(snk, sn, sck, ALU.mult))
    dve(p, cx, lambda e: e.tensor_tensor(csq, cs, scq, ALU.mult))
    dve(p, cx, lambda e: e.tensor_tensor(snq, sn, scq, ALU.mult))
    dve(p, cx, lambda e: e.tensor_copy(scq, csk))
    dve(p, cx, lambda e: e.tensor_copy(sck, snk))
    return dict(csq=csq, snq=snq, csk=scq, snk=sck)




def _af(cx, g, c0, n):
    return cx.slab[:, (A_F // 4) + g * 2049 + c0:(A_F // 4) + g * 2049 + c0 + n]


def stage_ffn(p, cx, h, only=None):
    d = cx.dram
    p.cur_stage = "ffn"
    p.barrier()
    wdv = p.wv(cx.s_dv, cx.s_ac, cx.s_pe, cx.s_st, cx.s_ld)
    if h == 0:
        for g in range(8):
            dve(p, cx, lambda e, g=g: e.memset(_af(cx, g, 0, 1), 0.0), wait=wdv if g == 0 else ())
        for g in range(8):
            dve(p, cx, lambda e, o=cx.small[:, SM_AHALO + g:SM_AHALO + g + 1],
                i=_af(cx, g, T_HALF, 1): e.tensor_copy(o, i))
    else:
        for g in range(8):
            dve(p, cx, lambda e, o=_af(cx, g, 0, 1),
                i=cx.small[:, SM_AHALO + g:SM_AHALO + g + 1]: e.tensor_copy(o, i),
                wait=wdv if g == 0 else ())
    for l in range(2):
        stage_attn_layer(p, cx, h, l)
        if only == "attn0":
            return
        stage_mlp_layer(p, cx, h, l)
        if only == "l0":
            return


def stage_attn_layer(p, cx, h, l, p1_only=False):
    d = cx.dram
    p.cur_stage = "attn"
    KB = lambda off, n: _regb(cx, off, n)
    wub = p.wv(cx.s_pe, cx.s_dv, cx.s_ac, cx.s_st, cx.s_ld)
    for kk in range(8):
        gdma(p, cx, KB(FF_WQ + kk * 3072, 1536),
             d[f"wqkv{l}"][kk * 128:(kk + 1) * 128, :], wait=wub)
    for kk in range(4):
        gdma(p, cx, KB(FF_WO + kk * 2048, 1024),
             d[f"wo{l}"][kk * 128:(kk + 1) * 128, :], wait=wub)
    wv0 = p.wv(cx.s_pe, cx.s_ac, cx.s_dv, cx.s_st)
    if h == 0:
        dve(p, cx, lambda e: e.memset(KB(FF_KF, 512), 0.0), wait=wv0)
        dve(p, cx, lambda e: e.memset(KB(FF_KF + 5120, 512), 0.0))
        dve(p, cx, lambda e: e.memset(KB(FF_KF + 10240, 512), 0.0))
        dve(p, cx, lambda e: e.memset(KB(FF_KF + 15360, 512), 0.0))
        for t4 in range(4):
            dve(p, cx, lambda e, t4=t4: e.memset(KB(FF_VT + t4 * 1024, 512), 0.0))
    else:
        for g in range(4):
            ldma(p, cx, KB(FF_KF + g * 5120, 512),
                 d["kcar"][l, g * 128:(g + 1) * 128, :], wait=wv0)
        for t4 in range(4):
            ldma(p, cx, KB(FF_VT + t4 * 1024, 512),
                 d["vcar"][l, t4 * 128:(t4 + 1) * 128, :], wait=wv0)
    for c in range(NCH):
        p.barrier()
        t0g = h * T_HALF + c * CH
        tabs = tables_chunk(p, cx, t0g)
        wab = p.wv(cx.s_pe, cx.s_ac)
        for g in range(8):
            dve(p, cx, lambda e, o=KB(FF_ABF + g * 1032, 513), a=_af(cx, g, c * CH, 513):
                e.tensor_copy(o, a), wait=wab)
        for m in range(12):
            bank = cx.PS[:, (m % 2) * 512:(m % 2) * 512 + 512]
            wmm = p.wv(cx.s_dv, cx.s_ac, cx.s_st)
            for kk in range(8):
                rhs = (KB(FF_ABF + kk * 1032, 513)[:, 1:513] if kk < 4
                       else KB(FF_ABF + kk * 1032, 513)[:, 0:512])
                lhsT = KB(FF_WQ + kk * 3072 + m * 256, 128)
                pe(p, cx, lambda e, o=bank, lt=lhsT, r=rhs, kk=kk:
                   e.matmul(o, lt, r, start=(kk == 0), stop=(kk == 7)), wait=wmm if kk == 0 else ())
            if m < 8:
                isq = m < 4
                tbl_cs = tabs["csq" if isq else "csk"]
                tbl_sn = tabs["snq" if isq else "snk"]
                scol = cx.small[:, SM_QS[l]:SM_QS[l] + 1] if isq else cx.small[:, SM_KS[l]:SM_KS[l] + 1]
                slot = KB(FF_SQB + (m % 4) * 1024, 512)
                act(p, cx, lambda e, s=slot, b=bank: e.activation(out=s, in_=b, func=ACTF.Square),
                    wait=p.wv(cx.s_pe))
                pe(p, cx, lambda e, s=slot: e.matmul(cx.PS[:, 1024:1536], cx.hmask_b[:, :], s,
                                                     start=True, stop=True), wait=p.wv(cx.s_ac, cx.s_dv))
                RRS = _reg(cx, FF_RRS, 512)
                WSC = _reg(cx, FF_WSCR, 512)
                Q2 = _reg(cx, FF_Q2, 512)
                UB = KB(FF_UB, 512)
                dve(p, cx, lambda e: e.tensor_scalar_max(RRS, cx.PS[:, 1024:1536], 1e-24),
                    wait=p.wv(cx.s_pe))
                act(p, cx, lambda e: e.activation(out=WSC, in_=RRS, func=ACTF.Ln), wait=p.wv(cx.s_dv))
                act(p, cx, lambda e: e.activation(out=RRS, in_=WSC, func=ACTF.Exp,
                                                  scale=ccol(p, cx, -0.5)))
                dve(p, cx, lambda e, b=bank, sc=scol: e.scalar_tensor_tensor(Q2, b, sc, RRS,
                                                                             ALU.mult, ALU.mult),
                    wait=p.wv(cx.s_ac))
                dve(p, cx, lambda e, t=tbl_sn: e.tensor_tensor(UB, Q2, t, ALU.mult),
                    wait=p.wv(cx.s_pe))
                pe(p, cx, lambda e: e.matmul(cx.PS[:, 1536:2048], cx.rot_b[:, :], UB,
                                             start=True, stop=True), wait=p.wv(cx.s_dv))
                dve(p, cx, lambda e, t=tbl_cs: e.tensor_tensor(WSC, Q2, t, ALU.mult))
                if isq:
                    dst = KB(FF_QH + m * 4096, 2048)[:, c * CH:(c + 1) * CH]
                else:
                    dst = KB(FF_KF + (m - 4) * 5120, 2560)[:, 512 + c * CH:512 + (c + 1) * CH]
                dve(p, cx, lambda e, o=dst: e.tensor_tensor(o, WSC, cx.PS[:, 1536:2048], ALU.add),
                    wait=p.wv(cx.s_pe, cx.s_st))
            else:
                mv = m - 8
                slot = KB(FF_SQB + (mv % 4) * 1024, 512)
                dve(p, cx, lambda e, s=slot, b=bank: e.tensor_copy(s, b), wait=p.wv(cx.s_pe))
                for b4 in range(4):
                    pe(p, cx, lambda e, s=slot, b4=b4,
                       o=cx.PSB[:, (m % 2) * 512 + b4 * 128:(m % 2) * 512 + (b4 + 1) * 128]:
                       e.transpose(o, s[:, b4 * 128:(b4 + 1) * 128], cx.ident_b[:, :]),
                       wait=p.wv(cx.s_dv))
                for b4 in range(4):
                    dst = KB(FF_VT + (4 + c * 4 + b4) * 1024, 512)[:, mv * 128:(mv + 1) * 128]
                    dve(p, cx, lambda e, o=dst,
                        i=cx.PSB[:, (m % 2) * 512 + b4 * 128:(m % 2) * 512 + (b4 + 1) * 128]:
                        e.tensor_copy(o, i), wait=p.wv(cx.s_pe, cx.s_st))
    if p1_only:
        return
    if h == 0:
        wkv = p.wv(cx.s_dv, cx.s_ac, cx.s_pe)
        for g in range(4):
            gdma(p, cx, d["kcar"][l, g * 128:(g + 1) * 128, :],
                 KB(FF_KF + g * 5120, 2560)[:, T_HALF:T_HALF + 512], wait=wkv)
        for t4 in range(4):
            gdma(p, cx, d["vcar"][l, t4 * 128:(t4 + 1) * 128, :],
                 KB(FF_VT + (16 + t4) * 1024, 512), wait=wkv)
    p.cur_stage = "p2"
    for c in range(NCH):
        p.barrier()
        for s4 in range(4):
            qt = c * 4 + s4
            for hp in range(4):
                for hh in range(2):
                    sb = hh * 1024
                    qh_t = KB(FF_QH + hp * 4096, 2048)
                    kf_t = KB(FF_KF + hp * 5120, 2560)
                    lhsT = qh_t[hh * 64:(hh + 1) * 64, qt * 128:qt * 128 + 128]
                    wmm = p.wv(cx.s_ac, cx.s_dv)
                    pe(p, cx, lambda e, lt=lhsT, r=kf_t[hh * 64:(hh + 1) * 64, qt * 128:qt * 128 + 512], sb=sb:
                       e.matmul(cx.PS[:, sb:sb + 512], lt, r, start=True, stop=False), wait=wmm)
                    if h == 0 and qt < 4:
                        for bb in range(4 - qt):
                            pe(p, cx, lambda e, sb=sb, bb=bb:
                               e.matmul(cx.PS[:, sb + bb * 128:sb + (bb + 1) * 128], cx.ident_b[:, :],
                                        cx.negb_b[:, :], start=False, stop=False))
                    pe(p, cx, lambda e, sb=sb: e.matmul(cx.PS[:, sb:sb + 128], cx.ident_b[:, :],
                                                        cx.triu_b[:, :], start=False, stop=True))
                    pe(p, cx, lambda e, lt=lhsT, r=kf_t[hh * 64:(hh + 1) * 64, qt * 128 + 512:qt * 128 + 640], sb=sb:
                       e.matmul(cx.PS[:, sb + 512:sb + 640], lt, r, start=True, stop=False))
                    pe(p, cx, lambda e, sb=sb: e.matmul(cx.PS[:, sb + 512:sb + 640], cx.ident_b[:, :],
                                                        cx.tril_b[:, :], start=False, stop=True))
                    sl = (qt * 8 + hp * 2 + hh) % 4
                    MX = cx.small[:, 2048 + sl:2049 + sl]
                    NMX = cx.small[:, 2052 + sl:2053 + sl]
                    RS = cx.small[:, 2056 + sl:2057 + sl]
                    RC = cx.small[:, 2060 + sl:2061 + sl]
                    dve(p, cx, lambda e, sb=sb, o=MX: e.tensor_reduce(o, cx.PS[:, sb:sb + 640],
                                                                      AX.X, ALU.max), wait=p.wv(cx.s_pe))
                    vnop(p, cx)
                    dve(p, cx, lambda e, o=NMX, i=MX: e.tensor_scalar_mul(o, i, -1.0),
                        wait=p.wv(cx.s_ac))
                    vnop(p, cx)
                    PSC = KB(FF_PSC + hh * 1280, 640)
                    act(p, cx, lambda e, sb=sb, o=PSC, b=NMX, a=RS:
                        e.activation(out=o, in_=cx.PS[:, sb:sb + 640], func=ACTF.Exp, bias=b,
                                     accum_out=a), wait=p.wv(cx.s_dv, cx.s_pe))
                    dve(p, cx, lambda e, o=RC, i=RS: e.reciprocal(o, i), wait=p.wv(cx.s_ac))
                    vnop(p, cx)
                    dve(p, cx, lambda e, o=PSC, i=RC: e.tensor_scalar_mul(o, o, i))
                    for b5 in range(5):
                        pe(p, cx, lambda e, i=PSC, b5=b5, hh=hh:
                           e.transpose(cx.PSB[:, hh * 1024 + b5 * 128:hh * 1024 + (b5 + 1) * 128],
                                       i[:, b5 * 128:(b5 + 1) * 128], cx.ident_b[:, :]),
                           wait=p.wv(cx.s_dv))
                    PT = KB(FF_PT, 640)
                    dve(p, cx, lambda e, o=PT, hh=hh: e.tensor_copy(o, cx.PSB[:, hh * 1024:hh * 1024 + 640]),
                        wait=p.wv(cx.s_pe))
                    for b5 in range(5):
                        vt_t = KB(FF_VT + (qt + b5) * 1024, 512)
                        lhsT_v = vt_t[:, hp * 128 + hh * 64:hp * 128 + hh * 64 + 64]
                        pe(p, cx, lambda e, lt=lhsT_v, r=PT[:, b5 * 128:(b5 + 1) * 128], hh=hh, b5=b5:
                           e.matmul(cx.PS[hh * 64:(hh + 1) * 64, 2048:2176], lt, r,
                                    start=(b5 == 0), stop=(b5 == 4)),
                           wait=p.wv(cx.s_dv) if b5 == 0 else ())
                ob = KB(FF_OB + hp * 4096, 2048)
                dve(p, cx, lambda e, o=ob[:, qt * 128:(qt + 1) * 128]: e.tensor_copy(o, cx.PS[:, 2048:2176]),
                    wait=p.wv(cx.s_pe))
    p.cur_stage = "wo"
    p.barrier()
    for c in range(NCH):
        for m in range(8):
            bank = cx.PS[:, (m % 2) * 512:(m % 2) * 512 + 512]
            wmm = p.wv(cx.s_dv, cx.s_ac)
            for kk in range(4):
                lhsT = KB(FF_WO + kk * 2048 + m * 256, 128)
                rhs = KB(FF_OB + kk * 4096, 2048)[:, c * CH:(c + 1) * CH]
                pe(p, cx, lambda e, o=bank, lt=lhsT, r=rhs, kk=kk:
                   e.matmul(o, lt, r, start=(kk == 0), stop=(kk == 3)), wait=wmm if kk == 0 else ())
            asl = _af(cx, m, c * CH + 1, CH)
            dve(p, cx, lambda e, o=asl, b=bank: e.tensor_tensor(o, b, o, ALU.add), wait=p.wv(cx.s_pe))
    if h == 0:
        for g in range(8):
            s0 = SM_AHALO + (2 * l + 1) * 8 + g
            dve(p, cx, lambda e, o=cx.small[:, s0:s0 + 1], i=_af(cx, g, T_HALF, 1):
                e.tensor_copy(o, i), wait=p.wv(cx.s_dv) if g == 0 else ())


def stage_mlp_layer(p, cx, h, l):
    d = cx.dram
    p.cur_stage = "mlp"
    KB = lambda off, n: _regb(cx, off, n)
    if h == 1:
        for g in range(8):
            s0 = SM_AHALO + (2 * l + 1) * 8 + g
            dve(p, cx, lambda e, o=_af(cx, g, 0, 1), i=cx.small[:, s0:s0 + 1]:
                e.tensor_copy(o, i), wait=p.wv(cx.s_dv, cx.s_pe) if g == 0 else ())
    p.barrier()
    for cm in range(NCH):
        wab = p.wv(cx.s_pe, cx.s_ac, cx.s_st)
        ab2 = [KB(FF_ABF2 + g * 1032, 513) for g in range(8)]
        for g in range(8):
            dve(p, cx, lambda e, o=ab2[g], a=_af(cx, g, cm * CH, 513):
                e.tensor_copy(o, a), wait=wab)
        sh_ap = [ab2[g][:, 1:513] if g < 4 else ab2[g][:, 0:512] for g in range(8)]
        ln_stats(p, cx, sh_ap, FF_SQ45)
        ln_rows(p, cx)
        wmr = p.wv(cx.s_pe)
        for g in range(8):
            tmp = _reg(cx, FF_TMP45 + g * 2048, 512)
            src = (_af(cx, g, cm * CH + 1, CH) if g < 4 else _af(cx, g, cm * CH, CH))
            dve(p, cx, lambda e, t=tmp, s=src: e.scalar_tensor_tensor(t, cx.PS[:, 1024:1536],
                                                                      -1.0, s, ALU.mult, ALU.add),
                wait=wmr if g == 0 else ())
            dve(p, cx, lambda e, t=tmp: e.tensor_tensor(t, t, cx.PS[:, 1536:2048], ALU.mult))
            dst = KB(FF_SHLN + g * 4096, 2048)[:, cm * CH:(cm + 1) * CH]
            dve(p, cx, lambda e, o=dst, t=tmp, g=g: e.tensor_scalar(
                o, t, cx.small[:, SM_LG[l] + g:SM_LG[l] + g + 1],
                cx.small[:, SM_LB[l] + g:SM_LB[l] + g + 1], ALU.mult, ALU.add))
    for hp2 in range(2):
        wst = p.wv(cx.s_pe, cx.s_dv, cx.s_ac, cx.s_st, cx.s_ld)
        for kk in range(8):
            for side in range(2):
                gdma(p, cx, KB(FF_W1 + (kk * 22 + side * 11) * 256, 1408),
                     d[f"w1{l}"][kk * 128:(kk + 1) * 128,
                                 side * HIDP + hp2 * 1408:side * HIDP + hp2 * 1408 + 1408],
                     wait=wst)
        for kl in range(11):
            gdma(p, cx, KB(FF_W2 + kl * 2048, 1024),
                 d[f"w2{l}"][(hp2 * 11 + kl) * 128:(hp2 * 11 + kl) * 128 + 128, :], wait=wst)
        for cm in range(NCH):
            p.barrier()
            for i in range(11):
                gi = hp2 * 11 + i
                ab_ = cx.PS[:, (i % 2) * 1024:(i % 2) * 1024 + 512]
                gb_ = cx.PS[:, (i % 2) * 1024 + 512:(i % 2) * 1024 + 1024]
                wmm = p.wv(cx.s_dv, cx.s_ac, cx.s_st)
                for kk in range(8):
                    rhs = KB(FF_SHLN + kk * 4096, 2048)[:, cm * CH:(cm + 1) * CH]
                    pe(p, cx, lambda e, o=ab_, lt=KB(FF_W1 + (kk * 22 + i) * 256, 128), r=rhs, kk=kk:
                       e.matmul(o, lt, r, start=(kk == 0), stop=(kk == 7)), wait=wmm if kk == 0 else ())
                for kk in range(8):
                    rhs = KB(FF_SHLN + kk * 4096, 2048)[:, cm * CH:(cm + 1) * CH]
                    pe(p, cx, lambda e, o=gb_, lt=KB(FF_W1 + (kk * 22 + 11 + i) * 256, 128), r=rhs, kk=kk:
                       e.matmul(o, lt, r, start=(kk == 0), stop=(kk == 7)))
                GLB = KB(FF_GLB, 512)
                act(p, cx, lambda e, b=gb_, gi=gi: e.activation(
                    out=GLB, in_=b, func=ACTF.Gelu,
                    bias=cx.small[:, SM_B1[l] + 22 + gi:SM_B1[l] + 22 + gi + 1]),
                    wait=p.wv(cx.s_pe, cx.s_dv))
                hp_t = KB(FF_HP + i * 1024, 512)
                dve(p, cx, lambda e, o=hp_t, b=ab_, gi=gi: e.scalar_tensor_tensor(
                    o, b, cx.small[:, SM_B1[l] + gi:SM_B1[l] + gi + 1], GLB, ALU.add, ALU.mult),
                    wait=p.wv(cx.s_ac, cx.s_pe))
            for mp in range(2):
                for m4 in range(4):
                    m = mp * 4 + m4
                    bank = cx.PS[:, m4 * 512:(m4 + 1) * 512]
                    wmm = p.wv(cx.s_dv, cx.s_ac)
                    for kl in range(11):
                        pe(p, cx, lambda e, o=bank, lt=KB(FF_W2 + kl * 2048 + m * 256, 128),
                           r=KB(FF_HP + kl * 1024, 512), kl=kl:
                           e.matmul(o, lt, r, start=(kl == 0), stop=(kl == 10)),
                           wait=wmm if kl == 0 else ())
                for m4 in range(4):
                    m = mp * 4 + m4
                    bank = cx.PS[:, m4 * 512:(m4 + 1) * 512]
                    asl = _af(cx, m, cm * CH + 1, CH)
                    if hp2 == 0:
                        dve(p, cx, lambda e, o=asl, b=bank: e.tensor_tensor(o, b, o, ALU.add),
                            wait=p.wv(cx.s_pe))
                    else:
                        dve(p, cx, lambda e, o=asl, b=bank, m=m: e.scalar_tensor_tensor(
                            o, b, cx.small[:, SM_B2[l] + m:SM_B2[l] + m + 1], o, ALU.add, ALU.add),
                            wait=p.wv(cx.s_pe))
    if h == 0 and l == 0:
        for g in range(8):
            s0 = SM_AHALO + 16 + g
            dve(p, cx, lambda e, o=cx.small[:, s0:s0 + 1], i=_af(cx, g, T_HALF, 1):
                e.tensor_copy(o, i), wait=p.wv(cx.s_dv) if g == 0 else ())
    if h == 1 and l == 0:
        for g in range(8):
            s0 = SM_AHALO + 16 + g
            dve(p, cx, lambda e, o=_af(cx, g, 0, 1), i=cx.small[:, s0:s0 + 1]:
                e.tensor_copy(o, i), wait=p.wv(cx.s_dv, cx.s_pe) if g == 0 else ())


def stage_combine(p, cx, h):
    d = cx.dram
    p.cur_stage = "combine"
    p.barrier()
    for c in range(NCH):
        t0 = h * T_HALF + c * CH
        for g in range(8):
            xch = _reg(cx, FF_XCH + g * 2048, CH)
            ldma(p, cx, xch, d["Xbuf"][g * 128:(g + 1) * 128, t0:t0 + CH],
                 wait=p.wv(*p.allsems))
            dve(p, cx, lambda e, o=xch, a=_af(cx, g, c * CH + 1, CH): e.scalar_tensor_tensor(
                o, a, 0.5, o, ALU.mult, ALU.add), wait=p.wv(*p.allsems))
            gdma(p, cx, d["Xbuf"][g * 128:(g + 1) * 128, t0:t0 + CH], xch,
                 wait=p.wv(*p.allsems))
    p.emit("sync", lambda e: e.nop(), wait=p.wv(cx.s_st))



def stage_x_in(p, cx):
    d = cx.dram
    tm = FF_QH
    for c in range(T_TOT // 512):
        wld = p.wv(cx.s_pe)
        for s in range(4):
            ldma(p, cx, _reg(cx, tm + s * 4096, 1024),
                 d["hs"][c * 512 + s * 128: c * 512 + (s + 1) * 128, :], wait=wld)
        for g in range(8):
            bank = cx.PS[:, (g % 2) * 512:(g % 2) * 512 + 512]
            for s in range(4):
                src = _reg(cx, tm + s * 4096 + g * 512, 128)
                pe(p, cx, lambda e, o=bank[:, s * 128:(s + 1) * 128], i=src:
                   e.transpose(o, i, cx.ident_f[:, :]), wait=p.wv(cx.s_ld, cx.s_dv))
            dst = _reg(cx, FF_OB + g * 2048, 512)
            dve(p, cx, lambda e, o=dst, i=bank: e.tensor_copy(o, i),
                wait=p.wv(cx.s_pe, cx.s_st))
            gdma(p, cx, d["Xbuf"][g * 128:(g + 1) * 128, c * 512:(c + 1) * 512], dst,
                 wait=p.wv(cx.s_dv))
    p.emit("sync", lambda e: e.nop(), wait=p.wv(cx.s_st))


def stage_x_out(p, cx):
    d = cx.dram
    tm = FF_QH
    fm = FF_OB
    for c in range(T_TOT // 512):
        wg = p.wv(cx.s_st, cx.s_pe)
        for g in range(8):
            ldma(p, cx, _reg(cx, fm + g * 2048, 512),
                 d["Xbuf"][g * 128:(g + 1) * 128, c * 512:(c + 1) * 512], wait=wg)
        for s in range(4):
            bank = cx.PS[:, (s % 2) * 512:(s % 2) * 512 + 512]
            for g in range(8):
                src = _reg(cx, fm + g * 2048 + s * 512, 128)
                pe(p, cx, lambda e, o=bank[:, (g % 4) * 128:(g % 4 + 1) * 128], i=src:
                   e.transpose(o, i, cx.ident_f[:, :]), wait=p.wv(cx.s_ld, cx.s_dv))
                if g % 4 == 3:
                    dst = _reg(cx, tm + s * 4096 + (g - 3) * 512, 512)
                    dve(p, cx, lambda e, o=dst, i=bank: e.tensor_copy(o, i), wait=p.wv(cx.s_pe))
            gdma(p, cx, d["out"][c * 512 + s * 128: c * 512 + (s + 1) * 128, :],
                 _reg(cx, tm + s * 4096, 1024), wait=p.wv(cx.s_dv))
    p.emit("gpsimd", lambda e: e.nop(), wait=p.wv(cx.s_st))


def dump_af_to_xbuf(p, cx, h):
    d = cx.dram
    for g in range(8):
        src = cx.slab[:, (A_F // 4) + g * 2049 + 1:(A_F // 4) + g * 2049 + 1 + T_HALF]
        gdma(p, cx, d["Xbuf"][g * 128:(g + 1) * 128, h * T_HALF:(h + 1) * T_HALF], src,
             wait=p.wv(cx.s_dv, cx.s_pe, cx.s_ac))
    p.emit("sync", lambda e: e.nop(), wait=p.wv(cx.s_st))



def build_program(debug=None):
    nc = bass.Bass("TRN2", target_bir_lowering=False, debug=False, detect_race_conditions=False)
    stack = contextlib.ExitStack()
    with stack:
        p = Prog(nc, stack)
        dram = {}
        for nm, shp in _IN_SHAPES.items():
            dram[nm] = nc.dram_tensor(nm, shp, F32, kind="ExternalInput").ap()
        dram["out"] = nc.dram_tensor("out", [T_TOT, D_MODEL], F32, kind="ExternalOutput").ap()
        for nm, shp in [("Xbuf", [D_MODEL, T_TOT]), ("xin_s", [D_INNER, T_TOT]),
                        ("xt_s", [D_INNER, T_TOT]), ("z_s", [D_INNER, T_TOT])]:
            dram[nm] = nc.dram_tensor(nm, shp, F32).ap()
        dram["lnx_s"] = nc.dram_tensor("lnx_s", [D_MODEL, T_TOT], F32).ap()
        for nm in ("kcar", "vcar"):
            dram[nm] = nc.dram_tensor(nm, [2, 512, 512], BF16).ap()

        cx = setup(p, dram)
        init_consts(p, cx)
        stage_x_in(p, cx)
        if debug == "xcopy":
            pass
        elif debug == "ln1":
            for h in range(N_HALF):
                stage_ln1(p, cx, h)
                dump_af_to_xbuf(p, cx, h)
        elif debug == "consts":
            wd = p.wv(cx.s_dv, cx.s_ac, cx.s_pe, cx.s_gp, cx.s_st, cx.s_ld)
            gdma(p, cx, dram["Xbuf"][0:128, 0:5], cx.small[:, SM_DIMIDX:SM_DIMIDX + 5], wait=wd)
            gdma(p, cx, dram["Xbuf"][0:128, 5:9], cx.small[:, SM_TMP:SM_TMP + 4], wait=wd)
            p.emit("sync", lambda e: e.nop(), wait=p.wv(cx.s_st))
        elif debug and debug.startswith("tbl"):
            tabs = tables_chunk(p, cx, int(debug[3:] or 0))
            for i, k in enumerate(("csq", "snq", "csk", "snk")):
                gdma(p, cx, dram["Xbuf"][i * 128:(i + 1) * 128, 0:CH], tabs[k],
                     wait=p.wv(cx.s_dv, cx.s_ac, cx.s_pe, cx.s_gp))
            p.emit("sync", lambda e: e.nop(), wait=p.wv(cx.s_st))
        elif debug == "p1":
            stage_ln1(p, cx, 0)
            for g in range(8):
                p.emit("vector", lambda e, g=g: e.memset(_af(cx, g, 0, 1), 0.0),
                       wait=p.wv(cx.s_dv, cx.s_ac, cx.s_pe), inc=(cx.s_dv, 1))
            stage_attn_layer(p, cx, 0, 0, p1_only=True)
            wd = p.wv(cx.s_dv, cx.s_ac, cx.s_pe)
            for g in range(4):
                qh = _regb(cx, FF_QH + g * 4096, 2048)
                dve(p, cx, lambda e, g=g, i=qh: e.tensor_copy(_reg(cx, FF_TSC, 512), i[:, 0:512]), wait=wd)
                gdma(p, cx, dram["Xbuf"][g * 128:(g + 1) * 128, 0:512], _reg(cx, FF_TSC, 512),
                     wait=p.wv(cx.s_dv))
            for g in range(4):
                kf = _regb(cx, FF_KF + g * 5120, 2560)
                dve(p, cx, lambda e, g=g, i=kf: e.tensor_copy(_reg(cx, FF_TSC, 512), i[:, 512:1024]),
                    wait=p.wv(cx.s_st))
                gdma(p, cx, dram["Xbuf"][512 + g * 128:512 + (g + 1) * 128, 0:512], _reg(cx, FF_TSC, 512),
                     wait=p.wv(cx.s_dv))
            for s in range(4):
                vt = _regb(cx, FF_VT + (4 + s) * 1024, 512)
                dve(p, cx, lambda e, i=vt: e.tensor_copy(_reg(cx, FF_TSC, 512), i),
                    wait=p.wv(cx.s_st))
                gdma(p, cx, dram["Xbuf"][512 + s * 128:512 + (s + 1) * 128, 512:1024],
                     _reg(cx, FF_TSC, 512), wait=p.wv(cx.s_dv))
            p.emit("sync", lambda e: e.nop(), wait=p.wv(cx.s_st))
        elif debug == "attn0":
            for h in range(N_HALF):
                stage_ln1(p, cx, h)
                stage_ffn(p, cx, h, only="attn0")
                dump_af_to_xbuf(p, cx, h)
        elif debug == "l0":
            for h in range(N_HALF):
                stage_ln1(p, cx, h)
                stage_ffn(p, cx, h, only="l0")
                dump_af_to_xbuf(p, cx, h)
        elif debug == "mamba":
            for h in range(N_HALF):
                stage_mamba(p, cx, h)
        elif debug == "ffn1":
            for h in range(N_HALF):
                stage_ln1(p, cx, h)
                stage_ffn(p, cx, h)
                dump_af_to_xbuf(p, cx, h)
        else:
            full_chain(p, cx)
        stage_x_out(p, cx)
        p.build()
    return nc


def full_chain(p, cx):
    for h in range(N_HALF):
        stage_ln1(p, cx, h)
        stage_ffn(p, cx, h)
        stage_combine(p, cx, h)
    for h in range(N_HALF):
        stage_mamba(p, cx, h)
    for h in range(N_HALF):
        stage_ln1(p, cx, h)
        stage_ffn(p, cx, h)
        stage_combine(p, cx, h)


def kernel(**inputs):
    pre = _prep_inputs(inputs)
    hs_full = np.ascontiguousarray(np.asarray(inputs["L0_attn_hidden_states"], np.float32))
    nc = build_program()
    in_maps = []
    for c in range(8):
        m = dict(pre)
        m["hs"] = hs_full[c // 2]
        in_maps.append(m)
    res = run_bass_kernel_spmd(nc, in_maps, core_ids=list(range(8)))
    outs = [res.results[c]["out"] for c in (0, 2, 4, 6)]
    return np.stack(outs, axis=0)



NM3 = T_HALF // TCM


def stage_mamba(p, cx, h):
    d = cx.dram
    KB = lambda off, n: _regb(cx, off, n)
    RG = lambda off, n: _reg(cx, off, n)
    p.barrier()
    p.cur_stage = "m1"

    wu = p.wv(cx.s_pe, cx.s_dv, cx.s_ac, cx.s_st, cx.s_ld)
    for kk in range(8):
        gdma(p, cx, KB(MB_WIN + kk * 8192, 4096),
             d["inT"][kk * 128:(kk + 1) * 128, :], wait=wu)
    if h == 0:
        dve(p, cx, lambda e: e.memset(cx.small[:, SM_STATE:SM_STATE + 256], 0.0), wait=wu)
    for c in range(NCH):
        p.barrier()
        t0 = h * T_HALF + c * CH
        xch = [RG(MB_XCH + g * 2048, CH) for g in range(8)]
        xb = [KB(MB_XB + g * 1024, CH) for g in range(8)]
        wld = p.wv(cx.s_dv, cx.s_ac, cx.s_pe, cx.s_st)
        for g in range(8):
            ldma(p, cx, xch[g], d["Xbuf"][g * 128:(g + 1) * 128, t0:t0 + CH], wait=wld)
            gdma(p, cx, xb[g], d["Xbuf"][g * 128:(g + 1) * 128, t0:t0 + CH], wait=wld)
        ln_stats(p, cx, xb, MB_SQ)
        ln_rows(p, cx, mamba=True)
        wmr = p.wv(cx.s_pe, cx.s_st)
        for g in range(8):
            lnx = RG(MB_LNX + g * 2048, CH)
            lnm = KB(MB_LNM + g * 1024, CH)
            nwc = cx.small[:, SM_NW + g:SM_NW + g + 1]
            dve(p, cx, lambda e, x_=xch[g]: e.scalar_tensor_tensor(x_, cx.PS[:, 1024:1024 + CH],
                                                                   -1.0, x_, ALU.mult, ALU.add),
                wait=wmr if g == 0 else ())
            dve(p, cx, lambda e, o=lnx, x_=xch[g]: e.tensor_tensor(o, x_, cx.PS[:, 0:CH], ALU.mult))
            dve(p, cx, lambda e, x_=xch[g]: e.tensor_tensor(x_, x_, cx.PS[:, 1536:1536 + CH], ALU.mult))
            dve(p, cx, lambda e, o=lnm, x_=xch[g], w=nwc: e.tensor_scalar_mul(o, x_, w))
            gdma(p, cx, d["lnx_s"][g * 128:(g + 1) * 128, t0:t0 + CH], lnx, wait=p.wv(cx.s_dv))
        for m in range(32):
            bank = cx.PS[:, (m % 2) * 512:(m % 2) * 512 + 512]
            wmm = p.wv(cx.s_dv, cx.s_ac, cx.s_st)
            for kk in range(8):
                lhsT = KB(MB_WIN + kk * 8192 + m * 256, 128)
                pe(p, cx, lambda e, o=bank, lt=lhsT, r=KB(MB_LNM + kk * 1024, CH), kk=kk:
                   e.matmul(o, lt, r, start=(kk == 0), stop=(kk == 7)), wait=wmm if kk == 0 else ())
            slot = RG(MB_XZS + (m % 4) * 2048, CH)
            dve(p, cx, lambda e, o=slot, b=bank: e.tensor_copy(o, b),
                wait=p.wv(cx.s_pe, cx.s_st))
            dst = (d["xin_s"][(m % 16) * 128:(m % 16 + 1) * 128, t0:t0 + CH] if m < 16
                   else d["z_s"][(m - 16) * 128:(m - 15) * 128, t0:t0 + CH])
            gdma(p, cx, dst, slot, wait=p.wv(cx.s_dv))

    p.cur_stage = "m2"
    wu = p.wv(cx.s_pe, cx.s_dv, cx.s_ac, cx.s_st, cx.s_ld)
    p.emit("sync", lambda e: e.nop(), wait=p.wv(cx.s_st))
    ldma(p, cx, cx.small[:, SM_CW:SM_CW + 64], d["convw"][:, :], wait=wu)
    for kk in range(16):
        gdma(p, cx, KB(MB_WXP + kk * 224, 112), d["xpT"][kk * 128:(kk + 1) * 128, :], wait=wu)
    for c in range(NCH):
        p.barrier()
        t0 = h * T_HALF + c * CH
        wld = p.wv(cx.s_dv, cx.s_ac, cx.s_pe, cx.s_st)
        for g in range(16):
            tile = RG(MB_XIN + g * 2060, 515)
            if t0 == 0:
                dve(p, cx, lambda e, o=tile[:, 0:3]: e.memset(o, 0.0), wait=wld if g == 0 else ())
                ldma(p, cx, tile[:, 3:515], d["xin_s"][g * 128:(g + 1) * 128, 0:CH], wait=wld)
            else:
                ldma(p, cx, tile[:, 0:515], d["xin_s"][g * 128:(g + 1) * 128, t0 - 3:t0 + CH],
                     wait=wld)
        for g in range(16):
            tile = RG(MB_XIN + g * 2060, 515)
            xt = RG(MB_XT + g * 2048, CH)
            xtb = KB(MB_XTB + g * 1024, CH)
            wcv = p.wv(cx.s_ld, cx.s_dv, cx.s_ac, cx.s_pe, cx.s_st)
            dve(p, cx, lambda e, o=xt, i=tile, w=cx.small[:, SM_CW + g:SM_CW + g + 1]:
                e.tensor_scalar_mul(o, i[:, 0:CH], w), wait=wcv)
            for k in range(1, 4):
                dve(p, cx, lambda e, o=xt, i=tile, k=k,
                    w=cx.small[:, SM_CW + k * 16 + g:SM_CW + k * 16 + g + 1]:
                    e.scalar_tensor_tensor(o, i[:, k:k + CH], w, o, ALU.mult, ALU.add))
            act(p, cx, lambda e, o=xt, g=g: e.activation(
                out=o, in_=o, func=ACTF.Silu, bias=cx.small[:, SM_CB + g:SM_CB + g + 1]),
                wait=p.wv(cx.s_dv))
            dve(p, cx, lambda e, o=xtb, i=xt: e.tensor_copy(o, i), wait=p.wv(cx.s_ac))
            gdma(p, cx, d["xt_s"][g * 128:(g + 1) * 128, t0:t0 + CH], xt, wait=p.wv(cx.s_dv))
        bank = cx.PS[:, 1024:1536]
        wmm = p.wv(cx.s_dv, cx.s_ac)
        for kk in range(16):
            pe(p, cx, lambda e, lt=KB(MB_WXP + kk * 224, 112), r=KB(MB_XTB + kk * 1024, CH), kk=kk:
               e.matmul(cx.PS[0:112, 1024:1536], lt, r, start=(kk == 0), stop=(kk == 15)),
               wait=wmm if kk == 0 else ())
        dve(p, cx, lambda e, o=RG(MB_XP, 2048)[0:112, c * CH:(c + 1) * CH]:
            e.tensor_copy(o, cx.PS[0:112, 1024:1536]), wait=p.wv(cx.s_pe))

    p.cur_stage = "m3"
    wu = p.wv(cx.s_pe, cx.s_dv, cx.s_ac, cx.s_st, cx.s_ld)
    for kk in range(16):
        gdma(p, cx, KB(M3_WOUT + kk * 2048, 1024), d["outT"][kk * 128:(kk + 1) * 128, :], wait=wu)
    gdma(p, cx, KB(M3_WDT, 2048)[0:64, :], d["dtT"][0:64, :], wait=wu)
    dve(p, cx, lambda e: e.tensor_copy(KB(M3_DTRB, 2048)[0:64, :], RG(MB_XP, 2048)[0:64, :]),
        wait=wu)
    for c8 in range(NM3):
        p.barrier()
        t0 = h * T_HALF + c8 * TCM
        cs = c8 * TCM
        seln = RG(M3_DA, 2048)
        ldma(p, cx, seln, d["seln"][:, :], wait=p.wv(cx.s_dv, cx.s_ac, cx.s_pe, cx.s_st))
        for n in range(16):
            wmm = p.wv(cx.s_dv, cx.s_ac, cx.s_ld)
            pe(p, cx, lambda e, lt=seln[64:80, n * 128:(n + 1) * 128],
               r=RG(MB_XP, 2048)[64:80, cs:cs + TCM]:
               e.matmul(cx.PS[:, 2048:2048 + TCM], lt, r, start=True, stop=True), wait=wmm)
            dve(p, cx, lambda e, o=RG(M3_BBC + n * 1024, TCM): e.tensor_copy(o, cx.PS[:, 2048:2048 + TCM]),
                wait=p.wv(cx.s_pe))
            pe(p, cx, lambda e, lt=seln[96:112, n * 128:(n + 1) * 128],
               r=RG(MB_XP, 2048)[96:112, cs:cs + TCM]:
               e.matmul(cx.PS[:, 2560:2560 + TCM], lt, r, start=True, stop=True,
                        tile_position=(96, 0)))
            dve(p, cx, lambda e, o=RG(M3_CBC + n * 1024, TCM): e.tensor_copy(o, cx.PS[:, 2560:2560 + TCM]),
                wait=p.wv(cx.s_pe))
        for g in range(16):
            DT = RG(M3_DT, TCM)
            UM = RG(M3_UM, TCM)
            XTG = RG(M3_XTG, TCM)
            ZG = RG(M3_ZG, TCM)
            SZG = RG(M3_SZG, TCM)
            DA = RG(M3_DA, 16 * TCM)
            BH = RG(M3_BH, 16 * TCM)
            wld = p.wv(cx.s_dv, cx.s_ac, cx.s_st)
            ldma(p, cx, XTG, d["xt_s"][g * 128:(g + 1) * 128, t0:t0 + TCM], wait=wld)
            ldma(p, cx, ZG, d["z_s"][g * 128:(g + 1) * 128, t0:t0 + TCM], wait=wld)
            wmm = p.wv(cx.s_dv, cx.s_ac)
            pe(p, cx, lambda e, lt=KB(M3_WDT + g * 256, 128)[0:64, :], r=KB(M3_DTRB, 2048)[0:64, cs:cs + TCM]:
               e.matmul(cx.PS[:, 1024:1024 + TCM], lt, r, start=True, stop=True), wait=wmm)
            act(p, cx, lambda e, g=g: e.activation(out=RG(M3_SZG, TCM), in_=cx.PS[:, 1024:1024 + TCM],
                                                   func=ACTF.Exp,
                                                   bias=cx.small[:, SM_DTB + g:SM_DTB + g + 1]),
                wait=p.wv(cx.s_pe))
            act(p, cx, lambda e: e.activation(out=DT, in_=RG(M3_SZG, TCM), func=ACTF.Ln,
                                              bias=ccol(p, cx, 1.0)))
            dve(p, cx, lambda e, o=UM, i=XTG: e.tensor_tensor(o, DT, i, ALU.mult),
                wait=p.wv(cx.s_ac, cx.s_ld))
            for n in range(16):
                acol = cx.small[:, SM_ANEG + g * 16 + n:SM_ANEG + g * 16 + n + 1]
                act(p, cx, lambda e, o=DA[:, n * TCM:(n + 1) * TCM], a=acol:
                    e.activation(out=o, in_=DT, func=ACTF.Exp, scale=a),
                    wait=p.wv(cx.s_dv) if n == 0 else ())
            for n in range(16):
                dve(p, cx, lambda e, o=BH[:, n * TCM:(n + 1) * TCM], b=RG(M3_BBC + n * 1024, TCM):
                    e.tensor_tensor(o, UM, b, ALU.mult), wait=p.wv(cx.s_ac) if n == 0 else ())
            for n in range(16):
                st = cx.small[:, SM_STATE + g * 16 + n:SM_STATE + g * 16 + n + 1]
                sl = BH[:, n * TCM:(n + 1) * TCM]
                dve(p, cx, lambda e, o=sl, a=DA[:, n * TCM:(n + 1) * TCM], b=sl, s=st:
                    e.tensor_tensor_scan(o, a, b, s, ALU.mult, ALU.add))
            dve(p, cx, lambda e, o=cx.small[:, SM_STATE + g * 16:SM_STATE + (g + 1) * 16],
                i=BH[:, TCM - 1::TCM]: e.tensor_copy(o, i))
            dav = DA.rearrange("p (n t) -> p n t", n=16)
            da_t = DA.rearrange("p (t n) -> p t n", n=16)
            for n in range(16):
                dve(p, cx, lambda e, o=da_t[:, :, n], hh_=BH[:, n * TCM:(n + 1) * TCM],
                    cc=RG(M3_CBC + n * 1024, TCM): e.tensor_tensor(o, hh_, cc, ALU.mult))
            dve(p, cx, lambda e: e.tensor_reduce(UM, da_t, AX.X, ALU.add))
            dve(p, cx, lambda e, i=XTG, dc=cx.small[:, SM_DVEC + g:SM_DVEC + g + 1]:
                e.scalar_tensor_tensor(UM, i, dc, UM, ALU.mult, ALU.add))
            act(p, cx, lambda e, i=ZG: e.activation(out=SZG, in_=i, func=ACTF.Silu),
                wait=p.wv(cx.s_dv))
            dve(p, cx, lambda e, o=KB(M3_YSB + g * 512, TCM): e.tensor_tensor(o, UM, SZG, ALU.mult),
                wait=p.wv(cx.s_ac))
        for m in range(8):
            bank = cx.PS[:, (m % 2) * 512:(m % 2) * 512 + 512][:, 0:TCM]
            wmm = p.wv(cx.s_dv, cx.s_ac, cx.s_ld)
            for kk in range(16):
                pe(p, cx, lambda e, o=bank, lt=KB(M3_WOUT + kk * 2048 + m * 256, 128),
                   r=KB(M3_YSB + kk * 512, TCM), kk=kk:
                   e.matmul(o, lt, r, start=(kk == 0), stop=(kk == 15)),
                   wait=wmm if kk == 0 else ())
            xrm = RG(M3_DA + m * 1024, TCM)
            lnxc = RG(M3_DA + 8192 + m * 1024, TCM)
            wrd = p.wv(cx.s_dv, cx.s_ac, cx.s_st)
            ldma(p, cx, xrm, d["Xbuf"][m * 128:(m + 1) * 128, t0:t0 + TCM], wait=wrd)
            ldma(p, cx, lnxc, d["lnx_s"][m * 128:(m + 1) * 128, t0:t0 + TCM], wait=wrd)
            dve(p, cx, lambda e, o=lnxc, b=bank: e.tensor_tensor(o, b, o, ALU.add),
                wait=p.wv(cx.s_pe, cx.s_ld))
            dve(p, cx, lambda e, o=xrm, i=lnxc: e.tensor_tensor(o, i, o, ALU.add))
            gdma(p, cx, d["Xbuf"][m * 128:(m + 1) * 128, t0:t0 + TCM], xrm, wait=p.wv(cx.s_dv))
    p.emit("sync", lambda e: e.nop(), wait=p.wv(cx.s_st))


# version 07
# speedup vs baseline: 1.0045x; 1.0045x over previous
import contextlib
import math
import numpy as np
import concourse.bass as bass
import concourse.mybir as mybir
from concourse.bass_utils import run_bass_kernel_spmd

F32 = mybir.dt.float32
BF16 = mybir.dt.bfloat16
I32 = mybir.dt.int32
ALU = mybir.AluOpType
ACTF = mybir.ActivationFunctionType
AX = mybir.AxisListType

D_MODEL = 1024
D_INNER = 2048
D_STATE = 16
DT_RANK = 64
HEADS = 8
WINDOW = 512
HID = 2730
HIDP = 2816
QK_SCALE = 8.0
T_TOT = 4096
NEG_BIG = -1e38
DM_T = 8
DI_T = 16
N_HALF = 2
T_HALF = 2048
CH = 512
NCH = T_HALF // CH
TCM = 256
PI = math.pi
TWO_PI_HI = np.float32(2 * PI)
TWO_PI_LO = np.float32(2 * PI - float(np.float32(2 * PI)))
INV_2PI = np.float32(1.0 / (2 * PI))
PI_HALF = np.float32(PI / 2)


class _Sem:
    __slots__ = ("h", "name", "n", "emitted")

    def __init__(self, h, name):
        self.h, self.name, self.n, self.emitted = h, name, 0, 0


class Prog:
    ENGINES = ("sync", "scalar", "vector", "gpsimd", "tensor")

    def __init__(self, nc, stack):
        self.nc, self.stack = nc, stack
        self.prog = {e: [] for e in self.ENGINES}
        self._semid = 0

    def sem(self, name=None):
        name = name or f"s{self._semid}"
        self._semid += 1
        h = self.stack.enter_context(self.nc.semaphore(name))
        return _Sem(h, name)

    def sbuf(self, name, shape, dt):
        return self.stack.enter_context(self.nc.sbuf_tensor(name, shape, dt))

    def psum(self, name, shape, dt=F32):
        return self.stack.enter_context(self.nc.psum_tensor(name, shape, dt))

    SAFE = False
    BARRIERS = False
    SAFE_STAGES = set()
    cur_stage = ""

    def emit(self, eng, fn, wait=None, inc=None):
        if (Prog.SAFE or (self.cur_stage in Prog.SAFE_STAGES)) and getattr(self, "allsems", None):
            wait = [(s, s.n) for s in self.allsems]
        if inc is not None:
            sem, k = inc
            sem.n += k
            inc = (sem, sem.n)
        self.prog[eng].append((fn, list(wait or []), inc))

    def wv(self, *sems):
        return [(s, s.n) for s in sems]

    def barrier(self):
        if not Prog.BARRIERS:
            return
        w = [(s, s.n) for s in self.allsems]
        for eng in self.ENGINES:
            self.prog[eng].append(((lambda e: e.nop()), list(w), None))

    def build(self):
        with self.nc.Block() as block:
            for eng_name in self.ENGINES:
                seq = self.prog[eng_name]
                if not seq:
                    continue

                def body(eng, seq=seq):
                    for fn, waits, inc in seq:
                        for s, v in waits:
                            if v > 0:
                                eng.wait_ge(s.h, v)
                        inst = fn(eng)
                        if inc is not None:
                            s, v = inc
                            inst.then_inc(s.h, v - s.emitted)
                            s.emitted = v

                getattr(block, eng_name)(body)



def _colvec(v):
    v = np.asarray(v, np.float32).reshape(-1)
    return np.ascontiguousarray(v.reshape(-1, 128).T)


def _prep_inputs(ix):
    f = np.float32
    pre = {}
    for l in range(2):
        pre[f"wqkv{l}"] = np.ascontiguousarray(ix["L0_ffn_wqkv"][l], f)
        pre[f"wo{l}"] = np.ascontiguousarray(ix["L0_ffn_wo"][l], f)
        w1 = np.asarray(ix["L0_ffn_w1"][l], f)
        w1p = np.zeros((D_MODEL, 2 * HIDP), f)
        w1p[:, :HID] = w1[:, :HID]
        w1p[:, HIDP:HIDP + HID] = w1[:, HID:]
        pre[f"w1{l}"] = w1p
        w2 = np.asarray(ix["L0_ffn_w2"][l], f)
        w2p = np.zeros((HIDP, D_MODEL), f)
        w2p[:HID] = w2
        pre[f"w2{l}"] = w2p
        b1 = np.asarray(ix["L0_ffn_b1"][l], f)
        b1p = np.zeros((2 * HIDP,), f)
        b1p[:HID] = b1[:HID]
        b1p[HIDP:HIDP + HID] = b1[HID:]
        pre[f"b1{l}"] = _colvec(b1p)
        pre[f"b2{l}"] = _colvec(ix["L0_ffn_b2"][l])
        pre[f"lg{l}"] = _colvec(ix["L0_ffn_ln_g"][l])
        pre[f"lb{l}"] = _colvec(ix["L0_ffn_ln_b"][l])
        pre[f"qs{l}"] = np.ascontiguousarray(np.tile(np.asarray(ix["L0_ffn_q_scale"][l], f), 2).reshape(128, 1))
        pre[f"ks{l}"] = np.ascontiguousarray(np.tile(np.asarray(ix["L0_ffn_k_scale"][l], f), 2).reshape(128, 1))
    pre["inT"] = np.ascontiguousarray(np.asarray(ix["L0_attn_in_proj_w"], f).T)
    pre["outT"] = np.ascontiguousarray(np.asarray(ix["L0_attn_out_proj_w"], f).T)
    xpT = np.asarray(ix["L0_attn_x_proj_w"], f).T
    xpp = np.zeros((D_INNER, 112), f)
    xpp[:, 0:80] = xpT[:, 0:80]
    xpp[:, 96:112] = xpT[:, 80:96]
    pre["xpT"] = xpp
    pre["dtT"] = np.ascontiguousarray(np.asarray(ix["L0_attn_dt_proj_w"], f).T)
    cw = np.asarray(ix["L0_attn_conv_w"], f)[:, 0, :]
    pre["convw"] = np.ascontiguousarray(
        np.concatenate([_colvec(cw[:, k]) for k in range(4)], axis=1))
    pre["nw"] = _colvec(ix["L0_attn_norm_weight"])
    pre["cb"] = _colvec(ix["L0_attn_conv_b"])
    pre["dtb"] = _colvec(ix["L0_attn_dt_proj_b"])
    pre["Dvec"] = _colvec(ix["L0_attn_D"])
    al = np.asarray(ix["L0_attn_A_log"], f).reshape(DI_T, 128, D_STATE)
    pre["AlogT"] = np.ascontiguousarray(al.transpose(1, 0, 2).reshape(128, DI_T * D_STATE))
    pre["dimidx"] = np.ascontiguousarray((2.0 * (np.arange(128) % 32)).reshape(128, 1).astype(f))
    pre["ident"] = np.ascontiguousarray(np.eye(128, dtype=f))
    R = np.zeros((128, 128), f)
    for m in range(128):
        if m % 64 < 32:
            R[(m + 32), m] = -1.0
        else:
            R[(m - 32), m] = 1.0
    pre["rotm"] = R
    bm = np.zeros((128, 128), f)
    bm[:64, :64] = 1.0
    bm[64:, 64:] = 1.0
    pre["headmask"] = bm
    sel = np.zeros((128, 16 * 128), f)
    for n in range(16):
        sel[64 + n, n * 128:(n + 1) * 128] = 1.0
        sel[96 + n, n * 128:(n + 1) * 128] = 1.0
    pre["seln"] = sel
    pre["ten_thousand"] = np.full((1, 1), 10000.0, f)
    dimidx = (2.0 * (np.arange(128) % 32)).astype(np.float64)
    l10k = np.log(np.float64(10000.0))
    invf = np.exp(dimidx * np.float64(-1.0 / 64.0) * l10k)
    t_all = np.arange(T_TOT, dtype=np.float64)
    fr = invf[:, None] * t_all[None, :]
    c = np.float64(np.float32(1.0 / (2.0 * math.pi)))
    pre["rotk1"] = np.floor(fr * c + 0.5).astype(f)
    pre["rotk2"] = np.floor(fr * c + 0.75).astype(f)
    return pre


_IN_SHAPES = {
    "hs": [T_TOT, D_MODEL],
    "inT": [D_MODEL, 2 * D_INNER], "outT": [D_INNER, D_MODEL],
    "xpT": [D_INNER, 112], "dtT": [DT_RANK, D_INNER], "convw": [128, 64],
    "nw": [128, 8], "cb": [128, 16], "dtb": [128, 16], "Dvec": [128, 16],
    "AlogT": [128, DI_T * D_STATE], "dimidx": [128, 1], "ident": [128, 128],
    "rotm": [128, 128], "headmask": [128, 128], "seln": [128, 16 * 128],
    "ten_thousand": [1, 1],
    "rotk1": [128, T_TOT], "rotk2": [128, T_TOT],
}
for _l in range(2):
    _IN_SHAPES.update({
        f"wqkv{_l}": [D_MODEL, 1536], f"wo{_l}": [512, D_MODEL],
        f"w1{_l}": [D_MODEL, 2 * HIDP], f"w2{_l}": [HIDP, D_MODEL],
        f"b1{_l}": [128, 44], f"b2{_l}": [128, 8], f"lg{_l}": [128, 8],
        f"lb{_l}": [128, 8], f"qs{_l}": [128, 1], f"ks{_l}": [128, 1],
    })



class Ctx:
    pass


def _reg(cx, byte_off, n_f32):
    assert byte_off % 4 == 0
    c = byte_off // 4
    assert c + n_f32 <= cx.SLAB_COLS, (byte_off, n_f32)
    return cx.slab[:, c:c + n_f32]


def _regb(cx, byte_off, n_bf16):
    assert byte_off % 4 == 0
    c = byte_off // 4
    assert c * 4 + n_bf16 * 2 <= cx.SLAB_COLS * 4
    return cx.slab[:, c:c + (n_bf16 + 1) // 2].bitcast(BF16)[:, :n_bf16]


def setup(p, dram):
    cx = Ctx()
    cx.dram = dram
    cx.ident_f = p.sbuf("ident_f", [128, 128], F32)
    cx.ident_b = p.sbuf("ident_b", [128, 128], BF16)
    cx.rot_b = p.sbuf("rot_b", [128, 128], BF16)
    cx.hmask_b = p.sbuf("hmask_b", [128, 128], BF16)
    cx.triu_b = p.sbuf("triu_b", [128, 128], BF16)
    cx.tril_b = p.sbuf("tril_b", [128, 128], BF16)
    cx.negb_b = p.sbuf("negb_b", [128, 128], BF16)
    cx.ones_b = p.sbuf("ones_b", [128, 1], BF16)
    cx.small = p.sbuf("small", [128, 3072], F32)
    cx.SLAB_COLS = 48592
    cx.slab = p.sbuf("slab", [128, cx.SLAB_COLS], F32)
    cx.PS = p.psum("PS", [128, 3072], F32)
    cx.PSB = p.psum("PSB", [128, 2048], BF16)
    cx.s_ld = p.sem("ld")
    cx.s_st = p.sem("st")
    cx.s_ld2 = p.sem("ld2")
    cx.s_pe = p.sem("pe")
    cx.s_dv = p.sem("dv")
    cx.s_ac = p.sem("ac")
    cx.s_gp = p.sem("gp")
    cx.consts = {}
    cx.n_consts = 0
    p.allsems = [cx.s_ld, cx.s_st, cx.s_ld2, cx.s_pe, cx.s_dv, cx.s_ac, cx.s_gp]
    return cx


SM_B1 = {0: 0, 1: 44}
SM_B2 = {0: 88, 1: 96}
SM_LG = {0: 104, 1: 112}
SM_LB = {0: 120, 1: 128}
SM_QS = {0: 136, 1: 137}
SM_KS = {0: 138, 1: 139}
SM_NW = 140
SM_CB = 148
SM_DTB = 164
SM_DVEC = 180
SM_ANEG = 196
SM_DIMIDX = 452
SM_INVF = 453
SM_LNB = 454
SM_L10K = 455
SM_BASE = 456
SM_MX = 460
SM_NMX = 461
SM_RS = 462
SM_RC = 463
SM_STATE = 464
SM_CONVH = 736
SM_AHALO = 800
SM_ONESR = 840
SM_STAT = 1024
SM_STAT2 = 1536
SM_TMP = 2048


SLAB_BYTES = 48592 * 4

A_F = 0
FF_ABF = 65568
FF_QH = 73856
FF_KF = 90240
FF_VT = 110720
FF_SQB = 131200
FF_TBL = 135296
FF_TSC = 143488
FF_RRS = FF_TSC
FF_Q2 = FF_TSC + 2048
FF_UB = FF_TSC + 4096
FF_WSCR = FF_TSC + 6144
FF_WQ = 151680
FF_WO = 176256
FF_OB = 131200
FF_XCH = FF_QH
FF_XB = FF_VT
FF_PSC = 184448
FF_PT = 187008

FF_SHLN = 65568
FF_W1 = 98336
FF_W2 = 143392
FF_HP = 165920
FF_GLB = 177184
FF_ABF2 = 98336
FF_TMP45 = 106624
FF_SQ45 = 123008

MB_LNM = 65568
MB_SQ = 73760
MB_XCH = 77856
MB_XB = 94240
MB_WIN = 102432
MB_XZS = 167968
MB_LNX = 176160
MB_XIN = 65568
MB_XT = 98560
MB_XTB = 131328
MB_WXP = 147712
M3_DA = 65568
M3_BH = 81952
M3_DT = 98336
M3_UM = 99360
M3_XTG = 100384
M3_ZG = 101408
M3_SZG = 102432
M3_YG = M3_UM
M3_YSB = 103456
M3_WDT = 111648
M3_DTRB = 115744
M3_WOUT = 119840
M3_BBC = 152608
M3_CBC = 168992
MB_XP = 185376

assert FF_PT + 1280 <= SLAB_BYTES
assert FF_GLB + 1024 <= SLAB_BYTES
assert MB_XZS + 8192 <= SLAB_BYTES
assert MB_XP + 8192 <= SLAB_BYTES



def _allsems(cx):
    return (cx.s_ld, cx.s_st, cx.s_ld2, cx.s_pe, cx.s_dv, cx.s_ac, cx.s_gp)


def barrier_wait(p, cx):
    return p.wv(*_allsems(cx))


def dve(p, cx, fn, wait=()):
    p.emit("vector", fn, wait=wait, inc=(cx.s_dv, 1))


def vnop(p, cx):
    p.emit("vector", lambda e: e.nop())


def act(p, cx, fn, wait=()):
    p.emit("scalar", fn, wait=wait, inc=(cx.s_ac, 1))


def pe(p, cx, fn, wait=()):
    p.emit("tensor", fn, wait=wait, inc=(cx.s_pe, 1))


def ldma(p, cx, out, in_, wait=()):
    p.emit("sync", lambda e, o=out, i=in_: e.dma_start(out=o, in_=i), wait=wait, inc=(cx.s_ld, 16))


def gdma(p, cx, out, in_, wait=()):
    p.emit("gpsimd", lambda e, o=out, i=in_: e.dma_start(out=o, in_=i), wait=wait, inc=(cx.s_st, 16))


SM_CONST = 968
SM_CW = 2100


def ccol(p, cx, val):
    val = float(val)
    if val in cx.consts:
        return cx.consts[val]
    idx = SM_CONST + cx.n_consts
    cx.n_consts += 1
    assert idx < 3072
    ap = cx.small[:, idx:idx + 1]
    p.emit("vector", lambda e, a=ap, v=val: e.memset(a, v), inc=(cx.s_dv, 1))
    cx.consts[val] = ap
    return ap


def sm(cx, col, n=1, row0=False):
    t = cx.small[0:1, col:col + n] if row0 else cx.small[:, col:col + n]
    return t



def init_consts(p, cx):
    d = cx.dram
    for v in (1e-5, 1.0, -0.5, -1.0, 0.5):
        ccol(p, cx, v)
    B = lambda: barrier_wait(p, cx)
    ldma(p, cx, cx.ident_f[:, :], d["ident"][:, :])
    gdma(p, cx, cx.ident_b[:, :], d["ident"][:, :])
    gdma(p, cx, cx.rot_b[:, :], d["rotm"][:, :])
    gdma(p, cx, cx.hmask_b[:, :], d["headmask"][:, :])
    for l in range(2):
        ldma(p, cx, cx.small[:, SM_B1[l]:SM_B1[l] + 44], d[f"b1{l}"][:, :])
        ldma(p, cx, cx.small[:, SM_B2[l]:SM_B2[l] + 8], d[f"b2{l}"][:, :])
        ldma(p, cx, cx.small[:, SM_LG[l]:SM_LG[l] + 8], d[f"lg{l}"][:, :])
        ldma(p, cx, cx.small[:, SM_LB[l]:SM_LB[l] + 8], d[f"lb{l}"][:, :])
        ldma(p, cx, cx.small[:, SM_QS[l]:SM_QS[l] + 1], d[f"qs{l}"][:, :])
        ldma(p, cx, cx.small[:, SM_KS[l]:SM_KS[l] + 1], d[f"ks{l}"][:, :])
    ldma(p, cx, cx.small[:, SM_NW:SM_NW + 8], d["nw"][:, :])
    ldma(p, cx, cx.small[:, SM_CB:SM_CB + 16], d["cb"][:, :])
    ldma(p, cx, cx.small[:, SM_DTB:SM_DTB + 16], d["dtb"][:, :])
    ldma(p, cx, cx.small[:, SM_DVEC:SM_DVEC + 16], d["Dvec"][:, :])
    ldma(p, cx, cx.small[:, SM_DIMIDX:SM_DIMIDX + 1], d["dimidx"][:, :])
    ldma(p, cx, cx.small[:, SM_TMP:SM_TMP + 256], d["AlogT"][:, :])
    ldma(p, cx, cx.small[0:1, SM_TMP + 384:SM_TMP + 385], d["ten_thousand"][:, :])
    act(p, cx, lambda e: e.activation(out=sm(cx, SM_TMP, 256), in_=sm(cx, SM_TMP, 256),
                                      func=ACTF.Exp), wait=B())
    dve(p, cx, lambda e: e.tensor_scalar_mul(sm(cx, SM_ANEG, 256), sm(cx, SM_TMP, 256), -1.0),
        wait=B())
    dve(p, cx, lambda e: e.memset(cx.ones_b[:, :], 1.0), wait=B())
    dve(p, cx, lambda e: e.memset(cx.negb_b[:, :], NEG_BIG))
    dve(p, cx, lambda e: e.memset(cx.small[0:1, SM_ONESR:SM_ONESR + 128], 1.0))
    p.emit("gpsimd", lambda e: e.iota(sm(cx, SM_TMP, 128), [[1, 128]], channel_multiplier=0,
                                      allow_small_or_imprecise_dtypes=True), wait=B(), inc=(cx.s_gp, 1))
    p.emit("gpsimd", lambda e: e.iota(sm(cx, SM_TMP + 128, 128), [[0, 128]], channel_multiplier=1,
                                      allow_small_or_imprecise_dtypes=True), inc=(cx.s_gp, 1))
    dve(p, cx, lambda e: e.tensor_tensor(sm(cx, SM_TMP + 256, 128), sm(cx, SM_TMP, 128),
                                         sm(cx, SM_TMP + 128, 128), ALU.is_lt), wait=B())
    dve(p, cx, lambda e: e.tensor_scalar_mul(cx.triu_b[:, :], sm(cx, SM_TMP + 256, 128), NEG_BIG),
        wait=B())
    dve(p, cx, lambda e: e.tensor_tensor(sm(cx, SM_TMP + 256, 128), sm(cx, SM_TMP, 128),
                                         sm(cx, SM_TMP + 128, 128), ALU.is_gt), wait=B())
    dve(p, cx, lambda e: e.tensor_scalar_mul(cx.tril_b[:, :], sm(cx, SM_TMP + 256, 128), NEG_BIG),
        wait=B())
    act(p, cx, lambda e: e.activation(out=cx.small[0:1, SM_TMP + 385:SM_TMP + 386],
                                      in_=cx.small[0:1, SM_TMP + 384:SM_TMP + 385], func=ACTF.Ln),
        wait=B())
    pe(p, cx, lambda e: e.matmul(cx.PS[:, 0:1], cx.small[0:1, SM_ONESR:SM_ONESR + 128],
                                 cx.small[0:1, SM_TMP + 385:SM_TMP + 386], start=True, stop=True),
       wait=B())
    dve(p, cx, lambda e: e.tensor_copy(sm(cx, SM_L10K), cx.PS[:, 0:1]), wait=B())
    vnop(p, cx)
    dve(p, cx, lambda e: e.tensor_scalar_mul(sm(cx, SM_TMP), sm(cx, SM_DIMIDX), -1.0 / 64.0),
        wait=B())
    vnop(p, cx)
    dve(p, cx, lambda e: e.tensor_tensor(sm(cx, SM_TMP + 1, 1), sm(cx, SM_TMP), sm(cx, SM_L10K),
                                         ALU.mult))
    vnop(p, cx)
    act(p, cx, lambda e: e.activation(out=sm(cx, SM_INVF), in_=sm(cx, SM_TMP + 1, 1),
                                      func=ACTF.Exp), wait=B())
    dve(p, cx, lambda e: e.tensor_scalar(sm(cx, SM_BASE), sm(cx, SM_DIMIDX), 25.6, 1.0 / 89.6,
                                         ALU.add, ALU.mult), wait=B())
    act(p, cx, lambda e: e.activation(out=sm(cx, SM_LNB), in_=sm(cx, SM_BASE), func=ACTF.Ln),
        wait=B())
    p.emit("sync", lambda e: e.nop(), wait=B())



def ln_stats(p, cx, xb_tiles, sq_slot_off, shifted=False):
    for g in range(8):
        pe(p, cx, lambda e, g=g: e.matmul(cx.PS[0:1, 2048:2048 + CH], cx.ones_b[:, 0:1],
                                          xb_tiles[g], start=(g == 0), stop=(g == 7)),
           wait=p.wv(cx.s_st, cx.s_dv, cx.s_ld))
        slot = _regb(cx, sq_slot_off + (g % 4) * 1024, CH)
        act(p, cx, lambda e, s=slot, g=g: e.activation(out=s, in_=xb_tiles[g], func=ACTF.Square),
            wait=p.wv(cx.s_pe, cx.s_st, cx.s_ld))
        pe(p, cx, lambda e, s=slot, g=g: e.matmul(cx.PS[0:1, 2560:2560 + CH], cx.ones_b[:, 0:1],
                                                  s, start=(g == 0), stop=(g == 7)),
           wait=p.wv(cx.s_ac))


def ln_rows(p, cx, mamba=False):
    S = 1024
    m_row = cx.small[0:1, S:S + CH]
    r_row = cx.small[0:1, S + 512:S + 512 + CH]
    t1 = cx.small[0:1, S + 1024:S + 1024 + CH]
    w = p.wv(cx.s_pe)
    dve(p, cx, lambda e: e.tensor_scalar_mul(m_row, cx.PS[0:1, 2048:2048 + CH], 1.0 / 1024.0), wait=w)
    dve(p, cx, lambda e: e.tensor_scalar_mul(t1, cx.PS[0:1, 2560:2560 + CH], 1.0 / 1024.0))
    dve(p, cx, lambda e: e.tensor_tensor(r_row, m_row, m_row, ALU.mult))
    dve(p, cx, lambda e: e.tensor_tensor(t1, t1, r_row, ALU.subtract))
    w = p.wv(cx.s_dv)
    act(p, cx, lambda e: e.activation(out=t1, in_=t1, func=ACTF.Ln,
                                      bias=ccol(p, cx, 1e-5)[0:1, :]), wait=w)
    if not mamba:
        act(p, cx, lambda e: e.activation(out=r_row, in_=t1, func=ACTF.Exp,
                                          scale=ccol(p, cx, -0.5)[0:1, :]))
    else:
        r1_row = cx.small[0:1, S + 1536:S + 1536 + CH]
        act(p, cx, lambda e: e.activation(out=r1_row, in_=t1, func=ACTF.Exp,
                                          scale=ccol(p, cx, -0.5)[0:1, :]))
        act(p, cx, lambda e: e.activation(out=r_row, in_=t1, func=ACTF.Exp,
                                          scale=ccol(p, cx, -1.0)[0:1, :]))
        dve(p, cx, lambda e: e.tensor_scalar(r_row, r_row, -1e-5, 1.0, ALU.mult, ALU.add),
            wait=p.wv(cx.s_ac))
        act(p, cx, lambda e: e.activation(out=r_row, in_=r_row, func=ACTF.Ln,
                                          bias=ccol(p, cx, 1e-5)[0:1, :]), wait=p.wv(cx.s_dv))
        dve(p, cx, lambda e: e.tensor_tensor(t1, t1, r_row, ALU.add), wait=p.wv(cx.s_ac))
        act(p, cx, lambda e: e.activation(out=r_row, in_=t1, func=ACTF.Exp,
                                          scale=ccol(p, cx, -0.5)[0:1, :]), wait=p.wv(cx.s_dv))
    w = p.wv(cx.s_ac, cx.s_dv)
    pe(p, cx, lambda e: e.matmul(cx.PS[:, 1024:1024 + CH], cx.small[0:1, SM_ONESR:SM_ONESR + 128],
                                 m_row, start=True, stop=True), wait=w)
    pe(p, cx, lambda e: e.matmul(cx.PS[:, 1536:1536 + CH], cx.small[0:1, SM_ONESR:SM_ONESR + 128],
                                 r_row, start=True, stop=True))
    if mamba:
        r1_row = cx.small[0:1, S + 1536:S + 1536 + CH]
        pe(p, cx, lambda e: e.matmul(cx.PS[:, 0:CH], cx.small[0:1, SM_ONESR:SM_ONESR + 128],
                                     r1_row, start=True, stop=True))


def stage_ln1(p, cx, h, mamba=False, dst="a"):
    d = cx.dram
    p.cur_stage = "ln1"
    for c in range(NCH):
        p.barrier()
        t0 = h * T_HALF + c * CH
        xch = [_reg(cx, (FF_XCH if not mamba else MB_XCH) + g * 2048, CH) for g in range(8)]
        xb = [_regb(cx, (FF_XB if not mamba else MB_XB) + g * 1024, CH) for g in range(8)]
        wld = p.wv(cx.s_dv, cx.s_ac, cx.s_pe, cx.s_st)
        for g in range(8):
            ldma(p, cx, xch[g], d["Xbuf"][g * 128:(g + 1) * 128, t0:t0 + CH], wait=wld)
            gdma(p, cx, xb[g], d["Xbuf"][g * 128:(g + 1) * 128, t0:t0 + CH], wait=wld)
        ln_stats(p, cx, xb, (FF_SQB if not mamba else MB_LNM))
        ln_rows(p, cx, mamba=mamba)
        wmr = p.wv(cx.s_pe)
        for g in range(8):
            dve(p, cx, lambda e, g=g: e.scalar_tensor_tensor(xch[g], cx.PS[:, 1024:1024 + CH],
                                                             -1.0, xch[g], ALU.mult, ALU.add),
                wait=wmr)
            if not mamba:
                dst_ap = cx.slab[:, (A_F // 4) + g * 2049 + c * CH + 1: (A_F // 4) + g * 2049 + c * CH + 1 + CH]
                dve(p, cx, lambda e, g=g, o=dst_ap: e.tensor_tensor(o, xch[g], cx.PS[:, 1536:1536 + CH],
                                                                    ALU.mult))
            else:
                dve(p, cx, lambda e, g=g: e.tensor_tensor(xch[g], xch[g], cx.PS[:, 1536:1536 + CH],
                                                          ALU.mult))
                lnm = _regb(cx, MB_LNM + g * 1024, CH)
                dve(p, cx, lambda e, g=g, o=lnm: e.tensor_scalar_mul(o, xch[g],
                                                                     cx.small[:, SM_NW + g:SM_NW + g + 1]))



def tables_chunk(p, cx, t0):
    FR = _reg(cx, FF_TSC, CH)
    KF_ = _reg(cx, FF_TSC + 2048, CH)
    S1 = _reg(cx, FF_TSC + 4096, CH)
    S2 = _reg(cx, FF_TSC + 6144, CH)
    cs = _reg(cx, FF_TBL, CH)
    sn = _reg(cx, FF_TBL + 2048, CH)
    scq = _reg(cx, FF_TBL + 4096, CH)
    sck = _reg(cx, FF_TBL + 6144, CH)
    w = p.wv(cx.s_dv, cx.s_ac, cx.s_pe)
    p.emit("gpsimd", lambda e: e.iota(FR, [[1, CH]], base=t0, channel_multiplier=0,
                                      allow_small_or_imprecise_dtypes=True), wait=w, inc=(cx.s_gp, 1))
    wg = p.wv(cx.s_gp)
    dve(p, cx, lambda e: e.tensor_scalar_mul(FR, FR, sm(cx, SM_INVF)), wait=wg)
    ldma(p, cx, KF_, cx.dram["rotk1"][:, t0:t0 + CH], wait=p.wv(cx.s_dv, cx.s_ac))
    dve(p, cx, lambda e: e.scalar_tensor_tensor(S1, KF_, -float(TWO_PI_HI), FR, ALU.mult, ALU.add),
        wait=p.wv(cx.s_ld))
    dve(p, cx, lambda e: e.scalar_tensor_tensor(S1, KF_, -float(TWO_PI_LO), S1, ALU.mult, ALU.add))
    act(p, cx, lambda e: e.activation(out=sn, in_=S1, func=ACTF.Sin), wait=p.wv(cx.s_dv))
    ldma(p, cx, KF_, cx.dram["rotk2"][:, t0:t0 + CH], wait=p.wv(cx.s_dv, cx.s_ac))
    dve(p, cx, lambda e: e.scalar_tensor_tensor(S2, KF_, -float(TWO_PI_HI), FR, ALU.mult, ALU.add),
        wait=p.wv(cx.s_ld))
    dve(p, cx, lambda e: e.scalar_tensor_tensor(S2, KF_, -float(TWO_PI_LO), S2, ALU.mult, ALU.add))
    dve(p, cx, lambda e: e.tensor_scalar_add(S2, S2, float(PI_HALF)))
    act(p, cx, lambda e: e.activation(out=cs, in_=S2, func=ACTF.Sin), wait=p.wv(cx.s_dv))
    p.emit("gpsimd", lambda e: e.iota(S1, [[1, CH]], base=t0 - 2048, channel_multiplier=0,
                                      allow_small_or_imprecise_dtypes=True),
           wait=p.wv(cx.s_dv), inc=(cx.s_gp, 1))
    dve(p, cx, lambda e: e.tensor_scalar_mul(S1, S1, 1.0 / 256.0), wait=p.wv(cx.s_gp))
    dve(p, cx, lambda e: e.tensor_scalar_mul(S1, S1, sm(cx, SM_LNB)))
    act(p, cx, lambda e: e.activation(out=scq, in_=S1, func=ACTF.Exp), wait=p.wv(cx.s_dv))
    act(p, cx, lambda e: e.activation(out=sck, in_=S1, func=ACTF.Exp, scale=ccol(p, cx, -1.0)))
    dve(p, cx, lambda e: e.tensor_scalar_mul(scq, scq, 8.0), wait=p.wv(cx.s_ac))
    w = p.wv(cx.s_ac)
    csq = cs
    snq = sn
    csk = _reg(cx, FF_TSC, CH)
    snk = _reg(cx, FF_TSC + 2048, CH)
    dve(p, cx, lambda e: e.tensor_tensor(csk, cs, sck, ALU.mult), wait=w)
    dve(p, cx, lambda e: e.tensor_tensor(snk, sn, sck, ALU.mult))
    dve(p, cx, lambda e: e.tensor_tensor(csq, cs, scq, ALU.mult))
    dve(p, cx, lambda e: e.tensor_tensor(snq, sn, scq, ALU.mult))
    dve(p, cx, lambda e: e.tensor_copy(scq, csk))
    dve(p, cx, lambda e: e.tensor_copy(sck, snk))
    return dict(csq=csq, snq=snq, csk=scq, snk=sck)




def _af(cx, g, c0, n):
    return cx.slab[:, (A_F // 4) + g * 2049 + c0:(A_F // 4) + g * 2049 + c0 + n]


def stage_ffn(p, cx, h, only=None):
    d = cx.dram
    p.cur_stage = "ffn"
    p.barrier()
    wdv = p.wv(cx.s_dv, cx.s_ac, cx.s_pe, cx.s_st, cx.s_ld)
    if h == 0:
        for g in range(8):
            dve(p, cx, lambda e, g=g: e.memset(_af(cx, g, 0, 1), 0.0), wait=wdv if g == 0 else ())
        for g in range(8):
            dve(p, cx, lambda e, o=cx.small[:, SM_AHALO + g:SM_AHALO + g + 1],
                i=_af(cx, g, T_HALF, 1): e.tensor_copy(o, i))
    else:
        for g in range(8):
            dve(p, cx, lambda e, o=_af(cx, g, 0, 1),
                i=cx.small[:, SM_AHALO + g:SM_AHALO + g + 1]: e.tensor_copy(o, i),
                wait=wdv if g == 0 else ())
    for l in range(2):
        stage_attn_layer(p, cx, h, l)
        if only == "attn0":
            return
        stage_mlp_layer(p, cx, h, l)
        if only == "l0":
            return


def stage_attn_layer(p, cx, h, l, p1_only=False):
    d = cx.dram
    p.cur_stage = "attn"
    KB = lambda off, n: _regb(cx, off, n)
    wub = p.wv(cx.s_pe, cx.s_dv, cx.s_ac, cx.s_st, cx.s_ld)
    for kk in range(8):
        gdma(p, cx, KB(FF_WQ + kk * 3072, 1536),
             d[f"wqkv{l}"][kk * 128:(kk + 1) * 128, :], wait=wub)
    for kk in range(4):
        gdma(p, cx, KB(FF_WO + kk * 2048, 1024),
             d[f"wo{l}"][kk * 128:(kk + 1) * 128, :], wait=wub)
    wv0 = p.wv(cx.s_pe, cx.s_ac, cx.s_dv, cx.s_st)
    if h == 0:
        dve(p, cx, lambda e: e.memset(KB(FF_KF, 512), 0.0), wait=wv0)
        dve(p, cx, lambda e: e.memset(KB(FF_KF + 5120, 512), 0.0))
        dve(p, cx, lambda e: e.memset(KB(FF_KF + 10240, 512), 0.0))
        dve(p, cx, lambda e: e.memset(KB(FF_KF + 15360, 512), 0.0))
        for t4 in range(4):
            dve(p, cx, lambda e, t4=t4: e.memset(KB(FF_VT + t4 * 1024, 512), 0.0))
    else:
        for g in range(4):
            ldma(p, cx, KB(FF_KF + g * 5120, 512),
                 d["kcar"][l, g * 128:(g + 1) * 128, :], wait=wv0)
        for t4 in range(4):
            ldma(p, cx, KB(FF_VT + t4 * 1024, 512),
                 d["vcar"][l, t4 * 128:(t4 + 1) * 128, :], wait=wv0)
    for c in range(NCH):
        p.barrier()
        t0g = h * T_HALF + c * CH
        tabs = tables_chunk(p, cx, t0g)
        wab = p.wv(cx.s_pe, cx.s_ac)
        for g in range(8):
            dve(p, cx, lambda e, o=KB(FF_ABF + g * 1032, 513), a=_af(cx, g, c * CH, 513):
                e.tensor_copy(o, a), wait=wab)
        for m in range(12):
            bank = cx.PS[:, (m % 2) * 512:(m % 2) * 512 + 512]
            wmm = p.wv(cx.s_dv, cx.s_ac, cx.s_st)
            for kk in range(8):
                rhs = (KB(FF_ABF + kk * 1032, 513)[:, 1:513] if kk < 4
                       else KB(FF_ABF + kk * 1032, 513)[:, 0:512])
                lhsT = KB(FF_WQ + kk * 3072 + m * 256, 128)
                pe(p, cx, lambda e, o=bank, lt=lhsT, r=rhs, kk=kk:
                   e.matmul(o, lt, r, start=(kk == 0), stop=(kk == 7)), wait=wmm if kk == 0 else ())
            if m < 8:
                isq = m < 4
                tbl_cs = tabs["csq" if isq else "csk"]
                tbl_sn = tabs["snq" if isq else "snk"]
                scol = cx.small[:, SM_QS[l]:SM_QS[l] + 1] if isq else cx.small[:, SM_KS[l]:SM_KS[l] + 1]
                slot = KB(FF_SQB + (m % 4) * 1024, 512)
                act(p, cx, lambda e, s=slot, b=bank: e.activation(out=s, in_=b, func=ACTF.Square),
                    wait=p.wv(cx.s_pe))
                pe(p, cx, lambda e, s=slot: e.matmul(cx.PS[:, 1024:1536], cx.hmask_b[:, :], s,
                                                     start=True, stop=True), wait=p.wv(cx.s_ac, cx.s_dv))
                RRS = _reg(cx, FF_RRS, 512)
                WSC = _reg(cx, FF_WSCR, 512)
                Q2 = _reg(cx, FF_Q2, 512)
                UB = KB(FF_UB, 512)
                dve(p, cx, lambda e: e.tensor_scalar_max(RRS, cx.PS[:, 1024:1536], 1e-24),
                    wait=p.wv(cx.s_pe))
                act(p, cx, lambda e: e.activation(out=WSC, in_=RRS, func=ACTF.Ln), wait=p.wv(cx.s_dv))
                act(p, cx, lambda e: e.activation(out=RRS, in_=WSC, func=ACTF.Exp,
                                                  scale=ccol(p, cx, -0.5)))
                dve(p, cx, lambda e, b=bank, sc=scol: e.scalar_tensor_tensor(Q2, b, sc, RRS,
                                                                             ALU.mult, ALU.mult),
                    wait=p.wv(cx.s_ac))
                dve(p, cx, lambda e, t=tbl_sn: e.tensor_tensor(UB, Q2, t, ALU.mult),
                    wait=p.wv(cx.s_pe))
                pe(p, cx, lambda e: e.matmul(cx.PS[:, 1536:2048], cx.rot_b[:, :], UB,
                                             start=True, stop=True), wait=p.wv(cx.s_dv))
                dve(p, cx, lambda e, t=tbl_cs: e.tensor_tensor(WSC, Q2, t, ALU.mult))
                if isq:
                    dst = KB(FF_QH + m * 4096, 2048)[:, c * CH:(c + 1) * CH]
                else:
                    dst = KB(FF_KF + (m - 4) * 5120, 2560)[:, 512 + c * CH:512 + (c + 1) * CH]
                dve(p, cx, lambda e, o=dst: e.tensor_tensor(o, WSC, cx.PS[:, 1536:2048], ALU.add),
                    wait=p.wv(cx.s_pe, cx.s_st))
            else:
                mv = m - 8
                slot = KB(FF_SQB + (mv % 4) * 1024, 512)
                dve(p, cx, lambda e, s=slot, b=bank: e.tensor_copy(s, b), wait=p.wv(cx.s_pe))
                for b4 in range(4):
                    pe(p, cx, lambda e, s=slot, b4=b4,
                       o=cx.PSB[:, (m % 2) * 512 + b4 * 128:(m % 2) * 512 + (b4 + 1) * 128]:
                       e.transpose(o, s[:, b4 * 128:(b4 + 1) * 128], cx.ident_b[:, :]),
                       wait=p.wv(cx.s_dv))
                for b4 in range(4):
                    dst = KB(FF_VT + (4 + c * 4 + b4) * 1024, 512)[:, mv * 128:(mv + 1) * 128]
                    dve(p, cx, lambda e, o=dst,
                        i=cx.PSB[:, (m % 2) * 512 + b4 * 128:(m % 2) * 512 + (b4 + 1) * 128]:
                        e.tensor_copy(o, i), wait=p.wv(cx.s_pe, cx.s_st))
    if p1_only:
        return
    if h == 0:
        wkv = p.wv(cx.s_dv, cx.s_ac, cx.s_pe)
        for g in range(4):
            gdma(p, cx, d["kcar"][l, g * 128:(g + 1) * 128, :],
                 KB(FF_KF + g * 5120, 2560)[:, T_HALF:T_HALF + 512], wait=wkv)
        for t4 in range(4):
            gdma(p, cx, d["vcar"][l, t4 * 128:(t4 + 1) * 128, :],
                 KB(FF_VT + (16 + t4) * 1024, 512), wait=wkv)
    p.cur_stage = "p2"
    for c in range(NCH):
        p.barrier()
        for s4 in range(4):
            qt = c * 4 + s4
            for hp in range(4):
                for hh in range(2):
                    sb = hh * 1024
                    qh_t = KB(FF_QH + hp * 4096, 2048)
                    kf_t = KB(FF_KF + hp * 5120, 2560)
                    lhsT = qh_t[hh * 64:(hh + 1) * 64, qt * 128:qt * 128 + 128]
                    wmm = p.wv(cx.s_ac, cx.s_dv)
                    pe(p, cx, lambda e, lt=lhsT, r=kf_t[hh * 64:(hh + 1) * 64, qt * 128:qt * 128 + 512], sb=sb:
                       e.matmul(cx.PS[:, sb:sb + 512], lt, r, start=True, stop=False), wait=wmm)
                    if h == 0 and qt < 4:
                        for bb in range(4 - qt):
                            pe(p, cx, lambda e, sb=sb, bb=bb:
                               e.matmul(cx.PS[:, sb + bb * 128:sb + (bb + 1) * 128], cx.ident_b[:, :],
                                        cx.negb_b[:, :], start=False, stop=False))
                    pe(p, cx, lambda e, sb=sb: e.matmul(cx.PS[:, sb:sb + 128], cx.ident_b[:, :],
                                                        cx.triu_b[:, :], start=False, stop=True))
                    pe(p, cx, lambda e, lt=lhsT, r=kf_t[hh * 64:(hh + 1) * 64, qt * 128 + 512:qt * 128 + 640], sb=sb:
                       e.matmul(cx.PS[:, sb + 512:sb + 640], lt, r, start=True, stop=False))
                    pe(p, cx, lambda e, sb=sb: e.matmul(cx.PS[:, sb + 512:sb + 640], cx.ident_b[:, :],
                                                        cx.tril_b[:, :], start=False, stop=True))
                    sl = (qt * 8 + hp * 2 + hh) % 4
                    MX = cx.small[:, 2048 + sl:2049 + sl]
                    NMX = cx.small[:, 2052 + sl:2053 + sl]
                    RS = cx.small[:, 2056 + sl:2057 + sl]
                    RC = cx.small[:, 2060 + sl:2061 + sl]
                    dve(p, cx, lambda e, sb=sb, o=MX: e.tensor_reduce(o, cx.PS[:, sb:sb + 640],
                                                                      AX.X, ALU.max), wait=p.wv(cx.s_pe))
                    vnop(p, cx)
                    dve(p, cx, lambda e, o=NMX, i=MX: e.tensor_scalar_mul(o, i, -1.0),
                        wait=p.wv(cx.s_ac))
                    vnop(p, cx)
                    PSC = KB(FF_PSC + hh * 1280, 640)
                    act(p, cx, lambda e, sb=sb, o=PSC, b=NMX, a=RS:
                        e.activation(out=o, in_=cx.PS[:, sb:sb + 640], func=ACTF.Exp, bias=b,
                                     accum_out=a), wait=p.wv(cx.s_dv, cx.s_pe))
                    dve(p, cx, lambda e, o=RC, i=RS: e.reciprocal(o, i), wait=p.wv(cx.s_ac))
                    vnop(p, cx)
                    dve(p, cx, lambda e, o=PSC, i=RC: e.tensor_scalar_mul(o, o, i))
                    for b5 in range(5):
                        pe(p, cx, lambda e, i=PSC, b5=b5, hh=hh:
                           e.transpose(cx.PSB[:, hh * 1024 + b5 * 128:hh * 1024 + (b5 + 1) * 128],
                                       i[:, b5 * 128:(b5 + 1) * 128], cx.ident_b[:, :]),
                           wait=p.wv(cx.s_dv))
                    PT = KB(FF_PT, 640)
                    dve(p, cx, lambda e, o=PT, hh=hh: e.tensor_copy(o, cx.PSB[:, hh * 1024:hh * 1024 + 640]),
                        wait=p.wv(cx.s_pe))
                    for b5 in range(5):
                        vt_t = KB(FF_VT + (qt + b5) * 1024, 512)
                        lhsT_v = vt_t[:, hp * 128 + hh * 64:hp * 128 + hh * 64 + 64]
                        pe(p, cx, lambda e, lt=lhsT_v, r=PT[:, b5 * 128:(b5 + 1) * 128], hh=hh, b5=b5:
                           e.matmul(cx.PS[hh * 64:(hh + 1) * 64, 2048:2176], lt, r,
                                    start=(b5 == 0), stop=(b5 == 4)),
                           wait=p.wv(cx.s_dv) if b5 == 0 else ())
                ob = KB(FF_OB + hp * 4096, 2048)
                dve(p, cx, lambda e, o=ob[:, qt * 128:(qt + 1) * 128]: e.tensor_copy(o, cx.PS[:, 2048:2176]),
                    wait=p.wv(cx.s_pe))
    p.cur_stage = "wo"
    p.barrier()
    for c in range(NCH):
        for m in range(8):
            bank = cx.PS[:, (m % 2) * 512:(m % 2) * 512 + 512]
            wmm = p.wv(cx.s_dv, cx.s_ac)
            for kk in range(4):
                lhsT = KB(FF_WO + kk * 2048 + m * 256, 128)
                rhs = KB(FF_OB + kk * 4096, 2048)[:, c * CH:(c + 1) * CH]
                pe(p, cx, lambda e, o=bank, lt=lhsT, r=rhs, kk=kk:
                   e.matmul(o, lt, r, start=(kk == 0), stop=(kk == 3)), wait=wmm if kk == 0 else ())
            asl = _af(cx, m, c * CH + 1, CH)
            dve(p, cx, lambda e, o=asl, b=bank: e.tensor_tensor(o, b, o, ALU.add), wait=p.wv(cx.s_pe))
    if h == 0:
        for g in range(8):
            s0 = SM_AHALO + (2 * l + 1) * 8 + g
            dve(p, cx, lambda e, o=cx.small[:, s0:s0 + 1], i=_af(cx, g, T_HALF, 1):
                e.tensor_copy(o, i), wait=p.wv(cx.s_dv) if g == 0 else ())


def stage_mlp_layer(p, cx, h, l):
    d = cx.dram
    p.cur_stage = "mlp"
    KB = lambda off, n: _regb(cx, off, n)
    if h == 1:
        for g in range(8):
            s0 = SM_AHALO + (2 * l + 1) * 8 + g
            dve(p, cx, lambda e, o=_af(cx, g, 0, 1), i=cx.small[:, s0:s0 + 1]:
                e.tensor_copy(o, i), wait=p.wv(cx.s_dv, cx.s_pe) if g == 0 else ())
    p.barrier()
    for cm in range(NCH):
        wab = p.wv(cx.s_pe, cx.s_ac, cx.s_st)
        ab2 = [KB(FF_ABF2 + g * 1032, 513) for g in range(8)]
        for g in range(8):
            dve(p, cx, lambda e, o=ab2[g], a=_af(cx, g, cm * CH, 513):
                e.tensor_copy(o, a), wait=wab)
        sh_ap = [ab2[g][:, 1:513] if g < 4 else ab2[g][:, 0:512] for g in range(8)]
        ln_stats(p, cx, sh_ap, FF_SQ45)
        ln_rows(p, cx)
        wmr = p.wv(cx.s_pe)
        for g in range(8):
            tmp = _reg(cx, FF_TMP45 + g * 2048, 512)
            src = (_af(cx, g, cm * CH + 1, CH) if g < 4 else _af(cx, g, cm * CH, CH))
            dve(p, cx, lambda e, t=tmp, s=src: e.scalar_tensor_tensor(t, cx.PS[:, 1024:1536],
                                                                      -1.0, s, ALU.mult, ALU.add),
                wait=wmr if g == 0 else ())
            dve(p, cx, lambda e, t=tmp: e.tensor_tensor(t, t, cx.PS[:, 1536:2048], ALU.mult))
            dst = KB(FF_SHLN + g * 4096, 2048)[:, cm * CH:(cm + 1) * CH]
            dve(p, cx, lambda e, o=dst, t=tmp, g=g: e.tensor_scalar(
                o, t, cx.small[:, SM_LG[l] + g:SM_LG[l] + g + 1],
                cx.small[:, SM_LB[l] + g:SM_LB[l] + g + 1], ALU.mult, ALU.add))
    for hp2 in range(2):
        wst = p.wv(cx.s_pe, cx.s_dv, cx.s_ac, cx.s_st, cx.s_ld)
        for kk in range(8):
            for side in range(2):
                gdma(p, cx, KB(FF_W1 + (kk * 22 + side * 11) * 256, 1408),
                     d[f"w1{l}"][kk * 128:(kk + 1) * 128,
                                 side * HIDP + hp2 * 1408:side * HIDP + hp2 * 1408 + 1408],
                     wait=wst)
        for kl in range(11):
            gdma(p, cx, KB(FF_W2 + kl * 2048, 1024),
                 d[f"w2{l}"][(hp2 * 11 + kl) * 128:(hp2 * 11 + kl) * 128 + 128, :], wait=wst)
        for cm in range(NCH):
            p.barrier()
            for i in range(11):
                gi = hp2 * 11 + i
                ab_ = cx.PS[:, (i % 2) * 1024:(i % 2) * 1024 + 512]
                gb_ = cx.PS[:, (i % 2) * 1024 + 512:(i % 2) * 1024 + 1024]
                wmm = p.wv(cx.s_dv, cx.s_ac, cx.s_st)
                for kk in range(8):
                    rhs = KB(FF_SHLN + kk * 4096, 2048)[:, cm * CH:(cm + 1) * CH]
                    pe(p, cx, lambda e, o=ab_, lt=KB(FF_W1 + (kk * 22 + i) * 256, 128), r=rhs, kk=kk:
                       e.matmul(o, lt, r, start=(kk == 0), stop=(kk == 7)), wait=wmm if kk == 0 else ())
                for kk in range(8):
                    rhs = KB(FF_SHLN + kk * 4096, 2048)[:, cm * CH:(cm + 1) * CH]
                    pe(p, cx, lambda e, o=gb_, lt=KB(FF_W1 + (kk * 22 + 11 + i) * 256, 128), r=rhs, kk=kk:
                       e.matmul(o, lt, r, start=(kk == 0), stop=(kk == 7)))
                GLB = KB(FF_GLB, 512)
                act(p, cx, lambda e, b=gb_, gi=gi: e.activation(
                    out=GLB, in_=b, func=ACTF.Gelu,
                    bias=cx.small[:, SM_B1[l] + 22 + gi:SM_B1[l] + 22 + gi + 1]),
                    wait=p.wv(cx.s_pe, cx.s_dv))
                hp_t = KB(FF_HP + i * 1024, 512)
                dve(p, cx, lambda e, o=hp_t, b=ab_, gi=gi: e.scalar_tensor_tensor(
                    o, b, cx.small[:, SM_B1[l] + gi:SM_B1[l] + gi + 1], GLB, ALU.add, ALU.mult),
                    wait=p.wv(cx.s_ac, cx.s_pe))
            for mp in range(2):
                for m4 in range(4):
                    m = mp * 4 + m4
                    bank = cx.PS[:, m4 * 512:(m4 + 1) * 512]
                    wmm = p.wv(cx.s_dv, cx.s_ac)
                    for kl in range(11):
                        pe(p, cx, lambda e, o=bank, lt=KB(FF_W2 + kl * 2048 + m * 256, 128),
                           r=KB(FF_HP + kl * 1024, 512), kl=kl:
                           e.matmul(o, lt, r, start=(kl == 0), stop=(kl == 10)),
                           wait=wmm if kl == 0 else ())
                for m4 in range(4):
                    m = mp * 4 + m4
                    bank = cx.PS[:, m4 * 512:(m4 + 1) * 512]
                    asl = _af(cx, m, cm * CH + 1, CH)
                    if hp2 == 0:
                        dve(p, cx, lambda e, o=asl, b=bank: e.tensor_tensor(o, b, o, ALU.add),
                            wait=p.wv(cx.s_pe))
                    else:
                        dve(p, cx, lambda e, o=asl, b=bank, m=m: e.scalar_tensor_tensor(
                            o, b, cx.small[:, SM_B2[l] + m:SM_B2[l] + m + 1], o, ALU.add, ALU.add),
                            wait=p.wv(cx.s_pe))
    if h == 0 and l == 0:
        for g in range(8):
            s0 = SM_AHALO + 16 + g
            dve(p, cx, lambda e, o=cx.small[:, s0:s0 + 1], i=_af(cx, g, T_HALF, 1):
                e.tensor_copy(o, i), wait=p.wv(cx.s_dv) if g == 0 else ())
    if h == 1 and l == 0:
        for g in range(8):
            s0 = SM_AHALO + 16 + g
            dve(p, cx, lambda e, o=_af(cx, g, 0, 1), i=cx.small[:, s0:s0 + 1]:
                e.tensor_copy(o, i), wait=p.wv(cx.s_dv, cx.s_pe) if g == 0 else ())


def stage_combine(p, cx, h):
    d = cx.dram
    p.cur_stage = "combine"
    p.barrier()
    for c in range(NCH):
        t0 = h * T_HALF + c * CH
        for g in range(8):
            xch = _reg(cx, FF_XCH + g * 2048, CH)
            ldma(p, cx, xch, d["Xbuf"][g * 128:(g + 1) * 128, t0:t0 + CH],
                 wait=p.wv(*p.allsems))
            dve(p, cx, lambda e, o=xch, a=_af(cx, g, c * CH + 1, CH): e.scalar_tensor_tensor(
                o, a, 0.5, o, ALU.mult, ALU.add), wait=p.wv(*p.allsems))
            gdma(p, cx, d["Xbuf"][g * 128:(g + 1) * 128, t0:t0 + CH], xch,
                 wait=p.wv(*p.allsems))
    p.emit("sync", lambda e: e.nop(), wait=p.wv(cx.s_st))



def stage_x_in(p, cx):
    d = cx.dram
    tm = FF_QH
    for c in range(T_TOT // 512):
        wld = p.wv(cx.s_pe)
        for s in range(4):
            ldma(p, cx, _reg(cx, tm + s * 4096, 1024),
                 d["hs"][c * 512 + s * 128: c * 512 + (s + 1) * 128, :], wait=wld)
        for g in range(8):
            bank = cx.PS[:, (g % 2) * 512:(g % 2) * 512 + 512]
            for s in range(4):
                src = _reg(cx, tm + s * 4096 + g * 512, 128)
                pe(p, cx, lambda e, o=bank[:, s * 128:(s + 1) * 128], i=src:
                   e.transpose(o, i, cx.ident_f[:, :]), wait=p.wv(cx.s_ld, cx.s_dv))
            dst = _reg(cx, FF_OB + g * 2048, 512)
            dve(p, cx, lambda e, o=dst, i=bank: e.tensor_copy(o, i),
                wait=p.wv(cx.s_pe, cx.s_st))
            gdma(p, cx, d["Xbuf"][g * 128:(g + 1) * 128, c * 512:(c + 1) * 512], dst,
                 wait=p.wv(cx.s_dv))
    p.emit("sync", lambda e: e.nop(), wait=p.wv(cx.s_st))


def stage_x_out(p, cx):
    d = cx.dram
    tm = FF_QH
    fm = FF_OB
    for c in range(T_TOT // 512):
        wg = p.wv(cx.s_st, cx.s_pe)
        for g in range(8):
            ldma(p, cx, _reg(cx, fm + g * 2048, 512),
                 d["Xbuf"][g * 128:(g + 1) * 128, c * 512:(c + 1) * 512], wait=wg)
        for s in range(4):
            bank = cx.PS[:, (s % 2) * 512:(s % 2) * 512 + 512]
            for g in range(8):
                src = _reg(cx, fm + g * 2048 + s * 512, 128)
                pe(p, cx, lambda e, o=bank[:, (g % 4) * 128:(g % 4 + 1) * 128], i=src:
                   e.transpose(o, i, cx.ident_f[:, :]), wait=p.wv(cx.s_ld, cx.s_dv))
                if g % 4 == 3:
                    dst = _reg(cx, tm + s * 4096 + (g - 3) * 512, 512)
                    dve(p, cx, lambda e, o=dst, i=bank: e.tensor_copy(o, i), wait=p.wv(cx.s_pe))
            gdma(p, cx, d["out"][c * 512 + s * 128: c * 512 + (s + 1) * 128, :],
                 _reg(cx, tm + s * 4096, 1024), wait=p.wv(cx.s_dv))
    p.emit("gpsimd", lambda e: e.nop(), wait=p.wv(cx.s_st))


def dump_af_to_xbuf(p, cx, h):
    d = cx.dram
    for g in range(8):
        src = cx.slab[:, (A_F // 4) + g * 2049 + 1:(A_F // 4) + g * 2049 + 1 + T_HALF]
        gdma(p, cx, d["Xbuf"][g * 128:(g + 1) * 128, h * T_HALF:(h + 1) * T_HALF], src,
             wait=p.wv(cx.s_dv, cx.s_pe, cx.s_ac))
    p.emit("sync", lambda e: e.nop(), wait=p.wv(cx.s_st))



def build_program(debug=None):
    nc = bass.Bass("TRN2", target_bir_lowering=False, debug=False, detect_race_conditions=False)
    stack = contextlib.ExitStack()
    with stack:
        p = Prog(nc, stack)
        dram = {}
        for nm, shp in _IN_SHAPES.items():
            dram[nm] = nc.dram_tensor(nm, shp, F32, kind="ExternalInput").ap()
        dram["out"] = nc.dram_tensor("out", [T_TOT, D_MODEL], F32, kind="ExternalOutput").ap()
        for nm, shp in [("Xbuf", [D_MODEL, T_TOT]), ("xin_s", [D_INNER, T_TOT]),
                        ("xt_s", [D_INNER, T_TOT]), ("z_s", [D_INNER, T_TOT])]:
            dram[nm] = nc.dram_tensor(nm, shp, F32).ap()
        dram["lnx_s"] = nc.dram_tensor("lnx_s", [D_MODEL, T_TOT], F32).ap()
        for nm in ("kcar", "vcar"):
            dram[nm] = nc.dram_tensor(nm, [2, 512, 512], BF16).ap()

        cx = setup(p, dram)
        init_consts(p, cx)
        stage_x_in(p, cx)
        if debug == "xcopy":
            pass
        elif debug == "ln1":
            for h in range(N_HALF):
                stage_ln1(p, cx, h)
                dump_af_to_xbuf(p, cx, h)
        elif debug == "consts":
            wd = p.wv(cx.s_dv, cx.s_ac, cx.s_pe, cx.s_gp, cx.s_st, cx.s_ld)
            gdma(p, cx, dram["Xbuf"][0:128, 0:5], cx.small[:, SM_DIMIDX:SM_DIMIDX + 5], wait=wd)
            gdma(p, cx, dram["Xbuf"][0:128, 5:9], cx.small[:, SM_TMP:SM_TMP + 4], wait=wd)
            p.emit("sync", lambda e: e.nop(), wait=p.wv(cx.s_st))
        elif debug and debug.startswith("tbl"):
            tabs = tables_chunk(p, cx, int(debug[3:] or 0))
            for i, k in enumerate(("csq", "snq", "csk", "snk")):
                gdma(p, cx, dram["Xbuf"][i * 128:(i + 1) * 128, 0:CH], tabs[k],
                     wait=p.wv(cx.s_dv, cx.s_ac, cx.s_pe, cx.s_gp))
            p.emit("sync", lambda e: e.nop(), wait=p.wv(cx.s_st))
        elif debug == "p1":
            stage_ln1(p, cx, 0)
            for g in range(8):
                p.emit("vector", lambda e, g=g: e.memset(_af(cx, g, 0, 1), 0.0),
                       wait=p.wv(cx.s_dv, cx.s_ac, cx.s_pe), inc=(cx.s_dv, 1))
            stage_attn_layer(p, cx, 0, 0, p1_only=True)
            wd = p.wv(cx.s_dv, cx.s_ac, cx.s_pe)
            for g in range(4):
                qh = _regb(cx, FF_QH + g * 4096, 2048)
                dve(p, cx, lambda e, g=g, i=qh: e.tensor_copy(_reg(cx, FF_TSC, 512), i[:, 0:512]), wait=wd)
                gdma(p, cx, dram["Xbuf"][g * 128:(g + 1) * 128, 0:512], _reg(cx, FF_TSC, 512),
                     wait=p.wv(cx.s_dv))
            for g in range(4):
                kf = _regb(cx, FF_KF + g * 5120, 2560)
                dve(p, cx, lambda e, g=g, i=kf: e.tensor_copy(_reg(cx, FF_TSC, 512), i[:, 512:1024]),
                    wait=p.wv(cx.s_st))
                gdma(p, cx, dram["Xbuf"][512 + g * 128:512 + (g + 1) * 128, 0:512], _reg(cx, FF_TSC, 512),
                     wait=p.wv(cx.s_dv))
            for s in range(4):
                vt = _regb(cx, FF_VT + (4 + s) * 1024, 512)
                dve(p, cx, lambda e, i=vt: e.tensor_copy(_reg(cx, FF_TSC, 512), i),
                    wait=p.wv(cx.s_st))
                gdma(p, cx, dram["Xbuf"][512 + s * 128:512 + (s + 1) * 128, 512:1024],
                     _reg(cx, FF_TSC, 512), wait=p.wv(cx.s_dv))
            p.emit("sync", lambda e: e.nop(), wait=p.wv(cx.s_st))
        elif debug == "attn0":
            for h in range(N_HALF):
                stage_ln1(p, cx, h)
                stage_ffn(p, cx, h, only="attn0")
                dump_af_to_xbuf(p, cx, h)
        elif debug == "l0":
            for h in range(N_HALF):
                stage_ln1(p, cx, h)
                stage_ffn(p, cx, h, only="l0")
                dump_af_to_xbuf(p, cx, h)
        elif debug == "mamba":
            for h in range(N_HALF):
                stage_mamba(p, cx, h)
        elif debug == "ffn1":
            for h in range(N_HALF):
                stage_ln1(p, cx, h)
                stage_ffn(p, cx, h)
                dump_af_to_xbuf(p, cx, h)
        else:
            full_chain(p, cx)
        stage_x_out(p, cx)
        p.build()
    return nc


def full_chain(p, cx):
    for h in range(N_HALF):
        stage_ln1(p, cx, h)
        stage_ffn(p, cx, h)
        stage_combine(p, cx, h)
    for h in range(N_HALF):
        stage_mamba(p, cx, h)
    for h in range(N_HALF):
        stage_ln1(p, cx, h)
        stage_ffn(p, cx, h)
        stage_combine(p, cx, h)


def kernel(**inputs):
    pre = _prep_inputs(inputs)
    hs_full = np.ascontiguousarray(np.asarray(inputs["L0_attn_hidden_states"], np.float32))
    nc = build_program()
    in_maps = []
    for c in range(8):
        m = dict(pre)
        m["hs"] = hs_full[c // 2]
        in_maps.append(m)
    res = run_bass_kernel_spmd(nc, in_maps, core_ids=list(range(8)))
    outs = [res.results[c]["out"] for c in (0, 2, 4, 6)]
    return np.stack(outs, axis=0)



NM3 = T_HALF // TCM


def stage_mamba(p, cx, h):
    d = cx.dram
    KB = lambda off, n: _regb(cx, off, n)
    RG = lambda off, n: _reg(cx, off, n)
    p.barrier()
    p.cur_stage = "m1"

    wu = p.wv(cx.s_pe, cx.s_dv, cx.s_ac, cx.s_st, cx.s_ld)
    for kk in range(8):
        gdma(p, cx, KB(MB_WIN + kk * 8192, 4096),
             d["inT"][kk * 128:(kk + 1) * 128, :], wait=wu)
    if h == 0:
        dve(p, cx, lambda e: e.memset(cx.small[:, SM_STATE:SM_STATE + 256], 0.0), wait=wu)
    for c in range(NCH):
        p.barrier()
        t0 = h * T_HALF + c * CH
        xch = [RG(MB_XCH + g * 2048, CH) for g in range(8)]
        xb = [KB(MB_XB + g * 1024, CH) for g in range(8)]
        wld = p.wv(cx.s_dv, cx.s_ac, cx.s_pe, cx.s_st)
        for g in range(8):
            ldma(p, cx, xch[g], d["Xbuf"][g * 128:(g + 1) * 128, t0:t0 + CH], wait=wld)
            gdma(p, cx, xb[g], d["Xbuf"][g * 128:(g + 1) * 128, t0:t0 + CH], wait=wld)
        ln_stats(p, cx, xb, MB_SQ)
        ln_rows(p, cx, mamba=True)
        wmr = p.wv(cx.s_pe, cx.s_st)
        for g in range(8):
            lnx = RG(MB_LNX + g * 2048, CH)
            lnm = KB(MB_LNM + g * 1024, CH)
            nwc = cx.small[:, SM_NW + g:SM_NW + g + 1]
            dve(p, cx, lambda e, x_=xch[g]: e.scalar_tensor_tensor(x_, cx.PS[:, 1024:1024 + CH],
                                                                   -1.0, x_, ALU.mult, ALU.add),
                wait=wmr if g == 0 else ())
            dve(p, cx, lambda e, o=lnx, x_=xch[g]: e.tensor_tensor(o, x_, cx.PS[:, 0:CH], ALU.mult))
            dve(p, cx, lambda e, x_=xch[g]: e.tensor_tensor(x_, x_, cx.PS[:, 1536:1536 + CH], ALU.mult))
            dve(p, cx, lambda e, o=lnm, x_=xch[g], w=nwc: e.tensor_scalar_mul(o, x_, w))
            gdma(p, cx, d["lnx_s"][g * 128:(g + 1) * 128, t0:t0 + CH], lnx, wait=p.wv(cx.s_dv))
        for m in range(32):
            bank = cx.PS[:, (m % 2) * 512:(m % 2) * 512 + 512]
            wmm = p.wv(cx.s_dv, cx.s_ac, cx.s_st)
            for kk in range(8):
                lhsT = KB(MB_WIN + kk * 8192 + m * 256, 128)
                pe(p, cx, lambda e, o=bank, lt=lhsT, r=KB(MB_LNM + kk * 1024, CH), kk=kk:
                   e.matmul(o, lt, r, start=(kk == 0), stop=(kk == 7)), wait=wmm if kk == 0 else ())
            slot = RG(MB_XZS + (m % 4) * 2048, CH)
            dve(p, cx, lambda e, o=slot, b=bank: e.tensor_copy(o, b),
                wait=p.wv(cx.s_pe, cx.s_st))
            dst = (d["xin_s"][(m % 16) * 128:(m % 16 + 1) * 128, t0:t0 + CH] if m < 16
                   else d["z_s"][(m - 16) * 128:(m - 15) * 128, t0:t0 + CH])
            gdma(p, cx, dst, slot, wait=p.wv(cx.s_dv))

    p.cur_stage = "m2"
    wu = p.wv(cx.s_pe, cx.s_dv, cx.s_ac, cx.s_st, cx.s_ld)
    p.emit("sync", lambda e: e.nop(), wait=p.wv(cx.s_st))
    ldma(p, cx, cx.small[:, SM_CW:SM_CW + 64], d["convw"][:, :], wait=wu)
    for kk in range(16):
        gdma(p, cx, KB(MB_WXP + kk * 224, 112), d["xpT"][kk * 128:(kk + 1) * 128, :], wait=wu)
    for c in range(NCH):
        p.barrier()
        t0 = h * T_HALF + c * CH
        wld = p.wv(cx.s_dv, cx.s_ac, cx.s_pe, cx.s_st)
        for g in range(16):
            tile = RG(MB_XIN + g * 2060, 515)
            if t0 == 0:
                dve(p, cx, lambda e, o=tile[:, 0:3]: e.memset(o, 0.0), wait=wld if g == 0 else ())
                ldma(p, cx, tile[:, 3:515], d["xin_s"][g * 128:(g + 1) * 128, 0:CH], wait=wld)
            else:
                ldma(p, cx, tile[:, 0:515], d["xin_s"][g * 128:(g + 1) * 128, t0 - 3:t0 + CH],
                     wait=wld)
        for g in range(16):
            tile = RG(MB_XIN + g * 2060, 515)
            xt = RG(MB_XT + g * 2048, CH)
            xtb = KB(MB_XTB + g * 1024, CH)
            wcv = p.wv(cx.s_ld, cx.s_dv, cx.s_ac, cx.s_pe, cx.s_st)
            dve(p, cx, lambda e, o=xt, i=tile, w=cx.small[:, SM_CW + g:SM_CW + g + 1]:
                e.tensor_scalar_mul(o, i[:, 0:CH], w), wait=wcv)
            for k in range(1, 4):
                dve(p, cx, lambda e, o=xt, i=tile, k=k,
                    w=cx.small[:, SM_CW + k * 16 + g:SM_CW + k * 16 + g + 1]:
                    e.scalar_tensor_tensor(o, i[:, k:k + CH], w, o, ALU.mult, ALU.add))
            act(p, cx, lambda e, o=xt, g=g: e.activation(
                out=o, in_=o, func=ACTF.Silu, bias=cx.small[:, SM_CB + g:SM_CB + g + 1]),
                wait=p.wv(cx.s_dv))
            dve(p, cx, lambda e, o=xtb, i=xt: e.tensor_copy(o, i), wait=p.wv(cx.s_ac))
            gdma(p, cx, d["xt_s"][g * 128:(g + 1) * 128, t0:t0 + CH], xt, wait=p.wv(cx.s_dv))
        bank = cx.PS[:, 1024:1536]
        wmm = p.wv(cx.s_dv, cx.s_ac)
        for kk in range(16):
            pe(p, cx, lambda e, lt=KB(MB_WXP + kk * 224, 112), r=KB(MB_XTB + kk * 1024, CH), kk=kk:
               e.matmul(cx.PS[0:112, 1024:1536], lt, r, start=(kk == 0), stop=(kk == 15)),
               wait=wmm if kk == 0 else ())
        dve(p, cx, lambda e, o=RG(MB_XP, 2048)[0:112, c * CH:(c + 1) * CH]:
            e.tensor_copy(o, cx.PS[0:112, 1024:1536]), wait=p.wv(cx.s_pe))

    p.cur_stage = "m3"
    wu = p.wv(cx.s_pe, cx.s_dv, cx.s_ac, cx.s_st, cx.s_ld)
    for kk in range(16):
        gdma(p, cx, KB(M3_WOUT + kk * 2048, 1024), d["outT"][kk * 128:(kk + 1) * 128, :], wait=wu)
    gdma(p, cx, KB(M3_WDT, 2048)[0:64, :], d["dtT"][0:64, :], wait=wu)
    dve(p, cx, lambda e: e.tensor_copy(KB(M3_DTRB, 2048)[0:64, :], RG(MB_XP, 2048)[0:64, :]),
        wait=wu)
    for c8 in range(NM3):
        p.barrier()
        t0 = h * T_HALF + c8 * TCM
        cs = c8 * TCM
        seln = RG(M3_DA, 2048)
        ldma(p, cx, seln, d["seln"][:, :], wait=p.wv(cx.s_dv, cx.s_ac, cx.s_pe, cx.s_st))
        for n in range(16):
            wmm = p.wv(cx.s_dv, cx.s_ac, cx.s_ld)
            pe(p, cx, lambda e, lt=seln[64:80, n * 128:(n + 1) * 128],
               r=RG(MB_XP, 2048)[64:80, cs:cs + TCM]:
               e.matmul(cx.PS[:, 2048:2048 + TCM], lt, r, start=True, stop=True), wait=wmm)
            dve(p, cx, lambda e, o=RG(M3_BBC + n * 1024, TCM): e.tensor_copy(o, cx.PS[:, 2048:2048 + TCM]),
                wait=p.wv(cx.s_pe))
            pe(p, cx, lambda e, lt=seln[96:112, n * 128:(n + 1) * 128],
               r=RG(MB_XP, 2048)[96:112, cs:cs + TCM]:
               e.matmul(cx.PS[:, 2560:2560 + TCM], lt, r, start=True, stop=True,
                        tile_position=(96, 0)))
            dve(p, cx, lambda e, o=RG(M3_CBC + n * 1024, TCM): e.tensor_copy(o, cx.PS[:, 2560:2560 + TCM]),
                wait=p.wv(cx.s_pe))
        for g in range(16):
            DT = RG(M3_DT, TCM)
            UM = RG(M3_UM, TCM)
            XTG = RG(M3_XTG, TCM)
            ZG = RG(M3_ZG, TCM)
            SZG = RG(M3_SZG, TCM)
            DA = RG(M3_DA, 16 * TCM)
            BH = RG(M3_BH, 16 * TCM)
            wld = p.wv(cx.s_dv, cx.s_ac, cx.s_st)
            ldma(p, cx, XTG, d["xt_s"][g * 128:(g + 1) * 128, t0:t0 + TCM], wait=wld)
            ldma(p, cx, ZG, d["z_s"][g * 128:(g + 1) * 128, t0:t0 + TCM], wait=wld)
            wmm = p.wv(cx.s_dv, cx.s_ac)
            pe(p, cx, lambda e, lt=KB(M3_WDT + g * 256, 128)[0:64, :], r=KB(M3_DTRB, 2048)[0:64, cs:cs + TCM]:
               e.matmul(cx.PS[:, 1024:1024 + TCM], lt, r, start=True, stop=True), wait=wmm)
            act(p, cx, lambda e, g=g: e.activation(out=RG(M3_SZG, TCM), in_=cx.PS[:, 1024:1024 + TCM],
                                                   func=ACTF.Exp,
                                                   bias=cx.small[:, SM_DTB + g:SM_DTB + g + 1]),
                wait=p.wv(cx.s_pe))
            act(p, cx, lambda e: e.activation(out=DT, in_=RG(M3_SZG, TCM), func=ACTF.Ln,
                                              bias=ccol(p, cx, 1.0)))
            dve(p, cx, lambda e, o=UM, i=XTG: e.tensor_tensor(o, DT, i, ALU.mult),
                wait=p.wv(cx.s_ac, cx.s_ld))
            for n in range(16):
                acol = cx.small[:, SM_ANEG + g * 16 + n:SM_ANEG + g * 16 + n + 1]
                act(p, cx, lambda e, o=DA[:, n * TCM:(n + 1) * TCM], a=acol:
                    e.activation(out=o, in_=DT, func=ACTF.Exp, scale=a),
                    wait=p.wv(cx.s_dv) if n == 0 else ())
            for n in range(16):
                dve(p, cx, lambda e, o=BH[:, n * TCM:(n + 1) * TCM], b=RG(M3_BBC + n * 1024, TCM):
                    e.tensor_tensor(o, UM, b, ALU.mult), wait=p.wv(cx.s_ac) if n == 0 else ())
            for n in range(16):
                st = cx.small[:, SM_STATE + g * 16 + n:SM_STATE + g * 16 + n + 1]
                sl = BH[:, n * TCM:(n + 1) * TCM]
                dve(p, cx, lambda e, o=sl, a=DA[:, n * TCM:(n + 1) * TCM], b=sl, s=st:
                    e.tensor_tensor_scan(o, a, b, s, ALU.mult, ALU.add))
            dve(p, cx, lambda e, o=cx.small[:, SM_STATE + g * 16:SM_STATE + (g + 1) * 16],
                i=BH[:, TCM - 1::TCM]: e.tensor_copy(o, i))
            dav = DA.rearrange("p (n t) -> p n t", n=16)
            da_t = DA.rearrange("p (t n) -> p t n", n=16)
            for n in range(16):
                dve(p, cx, lambda e, o=da_t[:, :, n], hh_=BH[:, n * TCM:(n + 1) * TCM],
                    cc=RG(M3_CBC + n * 1024, TCM): e.tensor_tensor(o, hh_, cc, ALU.mult))
            dve(p, cx, lambda e: e.tensor_reduce(UM, da_t, AX.X, ALU.add))
            dve(p, cx, lambda e, i=XTG, dc=cx.small[:, SM_DVEC + g:SM_DVEC + g + 1]:
                e.scalar_tensor_tensor(UM, i, dc, UM, ALU.mult, ALU.add))
            act(p, cx, lambda e, i=ZG: e.activation(out=SZG, in_=i, func=ACTF.Silu),
                wait=p.wv(cx.s_dv))
            dve(p, cx, lambda e, o=KB(M3_YSB + g * 512, TCM): e.tensor_tensor(o, UM, SZG, ALU.mult),
                wait=p.wv(cx.s_ac))
        for m in range(8):
            bank = cx.PS[:, (m % 2) * 512:(m % 2) * 512 + 512][:, 0:TCM]
            wmm = p.wv(cx.s_dv, cx.s_ac, cx.s_ld)
            for kk in range(16):
                pe(p, cx, lambda e, o=bank, lt=KB(M3_WOUT + kk * 2048 + m * 256, 128),
                   r=KB(M3_YSB + kk * 512, TCM), kk=kk:
                   e.matmul(o, lt, r, start=(kk == 0), stop=(kk == 15)),
                   wait=wmm if kk == 0 else ())
            xrm = RG(M3_DA + m * 1024, TCM)
            lnxc = RG(M3_DA + 8192 + m * 1024, TCM)
            wrd = p.wv(cx.s_dv, cx.s_ac, cx.s_st)
            ldma(p, cx, xrm, d["Xbuf"][m * 128:(m + 1) * 128, t0:t0 + TCM], wait=wrd)
            ldma(p, cx, lnxc, d["lnx_s"][m * 128:(m + 1) * 128, t0:t0 + TCM], wait=wrd)
            dve(p, cx, lambda e, o=lnxc, b=bank: e.tensor_tensor(o, b, o, ALU.add),
                wait=p.wv(cx.s_pe, cx.s_ld))
            dve(p, cx, lambda e, o=xrm, i=lnxc: e.tensor_tensor(o, i, o, ALU.add))
            gdma(p, cx, d["Xbuf"][m * 128:(m + 1) * 128, t0:t0 + TCM], xrm, wait=p.wv(cx.s_dv))
    p.emit("sync", lambda e: e.nop(), wait=p.wv(cx.s_st))


# version 08
# speedup vs baseline: 1.0346x; 1.0300x over previous
import contextlib
import math
import numpy as np
import concourse.bass as bass
import concourse.mybir as mybir
from concourse.bass_utils import run_bass_kernel_spmd

F32 = mybir.dt.float32
BF16 = mybir.dt.bfloat16
I32 = mybir.dt.int32
ALU = mybir.AluOpType
ACTF = mybir.ActivationFunctionType
AX = mybir.AxisListType

D_MODEL = 1024
D_INNER = 2048
D_STATE = 16
DT_RANK = 64
HEADS = 8
WINDOW = 512
HID = 2730
HIDP = 2816
QK_SCALE = 8.0
T_TOT = 4096
NEG_BIG = -1e38
DM_T = 8
DI_T = 16
N_HALF = 2
T_HALF = 2048
CH = 512
NCH = T_HALF // CH
TCM = 256
PI = math.pi
TWO_PI_HI = np.float32(2 * PI)
TWO_PI_LO = np.float32(2 * PI - float(np.float32(2 * PI)))
INV_2PI = np.float32(1.0 / (2 * PI))
PI_HALF = np.float32(PI / 2)


class _Sem:
    __slots__ = ("h", "name", "n", "emitted")

    def __init__(self, h, name):
        self.h, self.name, self.n, self.emitted = h, name, 0, 0


class Prog:
    ENGINES = ("sync", "scalar", "vector", "gpsimd", "tensor")

    def __init__(self, nc, stack):
        self.nc, self.stack = nc, stack
        self.prog = {e: [] for e in self.ENGINES}
        self._semid = 0

    def sem(self, name=None):
        name = name or f"s{self._semid}"
        self._semid += 1
        h = self.stack.enter_context(self.nc.semaphore(name))
        return _Sem(h, name)

    def sbuf(self, name, shape, dt):
        return self.stack.enter_context(self.nc.sbuf_tensor(name, shape, dt))

    def psum(self, name, shape, dt=F32):
        return self.stack.enter_context(self.nc.psum_tensor(name, shape, dt))

    SAFE = False
    BARRIERS = False
    SAFE_STAGES = set()
    cur_stage = ""

    def emit(self, eng, fn, wait=None, inc=None):
        if (Prog.SAFE or (self.cur_stage in Prog.SAFE_STAGES)) and getattr(self, "allsems", None):
            wait = [(s, s.n) for s in self.allsems]
        if inc is not None:
            sem, k = inc
            sem.n += k
            inc = (sem, sem.n)
        self.prog[eng].append((fn, list(wait or []), inc))

    def wv(self, *sems):
        return [(s, s.n) for s in sems]

    def barrier(self):
        if not Prog.BARRIERS:
            return
        w = [(s, s.n) for s in self.allsems]
        for eng in self.ENGINES:
            self.prog[eng].append(((lambda e: e.nop()), list(w), None))

    def build(self):
        with self.nc.Block() as block:
            for eng_name in self.ENGINES:
                seq = self.prog[eng_name]
                if not seq:
                    continue

                def body(eng, seq=seq):
                    for fn, waits, inc in seq:
                        for s, v in waits:
                            if v > 0:
                                eng.wait_ge(s.h, v)
                        inst = fn(eng)
                        if inc is not None:
                            s, v = inc
                            inst.then_inc(s.h, v - s.emitted)
                            s.emitted = v

                getattr(block, eng_name)(body)



def _colvec(v):
    v = np.asarray(v, np.float32).reshape(-1)
    return np.ascontiguousarray(v.reshape(-1, 128).T)


def _prep_inputs(ix):
    f = np.float32
    pre = {}
    for l in range(2):
        pre[f"wqkv{l}"] = np.ascontiguousarray(ix["L0_ffn_wqkv"][l], f)
        pre[f"wo{l}"] = np.ascontiguousarray(ix["L0_ffn_wo"][l], f)
        w1 = np.asarray(ix["L0_ffn_w1"][l], f)
        w1p = np.zeros((D_MODEL, 2 * HIDP), f)
        w1p[:, :HID] = w1[:, :HID]
        w1p[:, HIDP:HIDP + HID] = w1[:, HID:]
        pre[f"w1{l}"] = w1p
        w2 = np.asarray(ix["L0_ffn_w2"][l], f)
        w2p = np.zeros((HIDP, D_MODEL), f)
        w2p[:HID] = w2
        pre[f"w2{l}"] = w2p
        b1 = np.asarray(ix["L0_ffn_b1"][l], f)
        b1p = np.zeros((2 * HIDP,), f)
        b1p[:HID] = b1[:HID]
        b1p[HIDP:HIDP + HID] = b1[HID:]
        pre[f"b1{l}"] = _colvec(b1p)
        pre[f"b2{l}"] = _colvec(ix["L0_ffn_b2"][l])
        pre[f"lg{l}"] = _colvec(ix["L0_ffn_ln_g"][l])
        pre[f"lb{l}"] = _colvec(ix["L0_ffn_ln_b"][l])
        pre[f"qs{l}"] = np.ascontiguousarray(np.tile(np.asarray(ix["L0_ffn_q_scale"][l], f), 2).reshape(128, 1))
        pre[f"ks{l}"] = np.ascontiguousarray(np.tile(np.asarray(ix["L0_ffn_k_scale"][l], f), 2).reshape(128, 1))
    pre["inT"] = np.ascontiguousarray(np.asarray(ix["L0_attn_in_proj_w"], f).T)
    pre["outT"] = np.ascontiguousarray(np.asarray(ix["L0_attn_out_proj_w"], f).T)
    xpT = np.asarray(ix["L0_attn_x_proj_w"], f).T
    xpp = np.zeros((D_INNER, 112), f)
    xpp[:, 0:80] = xpT[:, 0:80]
    xpp[:, 96:112] = xpT[:, 80:96]
    pre["xpT"] = xpp
    pre["dtT"] = np.ascontiguousarray(np.asarray(ix["L0_attn_dt_proj_w"], f).T)
    cw = np.asarray(ix["L0_attn_conv_w"], f)[:, 0, :]
    pre["convw"] = np.ascontiguousarray(
        np.concatenate([_colvec(cw[:, k]) for k in range(4)], axis=1))
    pre["nw"] = _colvec(ix["L0_attn_norm_weight"])
    pre["cb"] = _colvec(ix["L0_attn_conv_b"])
    pre["dtb"] = _colvec(ix["L0_attn_dt_proj_b"])
    pre["Dvec"] = _colvec(ix["L0_attn_D"])
    al = np.asarray(ix["L0_attn_A_log"], f).reshape(DI_T, 128, D_STATE)
    pre["AlogT"] = np.ascontiguousarray(al.transpose(1, 0, 2).reshape(128, DI_T * D_STATE))
    pre["dimidx"] = np.ascontiguousarray((2.0 * (np.arange(128) % 32)).reshape(128, 1).astype(f))
    pre["ident"] = np.ascontiguousarray(np.eye(128, dtype=f))
    R = np.zeros((128, 128), f)
    for m in range(128):
        if m % 64 < 32:
            R[(m + 32), m] = -1.0
        else:
            R[(m - 32), m] = 1.0
    pre["rotm"] = R
    bm = np.zeros((128, 128), f)
    bm[:64, :64] = 1.0
    bm[64:, 64:] = 1.0
    pre["headmask"] = bm
    sel = np.zeros((128, 16 * 128), f)
    for n in range(16):
        sel[64 + n, n * 128:(n + 1) * 128] = 1.0
        sel[96 + n, n * 128:(n + 1) * 128] = 1.0
    pre["seln"] = sel
    pre["ten_thousand"] = np.full((1, 1), 10000.0, f)
    dimidx = (2.0 * (np.arange(128) % 32)).astype(np.float64)
    l10k = np.log(np.float64(10000.0))
    invf = np.exp(dimidx * np.float64(-1.0 / 64.0) * l10k)
    t_all = np.arange(T_TOT, dtype=np.float64)
    fr = invf[:, None] * t_all[None, :]
    c = np.float64(np.float32(1.0 / (2.0 * math.pi)))
    pre["rotk1"] = np.floor(fr * c + 0.5).astype(f)
    pre["rotk2"] = np.floor(fr * c + 0.75).astype(f)
    return pre


_IN_SHAPES = {
    "hs": [T_TOT, D_MODEL],
    "inT": [D_MODEL, 2 * D_INNER], "outT": [D_INNER, D_MODEL],
    "xpT": [D_INNER, 112], "dtT": [DT_RANK, D_INNER], "convw": [128, 64],
    "nw": [128, 8], "cb": [128, 16], "dtb": [128, 16], "Dvec": [128, 16],
    "AlogT": [128, DI_T * D_STATE], "dimidx": [128, 1], "ident": [128, 128],
    "rotm": [128, 128], "headmask": [128, 128], "seln": [128, 16 * 128],
    "ten_thousand": [1, 1],
    "rotk1": [128, T_TOT], "rotk2": [128, T_TOT],
}
for _l in range(2):
    _IN_SHAPES.update({
        f"wqkv{_l}": [D_MODEL, 1536], f"wo{_l}": [512, D_MODEL],
        f"w1{_l}": [D_MODEL, 2 * HIDP], f"w2{_l}": [HIDP, D_MODEL],
        f"b1{_l}": [128, 44], f"b2{_l}": [128, 8], f"lg{_l}": [128, 8],
        f"lb{_l}": [128, 8], f"qs{_l}": [128, 1], f"ks{_l}": [128, 1],
    })



class Ctx:
    pass


def _reg(cx, byte_off, n_f32):
    assert byte_off % 4 == 0
    c = byte_off // 4
    assert c + n_f32 <= cx.SLAB_COLS, (byte_off, n_f32)
    return cx.slab[:, c:c + n_f32]


def _regb(cx, byte_off, n_bf16):
    assert byte_off % 4 == 0
    c = byte_off // 4
    assert c * 4 + n_bf16 * 2 <= cx.SLAB_COLS * 4
    return cx.slab[:, c:c + (n_bf16 + 1) // 2].bitcast(BF16)[:, :n_bf16]


def setup(p, dram):
    cx = Ctx()
    cx.dram = dram
    cx.ident_f = p.sbuf("ident_f", [128, 128], F32)
    cx.ident_b = p.sbuf("ident_b", [128, 128], BF16)
    cx.rot_b = p.sbuf("rot_b", [128, 128], BF16)
    cx.hmask_b = p.sbuf("hmask_b", [128, 128], BF16)
    cx.triu_b = p.sbuf("triu_b", [128, 128], BF16)
    cx.tril_b = p.sbuf("tril_b", [128, 128], BF16)
    cx.negb_b = p.sbuf("negb_b", [128, 128], BF16)
    cx.ones_b = p.sbuf("ones_b", [128, 1], BF16)
    cx.small = p.sbuf("small", [128, 3072], F32)
    cx.SLAB_COLS = 48592
    cx.slab = p.sbuf("slab", [128, cx.SLAB_COLS], F32)
    cx.PS = p.psum("PS", [128, 3072], F32)
    cx.PSB = p.psum("PSB", [128, 2048], BF16)
    cx.s_ld = p.sem("ld")
    cx.s_st = p.sem("st")
    cx.s_ld2 = p.sem("ld2")
    cx.s_pe = p.sem("pe")
    cx.s_dv = p.sem("dv")
    cx.s_ac = p.sem("ac")
    cx.s_gp = p.sem("gp")
    cx.consts = {}
    cx.n_consts = 0
    p.allsems = [cx.s_ld, cx.s_st, cx.s_ld2, cx.s_pe, cx.s_dv, cx.s_ac, cx.s_gp]
    return cx


SM_B1 = {0: 0, 1: 44}
SM_B2 = {0: 88, 1: 96}
SM_LG = {0: 104, 1: 112}
SM_LB = {0: 120, 1: 128}
SM_QS = {0: 136, 1: 137}
SM_KS = {0: 138, 1: 139}
SM_NW = 140
SM_CB = 148
SM_DTB = 164
SM_DVEC = 180
SM_ANEG = 196
SM_DIMIDX = 452
SM_INVF = 453
SM_LNB = 454
SM_L10K = 455
SM_BASE = 456
SM_MX = 460
SM_NMX = 461
SM_RS = 462
SM_RC = 463
SM_STATE = 464
SM_CONVH = 736
SM_AHALO = 800
SM_ONESR = 840
SM_STAT = 1024
SM_STAT2 = 1536
SM_TMP = 2048


SLAB_BYTES = 48592 * 4

A_F = 0
FF_ABF = 65568
FF_QH = 73856
FF_KF = 90240
FF_VT = 110720
FF_SQB = 131200
FF_TBL = 135296
FF_TSC = 143488
FF_RRS = FF_TSC
FF_Q2 = FF_TSC + 2048
FF_UB = FF_TSC + 4096
FF_WSCR = FF_TSC + 6144
FF_WQ = 151680
FF_WO = 176256
FF_OB = 131200
FF_XCH = FF_QH
FF_XB = FF_VT
FF_PSC = 184448
FF_PT = 187008

FF_SHLN = 65568
FF_W1 = 98336
FF_W2 = 143392
FF_HP = 165920
FF_GLB = 177184
FF_ABF2 = 98336
FF_TMP45 = 106624
FF_SQ45 = 123008

MB_LNM = 65568
MB_SQ = 73760
MB_XCH = 77856
MB_XB = 94240
MB_WIN = 102432
MB_XZS = 167968
MB_LNX = 176160
MB_XIN = 65568
MB_XT = 98560
MB_XTB = 131328
MB_WXP = 147712
M3_DA = 65568
M3_BH = 81952
M3_DT = 98336
M3_UM = 99360
M3_XTG = 100384
M3_ZG = 101408
M3_SZG = 102432
M3_YG = M3_UM
M3_YSB = 103456
M3_WDT = 111648
M3_DTRB = 115744
M3_WOUT = 119840
M3_BBC = 152608
M3_CBC = 168992
MB_XP = 185376

assert FF_PT + 1280 <= SLAB_BYTES
assert FF_GLB + 1024 <= SLAB_BYTES
assert MB_XZS + 8192 <= SLAB_BYTES
assert MB_XP + 8192 <= SLAB_BYTES



def _allsems(cx):
    return (cx.s_ld, cx.s_st, cx.s_ld2, cx.s_pe, cx.s_dv, cx.s_ac, cx.s_gp)


def barrier_wait(p, cx):
    return p.wv(*_allsems(cx))


def dve(p, cx, fn, wait=()):
    p.emit("vector", fn, wait=wait, inc=(cx.s_dv, 1))


def vnop(p, cx):
    p.emit("vector", lambda e: e.nop())


def act(p, cx, fn, wait=()):
    p.emit("scalar", fn, wait=wait, inc=(cx.s_ac, 1))


def pe(p, cx, fn, wait=()):
    p.emit("tensor", fn, wait=wait, inc=(cx.s_pe, 1))


def ldma(p, cx, out, in_, wait=()):
    p.emit("sync", lambda e, o=out, i=in_: e.dma_start(out=o, in_=i), wait=wait, inc=(cx.s_ld, 16))


def gdma(p, cx, out, in_, wait=()):
    p.emit("gpsimd", lambda e, o=out, i=in_: e.dma_start(out=o, in_=i), wait=wait, inc=(cx.s_st, 16))


SM_CONST = 968
SM_CW = 2100


def ccol(p, cx, val):
    val = float(val)
    if val in cx.consts:
        return cx.consts[val]
    idx = SM_CONST + cx.n_consts
    cx.n_consts += 1
    assert idx < 3072
    ap = cx.small[:, idx:idx + 1]
    p.emit("vector", lambda e, a=ap, v=val: e.memset(a, v), inc=(cx.s_dv, 1))
    cx.consts[val] = ap
    return ap


def sm(cx, col, n=1, row0=False):
    t = cx.small[0:1, col:col + n] if row0 else cx.small[:, col:col + n]
    return t



def init_consts(p, cx):
    d = cx.dram
    for v in (1e-5, 1.0, -0.5, -1.0, 0.5):
        ccol(p, cx, v)
    B = lambda: barrier_wait(p, cx)
    ldma(p, cx, cx.ident_f[:, :], d["ident"][:, :])
    gdma(p, cx, cx.ident_b[:, :], d["ident"][:, :])
    gdma(p, cx, cx.rot_b[:, :], d["rotm"][:, :])
    gdma(p, cx, cx.hmask_b[:, :], d["headmask"][:, :])
    for l in range(2):
        ldma(p, cx, cx.small[:, SM_B1[l]:SM_B1[l] + 44], d[f"b1{l}"][:, :])
        ldma(p, cx, cx.small[:, SM_B2[l]:SM_B2[l] + 8], d[f"b2{l}"][:, :])
        ldma(p, cx, cx.small[:, SM_LG[l]:SM_LG[l] + 8], d[f"lg{l}"][:, :])
        ldma(p, cx, cx.small[:, SM_LB[l]:SM_LB[l] + 8], d[f"lb{l}"][:, :])
        ldma(p, cx, cx.small[:, SM_QS[l]:SM_QS[l] + 1], d[f"qs{l}"][:, :])
        ldma(p, cx, cx.small[:, SM_KS[l]:SM_KS[l] + 1], d[f"ks{l}"][:, :])
    ldma(p, cx, cx.small[:, SM_NW:SM_NW + 8], d["nw"][:, :])
    ldma(p, cx, cx.small[:, SM_CB:SM_CB + 16], d["cb"][:, :])
    ldma(p, cx, cx.small[:, SM_DTB:SM_DTB + 16], d["dtb"][:, :])
    ldma(p, cx, cx.small[:, SM_DVEC:SM_DVEC + 16], d["Dvec"][:, :])
    ldma(p, cx, cx.small[:, SM_DIMIDX:SM_DIMIDX + 1], d["dimidx"][:, :])
    ldma(p, cx, cx.small[:, SM_TMP:SM_TMP + 256], d["AlogT"][:, :])
    ldma(p, cx, cx.small[0:1, SM_TMP + 384:SM_TMP + 385], d["ten_thousand"][:, :])
    act(p, cx, lambda e: e.activation(out=sm(cx, SM_TMP, 256), in_=sm(cx, SM_TMP, 256),
                                      func=ACTF.Exp), wait=B())
    dve(p, cx, lambda e: e.tensor_scalar_mul(sm(cx, SM_ANEG, 256), sm(cx, SM_TMP, 256), -1.0),
        wait=B())
    dve(p, cx, lambda e: e.memset(cx.ones_b[:, :], 1.0), wait=B())
    dve(p, cx, lambda e: e.memset(cx.negb_b[:, :], NEG_BIG))
    dve(p, cx, lambda e: e.memset(cx.small[0:1, SM_ONESR:SM_ONESR + 128], 1.0))
    p.emit("gpsimd", lambda e: e.iota(sm(cx, SM_TMP, 128), [[1, 128]], channel_multiplier=0,
                                      allow_small_or_imprecise_dtypes=True), wait=B(), inc=(cx.s_gp, 1))
    p.emit("gpsimd", lambda e: e.iota(sm(cx, SM_TMP + 128, 128), [[0, 128]], channel_multiplier=1,
                                      allow_small_or_imprecise_dtypes=True), inc=(cx.s_gp, 1))
    dve(p, cx, lambda e: e.tensor_tensor(sm(cx, SM_TMP + 256, 128), sm(cx, SM_TMP, 128),
                                         sm(cx, SM_TMP + 128, 128), ALU.is_lt), wait=B())
    dve(p, cx, lambda e: e.tensor_scalar_mul(cx.triu_b[:, :], sm(cx, SM_TMP + 256, 128), NEG_BIG),
        wait=B())
    dve(p, cx, lambda e: e.tensor_tensor(sm(cx, SM_TMP + 256, 128), sm(cx, SM_TMP, 128),
                                         sm(cx, SM_TMP + 128, 128), ALU.is_gt), wait=B())
    dve(p, cx, lambda e: e.tensor_scalar_mul(cx.tril_b[:, :], sm(cx, SM_TMP + 256, 128), NEG_BIG),
        wait=B())
    act(p, cx, lambda e: e.activation(out=cx.small[0:1, SM_TMP + 385:SM_TMP + 386],
                                      in_=cx.small[0:1, SM_TMP + 384:SM_TMP + 385], func=ACTF.Ln),
        wait=B())
    pe(p, cx, lambda e: e.matmul(cx.PS[:, 0:1], cx.small[0:1, SM_ONESR:SM_ONESR + 128],
                                 cx.small[0:1, SM_TMP + 385:SM_TMP + 386], start=True, stop=True),
       wait=B())
    dve(p, cx, lambda e: e.tensor_copy(sm(cx, SM_L10K), cx.PS[:, 0:1]), wait=B())
    vnop(p, cx)
    dve(p, cx, lambda e: e.tensor_scalar_mul(sm(cx, SM_TMP), sm(cx, SM_DIMIDX), -1.0 / 64.0),
        wait=B())
    vnop(p, cx)
    dve(p, cx, lambda e: e.tensor_tensor(sm(cx, SM_TMP + 1, 1), sm(cx, SM_TMP), sm(cx, SM_L10K),
                                         ALU.mult))
    vnop(p, cx)
    act(p, cx, lambda e: e.activation(out=sm(cx, SM_INVF), in_=sm(cx, SM_TMP + 1, 1),
                                      func=ACTF.Exp), wait=B())
    dve(p, cx, lambda e: e.tensor_scalar(sm(cx, SM_BASE), sm(cx, SM_DIMIDX), 25.6, 1.0 / 89.6,
                                         ALU.add, ALU.mult), wait=B())
    act(p, cx, lambda e: e.activation(out=sm(cx, SM_LNB), in_=sm(cx, SM_BASE), func=ACTF.Ln),
        wait=B())
    p.emit("sync", lambda e: e.nop(), wait=B())



def ln_stats(p, cx, xb_tiles, sq_slot_off, shifted=False):
    for g in range(8):
        pe(p, cx, lambda e, g=g: e.matmul(cx.PS[0:1, 2048:2048 + CH], cx.ones_b[:, 0:1],
                                          xb_tiles[g], start=(g == 0), stop=(g == 7)),
           wait=p.wv(cx.s_st, cx.s_dv, cx.s_ld))
        slot = _regb(cx, sq_slot_off + (g % 4) * 1024, CH)
        act(p, cx, lambda e, s=slot, g=g: e.activation(out=s, in_=xb_tiles[g], func=ACTF.Square),
            wait=p.wv(cx.s_pe, cx.s_st, cx.s_ld))
        pe(p, cx, lambda e, s=slot, g=g: e.matmul(cx.PS[0:1, 2560:2560 + CH], cx.ones_b[:, 0:1],
                                                  s, start=(g == 0), stop=(g == 7)),
           wait=p.wv(cx.s_ac))


def ln_rows(p, cx, mamba=False):
    S = 1024
    m_row = cx.small[0:1, S:S + CH]
    r_row = cx.small[0:1, S + 512:S + 512 + CH]
    t1 = cx.small[0:1, S + 1024:S + 1024 + CH]
    w = p.wv(cx.s_pe)
    dve(p, cx, lambda e: e.tensor_scalar_mul(m_row, cx.PS[0:1, 2048:2048 + CH], 1.0 / 1024.0), wait=w)
    dve(p, cx, lambda e: e.tensor_scalar_mul(t1, cx.PS[0:1, 2560:2560 + CH], 1.0 / 1024.0))
    dve(p, cx, lambda e: e.tensor_tensor(r_row, m_row, m_row, ALU.mult))
    dve(p, cx, lambda e: e.tensor_tensor(t1, t1, r_row, ALU.subtract))
    w = p.wv(cx.s_dv)
    act(p, cx, lambda e: e.activation(out=t1, in_=t1, func=ACTF.Ln,
                                      bias=ccol(p, cx, 1e-5)[0:1, :]), wait=w)
    if not mamba:
        act(p, cx, lambda e: e.activation(out=r_row, in_=t1, func=ACTF.Exp,
                                          scale=ccol(p, cx, -0.5)[0:1, :]))
    else:
        r1_row = cx.small[0:1, S + 1536:S + 1536 + CH]
        act(p, cx, lambda e: e.activation(out=r1_row, in_=t1, func=ACTF.Exp,
                                          scale=ccol(p, cx, -0.5)[0:1, :]))
        act(p, cx, lambda e: e.activation(out=r_row, in_=t1, func=ACTF.Exp,
                                          scale=ccol(p, cx, -1.0)[0:1, :]))
        dve(p, cx, lambda e: e.tensor_scalar(r_row, r_row, -1e-5, 1.0, ALU.mult, ALU.add),
            wait=p.wv(cx.s_ac))
        act(p, cx, lambda e: e.activation(out=r_row, in_=r_row, func=ACTF.Ln,
                                          bias=ccol(p, cx, 1e-5)[0:1, :]), wait=p.wv(cx.s_dv))
        dve(p, cx, lambda e: e.tensor_tensor(t1, t1, r_row, ALU.add), wait=p.wv(cx.s_ac))
        act(p, cx, lambda e: e.activation(out=r_row, in_=t1, func=ACTF.Exp,
                                          scale=ccol(p, cx, -0.5)[0:1, :]), wait=p.wv(cx.s_dv))
    w = p.wv(cx.s_ac, cx.s_dv)
    pe(p, cx, lambda e: e.matmul(cx.PS[:, 1024:1024 + CH], cx.small[0:1, SM_ONESR:SM_ONESR + 128],
                                 m_row, start=True, stop=True), wait=w)
    pe(p, cx, lambda e: e.matmul(cx.PS[:, 1536:1536 + CH], cx.small[0:1, SM_ONESR:SM_ONESR + 128],
                                 r_row, start=True, stop=True))
    if mamba:
        r1_row = cx.small[0:1, S + 1536:S + 1536 + CH]
        pe(p, cx, lambda e: e.matmul(cx.PS[:, 0:CH], cx.small[0:1, SM_ONESR:SM_ONESR + 128],
                                     r1_row, start=True, stop=True))


def stage_ln1(p, cx, h, mamba=False, dst="a"):
    d = cx.dram
    p.cur_stage = "ln1"
    for c in range(NCH):
        p.barrier()
        t0 = h * T_HALF + c * CH
        xch = [_reg(cx, (FF_XCH if not mamba else MB_XCH) + g * 2048, CH) for g in range(8)]
        xb = [_regb(cx, (FF_XB if not mamba else MB_XB) + g * 1024, CH) for g in range(8)]
        wld = p.wv(cx.s_dv, cx.s_ac, cx.s_pe, cx.s_st)
        for g in range(8):
            ldma(p, cx, xch[g], d["Xbuf"][g * 128:(g + 1) * 128, t0:t0 + CH], wait=wld)
            gdma(p, cx, xb[g], d["Xbuf"][g * 128:(g + 1) * 128, t0:t0 + CH], wait=wld)
        ln_stats(p, cx, xb, (FF_SQB if not mamba else MB_LNM))
        ln_rows(p, cx, mamba=mamba)
        wmr = p.wv(cx.s_pe)
        for g in range(8):
            dve(p, cx, lambda e, g=g: e.scalar_tensor_tensor(xch[g], cx.PS[:, 1024:1024 + CH],
                                                             -1.0, xch[g], ALU.mult, ALU.add),
                wait=wmr)
            if not mamba:
                dst_ap = cx.slab[:, (A_F // 4) + g * 2049 + c * CH + 1: (A_F // 4) + g * 2049 + c * CH + 1 + CH]
                dve(p, cx, lambda e, g=g, o=dst_ap: e.tensor_tensor(o, xch[g], cx.PS[:, 1536:1536 + CH],
                                                                    ALU.mult))
            else:
                dve(p, cx, lambda e, g=g: e.tensor_tensor(xch[g], xch[g], cx.PS[:, 1536:1536 + CH],
                                                          ALU.mult))
                lnm = _regb(cx, MB_LNM + g * 1024, CH)
                dve(p, cx, lambda e, g=g, o=lnm: e.tensor_scalar_mul(o, xch[g],
                                                                     cx.small[:, SM_NW + g:SM_NW + g + 1]))



def tables_chunk(p, cx, t0):
    FR = _reg(cx, FF_TSC, CH)
    KF_ = _reg(cx, FF_TSC + 2048, CH)
    S1 = _reg(cx, FF_TSC + 4096, CH)
    S2 = _reg(cx, FF_TSC + 6144, CH)
    cs = _reg(cx, FF_TBL, CH)
    sn = _reg(cx, FF_TBL + 2048, CH)
    scq = _reg(cx, FF_TBL + 4096, CH)
    sck = _reg(cx, FF_TBL + 6144, CH)
    w = p.wv(cx.s_dv, cx.s_ac, cx.s_pe)
    p.emit("gpsimd", lambda e: e.iota(FR, [[1, CH]], base=t0, channel_multiplier=0,
                                      allow_small_or_imprecise_dtypes=True), wait=w, inc=(cx.s_gp, 1))
    wg = p.wv(cx.s_gp)
    dve(p, cx, lambda e: e.tensor_scalar_mul(FR, FR, sm(cx, SM_INVF)), wait=wg)
    ldma(p, cx, KF_, cx.dram["rotk1"][:, t0:t0 + CH], wait=p.wv(cx.s_dv, cx.s_ac))
    dve(p, cx, lambda e: e.scalar_tensor_tensor(S1, KF_, -float(TWO_PI_HI), FR, ALU.mult, ALU.add),
        wait=p.wv(cx.s_ld))
    dve(p, cx, lambda e: e.scalar_tensor_tensor(S1, KF_, -float(TWO_PI_LO), S1, ALU.mult, ALU.add))
    act(p, cx, lambda e: e.activation(out=sn, in_=S1, func=ACTF.Sin), wait=p.wv(cx.s_dv))
    ldma(p, cx, KF_, cx.dram["rotk2"][:, t0:t0 + CH], wait=p.wv(cx.s_dv, cx.s_ac))
    dve(p, cx, lambda e: e.scalar_tensor_tensor(S2, KF_, -float(TWO_PI_HI), FR, ALU.mult, ALU.add),
        wait=p.wv(cx.s_ld))
    dve(p, cx, lambda e: e.scalar_tensor_tensor(S2, KF_, -float(TWO_PI_LO), S2, ALU.mult, ALU.add))
    dve(p, cx, lambda e: e.tensor_scalar_add(S2, S2, float(PI_HALF)))
    act(p, cx, lambda e: e.activation(out=cs, in_=S2, func=ACTF.Sin), wait=p.wv(cx.s_dv))
    p.emit("gpsimd", lambda e: e.iota(S1, [[1, CH]], base=t0 - 2048, channel_multiplier=0,
                                      allow_small_or_imprecise_dtypes=True),
           wait=p.wv(cx.s_dv), inc=(cx.s_gp, 1))
    dve(p, cx, lambda e: e.tensor_scalar_mul(S1, S1, 1.0 / 256.0), wait=p.wv(cx.s_gp))
    dve(p, cx, lambda e: e.tensor_scalar_mul(S1, S1, sm(cx, SM_LNB)))
    act(p, cx, lambda e: e.activation(out=scq, in_=S1, func=ACTF.Exp), wait=p.wv(cx.s_dv))
    act(p, cx, lambda e: e.activation(out=sck, in_=S1, func=ACTF.Exp, scale=ccol(p, cx, -1.0)))
    dve(p, cx, lambda e: e.tensor_scalar_mul(scq, scq, 8.0), wait=p.wv(cx.s_ac))
    w = p.wv(cx.s_ac)
    csq = cs
    snq = sn
    csk = _reg(cx, FF_TSC, CH)
    snk = _reg(cx, FF_TSC + 2048, CH)
    dve(p, cx, lambda e: e.tensor_tensor(csk, cs, sck, ALU.mult), wait=w)
    dve(p, cx, lambda e: e.tensor_tensor(snk, sn, sck, ALU.mult))
    dve(p, cx, lambda e: e.tensor_tensor(csq, cs, scq, ALU.mult))
    dve(p, cx, lambda e: e.tensor_tensor(snq, sn, scq, ALU.mult))
    dve(p, cx, lambda e: e.tensor_copy(scq, csk))
    dve(p, cx, lambda e: e.tensor_copy(sck, snk))
    return dict(csq=csq, snq=snq, csk=scq, snk=sck)




def _af(cx, g, c0, n):
    return cx.slab[:, (A_F // 4) + g * 2049 + c0:(A_F // 4) + g * 2049 + c0 + n]


def stage_ffn(p, cx, h, only=None):
    d = cx.dram
    p.cur_stage = "ffn"
    p.barrier()
    wdv = p.wv(cx.s_dv, cx.s_ac, cx.s_pe, cx.s_st, cx.s_ld)
    if h == 0:
        for g in range(8):
            dve(p, cx, lambda e, g=g: e.memset(_af(cx, g, 0, 1), 0.0), wait=wdv if g == 0 else ())
        for g in range(8):
            dve(p, cx, lambda e, o=cx.small[:, SM_AHALO + g:SM_AHALO + g + 1],
                i=_af(cx, g, T_HALF, 1): e.tensor_copy(o, i))
    else:
        for g in range(8):
            dve(p, cx, lambda e, o=_af(cx, g, 0, 1),
                i=cx.small[:, SM_AHALO + g:SM_AHALO + g + 1]: e.tensor_copy(o, i),
                wait=wdv if g == 0 else ())
    for l in range(2):
        stage_attn_layer(p, cx, h, l)
        if only == "attn0":
            return
        stage_mlp_layer(p, cx, h, l)
        if only == "l0":
            return


def stage_attn_layer(p, cx, h, l, p1_only=False):
    d = cx.dram
    p.cur_stage = "attn"
    KB = lambda off, n: _regb(cx, off, n)
    wub = p.wv(cx.s_pe, cx.s_dv, cx.s_ac, cx.s_st, cx.s_ld)
    for kk in range(8):
        gdma(p, cx, KB(FF_WQ + kk * 3072, 1536),
             d[f"wqkv{l}"][kk * 128:(kk + 1) * 128, :], wait=wub)
    for kk in range(4):
        gdma(p, cx, KB(FF_WO + kk * 2048, 1024),
             d[f"wo{l}"][kk * 128:(kk + 1) * 128, :], wait=wub)
    wv0 = p.wv(cx.s_pe, cx.s_ac, cx.s_dv, cx.s_st)
    if h == 0:
        dve(p, cx, lambda e: e.memset(KB(FF_KF, 512), 0.0), wait=wv0)
        dve(p, cx, lambda e: e.memset(KB(FF_KF + 5120, 512), 0.0))
        dve(p, cx, lambda e: e.memset(KB(FF_KF + 10240, 512), 0.0))
        dve(p, cx, lambda e: e.memset(KB(FF_KF + 15360, 512), 0.0))
        for t4 in range(4):
            dve(p, cx, lambda e, t4=t4: e.memset(KB(FF_VT + t4 * 1024, 512), 0.0))
    else:
        for g in range(4):
            ldma(p, cx, KB(FF_KF + g * 5120, 512),
                 d["kcar"][l, g * 128:(g + 1) * 128, :], wait=wv0)
        for t4 in range(4):
            ldma(p, cx, KB(FF_VT + t4 * 1024, 512),
                 d["vcar"][l, t4 * 128:(t4 + 1) * 128, :], wait=wv0)
    for c in range(NCH):
        p.barrier()
        t0g = h * T_HALF + c * CH
        tabs = tables_chunk(p, cx, t0g)
        wab = p.wv(cx.s_pe, cx.s_ac)
        for g in range(8):
            dve(p, cx, lambda e, o=KB(FF_ABF + g * 1032, 513), a=_af(cx, g, c * CH, 513):
                e.tensor_copy(o, a), wait=wab)
        for m in range(12):
            bank = cx.PS[:, (m % 2) * 512:(m % 2) * 512 + 512]
            wmm = p.wv(cx.s_dv, cx.s_ac, cx.s_st)
            for kk in range(8):
                rhs = (KB(FF_ABF + kk * 1032, 513)[:, 1:513] if kk < 4
                       else KB(FF_ABF + kk * 1032, 513)[:, 0:512])
                lhsT = KB(FF_WQ + kk * 3072 + m * 256, 128)
                pe(p, cx, lambda e, o=bank, lt=lhsT, r=rhs, kk=kk:
                   e.matmul(o, lt, r, start=(kk == 0), stop=(kk == 7)), wait=wmm if kk == 0 else ())
            if m < 8:
                isq = m < 4
                tbl_cs = tabs["csq" if isq else "csk"]
                tbl_sn = tabs["snq" if isq else "snk"]
                scol = cx.small[:, SM_QS[l]:SM_QS[l] + 1] if isq else cx.small[:, SM_KS[l]:SM_KS[l] + 1]
                slot = KB(FF_SQB + (m % 4) * 1024, 512)
                act(p, cx, lambda e, s=slot, b=bank: e.activation(out=s, in_=b, func=ACTF.Square),
                    wait=p.wv(cx.s_pe))
                pe(p, cx, lambda e, s=slot: e.matmul(cx.PS[:, 1024:1536], cx.hmask_b[:, :], s,
                                                     start=True, stop=True), wait=p.wv(cx.s_ac, cx.s_dv))
                RRS = _reg(cx, FF_RRS, 512)
                WSC = _reg(cx, FF_WSCR, 512)
                Q2 = _reg(cx, FF_Q2, 512)
                UB = KB(FF_UB, 512)
                dve(p, cx, lambda e: e.tensor_scalar_max(RRS, cx.PS[:, 1024:1536], 1e-24),
                    wait=p.wv(cx.s_pe))
                act(p, cx, lambda e: e.activation(out=WSC, in_=RRS, func=ACTF.Ln), wait=p.wv(cx.s_dv))
                act(p, cx, lambda e: e.activation(out=RRS, in_=WSC, func=ACTF.Exp,
                                                  scale=ccol(p, cx, -0.5)))
                dve(p, cx, lambda e, b=bank, sc=scol: e.scalar_tensor_tensor(Q2, b, sc, RRS,
                                                                             ALU.mult, ALU.mult),
                    wait=p.wv(cx.s_ac))
                dve(p, cx, lambda e, t=tbl_sn: e.tensor_tensor(UB, Q2, t, ALU.mult),
                    wait=p.wv(cx.s_pe))
                pe(p, cx, lambda e: e.matmul(cx.PS[:, 1536:2048], cx.rot_b[:, :], UB,
                                             start=True, stop=True), wait=p.wv(cx.s_dv))
                dve(p, cx, lambda e, t=tbl_cs: e.tensor_tensor(WSC, Q2, t, ALU.mult))
                if isq:
                    dst = KB(FF_QH + m * 4096, 2048)[:, c * CH:(c + 1) * CH]
                else:
                    dst = KB(FF_KF + (m - 4) * 5120, 2560)[:, 512 + c * CH:512 + (c + 1) * CH]
                dve(p, cx, lambda e, o=dst: e.tensor_tensor(o, WSC, cx.PS[:, 1536:2048], ALU.add),
                    wait=p.wv(cx.s_pe, cx.s_st))
            else:
                mv = m - 8
                slot = KB(FF_SQB + (mv % 4) * 1024, 512)
                dve(p, cx, lambda e, s=slot, b=bank: e.tensor_copy(s, b), wait=p.wv(cx.s_pe))
                for b4 in range(4):
                    pe(p, cx, lambda e, s=slot, b4=b4,
                       o=cx.PSB[:, (m % 2) * 512 + b4 * 128:(m % 2) * 512 + (b4 + 1) * 128]:
                       e.transpose(o, s[:, b4 * 128:(b4 + 1) * 128], cx.ident_b[:, :]),
                       wait=p.wv(cx.s_dv))
                for b4 in range(4):
                    dst = KB(FF_VT + (4 + c * 4 + b4) * 1024, 512)[:, mv * 128:(mv + 1) * 128]
                    dve(p, cx, lambda e, o=dst,
                        i=cx.PSB[:, (m % 2) * 512 + b4 * 128:(m % 2) * 512 + (b4 + 1) * 128]:
                        e.tensor_copy(o, i), wait=p.wv(cx.s_pe, cx.s_st))
    if p1_only:
        return
    if h == 0:
        wkv = p.wv(cx.s_dv, cx.s_ac, cx.s_pe)
        for g in range(4):
            gdma(p, cx, d["kcar"][l, g * 128:(g + 1) * 128, :],
                 KB(FF_KF + g * 5120, 2560)[:, T_HALF:T_HALF + 512], wait=wkv)
        for t4 in range(4):
            gdma(p, cx, d["vcar"][l, t4 * 128:(t4 + 1) * 128, :],
                 KB(FF_VT + (16 + t4) * 1024, 512), wait=wkv)
    p.cur_stage = "p2"
    for c in range(NCH):
        p.barrier()
        for s4 in range(4):
            qt = c * 4 + s4
            for hp in range(4):
                qh_t = KB(FF_QH + hp * 4096, 2048)
                kf_t = KB(FF_KF + hp * 5120, 2560)
                wS = {}
                for hh in range(2):
                    sb = hh * 1024
                    lhsT = qh_t[hh * 64:(hh + 1) * 64, qt * 128:qt * 128 + 128]
                    wmm = p.wv(cx.s_ac, cx.s_dv)
                    pe(p, cx, lambda e, lt=lhsT, r=kf_t[hh * 64:(hh + 1) * 64, qt * 128:qt * 128 + 512], sb=sb:
                       e.matmul(cx.PS[:, sb:sb + 512], lt, r, start=True, stop=False), wait=wmm)
                    if h == 0 and qt < 4:
                        for bb in range(4 - qt):
                            pe(p, cx, lambda e, sb=sb, bb=bb:
                               e.matmul(cx.PS[:, sb + bb * 128:sb + (bb + 1) * 128], cx.ident_b[:, :],
                                        cx.negb_b[:, :], start=False, stop=False))
                    pe(p, cx, lambda e, sb=sb: e.matmul(cx.PS[:, sb:sb + 128], cx.ident_b[:, :],
                                                        cx.triu_b[:, :], start=False, stop=True))
                    pe(p, cx, lambda e, lt=lhsT, r=kf_t[hh * 64:(hh + 1) * 64, qt * 128 + 512:qt * 128 + 640], sb=sb:
                       e.matmul(cx.PS[:, sb + 512:sb + 640], lt, r, start=True, stop=False))
                    pe(p, cx, lambda e, sb=sb: e.matmul(cx.PS[:, sb + 512:sb + 640], cx.ident_b[:, :],
                                                        cx.tril_b[:, :], start=False, stop=True))
                    wS[hh] = p.wv(cx.s_pe)
                for hh in range(2):
                    sb = hh * 1024
                    sl = (qt * 8 + hp * 2 + hh) % 4
                    NMX = cx.small[:, 2052 + sl:2053 + sl]
                    RS = cx.small[:, 2056 + sl:2057 + sl]
                    RC = cx.small[:, 2060 + sl:2061 + sl]
                    dve(p, cx, lambda e, sb=sb, o=NMX: e.tensor_reduce(o, cx.PS[:, sb:sb + 640],
                                                                       AX.X, ALU.max, negate=True),
                        wait=wS[hh])
                    PSC = KB(FF_PSC + hh * 1280, 640)
                    act(p, cx, lambda e, sb=sb, o=PSC, b=NMX, a=RS:
                        e.activation(out=o, in_=cx.PS[:, sb:sb + 640], func=ACTF.Exp, bias=b,
                                     accum_out=a), wait=p.wv(cx.s_dv) + wS[hh])
                    dve(p, cx, lambda e, o=RC, i=RS: e.reciprocal(o, i), wait=p.wv(cx.s_ac))
                    vnop(p, cx)
                    dve(p, cx, lambda e, o=PSC, i=RC: e.tensor_scalar_mul(o, o, i))
                    for b5 in range(5):
                        pe(p, cx, lambda e, i=PSC, b5=b5, hh=hh:
                           e.transpose(cx.PSB[:, hh * 1024 + b5 * 128:hh * 1024 + (b5 + 1) * 128],
                                       i[:, b5 * 128:(b5 + 1) * 128], cx.ident_b[:, :]),
                           wait=p.wv(cx.s_dv))
                    PT = KB(FF_PT + hh * 1280, 640)
                    dve(p, cx, lambda e, o=PT, hh=hh: e.tensor_copy(o, cx.PSB[:, hh * 1024:hh * 1024 + 640]),
                        wait=p.wv(cx.s_pe))
                    for b5 in range(5):
                        vt_t = KB(FF_VT + (qt + b5) * 1024, 512)
                        lhsT_v = vt_t[:, hp * 128 + hh * 64:hp * 128 + hh * 64 + 64]
                        pe(p, cx, lambda e, lt=lhsT_v, r=PT[:, b5 * 128:(b5 + 1) * 128], hh=hh, b5=b5:
                           e.matmul(cx.PS[hh * 64:(hh + 1) * 64, 2048:2176], lt, r,
                                    start=(b5 == 0), stop=(b5 == 4)),
                           wait=p.wv(cx.s_dv) if b5 == 0 else ())
                ob = KB(FF_OB + hp * 4096, 2048)
                dve(p, cx, lambda e, o=ob[:, qt * 128:(qt + 1) * 128]: e.tensor_copy(o, cx.PS[:, 2048:2176]),
                    wait=p.wv(cx.s_pe))
    p.cur_stage = "wo"
    p.barrier()
    for c in range(NCH):
        for m in range(8):
            bank = cx.PS[:, (m % 2) * 512:(m % 2) * 512 + 512]
            wmm = p.wv(cx.s_dv, cx.s_ac)
            for kk in range(4):
                lhsT = KB(FF_WO + kk * 2048 + m * 256, 128)
                rhs = KB(FF_OB + kk * 4096, 2048)[:, c * CH:(c + 1) * CH]
                pe(p, cx, lambda e, o=bank, lt=lhsT, r=rhs, kk=kk:
                   e.matmul(o, lt, r, start=(kk == 0), stop=(kk == 3)), wait=wmm if kk == 0 else ())
            asl = _af(cx, m, c * CH + 1, CH)
            dve(p, cx, lambda e, o=asl, b=bank: e.tensor_tensor(o, b, o, ALU.add), wait=p.wv(cx.s_pe))
    if h == 0:
        for g in range(8):
            s0 = SM_AHALO + (2 * l + 1) * 8 + g
            dve(p, cx, lambda e, o=cx.small[:, s0:s0 + 1], i=_af(cx, g, T_HALF, 1):
                e.tensor_copy(o, i), wait=p.wv(cx.s_dv) if g == 0 else ())


def stage_mlp_layer(p, cx, h, l):
    d = cx.dram
    p.cur_stage = "mlp"
    KB = lambda off, n: _regb(cx, off, n)
    if h == 1:
        for g in range(8):
            s0 = SM_AHALO + (2 * l + 1) * 8 + g
            dve(p, cx, lambda e, o=_af(cx, g, 0, 1), i=cx.small[:, s0:s0 + 1]:
                e.tensor_copy(o, i), wait=p.wv(cx.s_dv, cx.s_pe) if g == 0 else ())
    p.barrier()
    for cm in range(NCH):
        wab = p.wv(cx.s_pe, cx.s_ac, cx.s_st)
        ab2 = [KB(FF_ABF2 + g * 1032, 513) for g in range(8)]
        for g in range(8):
            dve(p, cx, lambda e, o=ab2[g], a=_af(cx, g, cm * CH, 513):
                e.tensor_copy(o, a), wait=wab)
        sh_ap = [ab2[g][:, 1:513] if g < 4 else ab2[g][:, 0:512] for g in range(8)]
        ln_stats(p, cx, sh_ap, FF_SQ45)
        ln_rows(p, cx)
        wmr = p.wv(cx.s_pe)
        for g in range(8):
            tmp = _reg(cx, FF_TMP45 + g * 2048, 512)
            src = (_af(cx, g, cm * CH + 1, CH) if g < 4 else _af(cx, g, cm * CH, CH))
            dve(p, cx, lambda e, t=tmp, s=src: e.scalar_tensor_tensor(t, cx.PS[:, 1024:1536],
                                                                      -1.0, s, ALU.mult, ALU.add),
                wait=wmr if g == 0 else ())
            dve(p, cx, lambda e, t=tmp: e.tensor_tensor(t, t, cx.PS[:, 1536:2048], ALU.mult))
            dst = KB(FF_SHLN + g * 4096, 2048)[:, cm * CH:(cm + 1) * CH]
            dve(p, cx, lambda e, o=dst, t=tmp, g=g: e.tensor_scalar(
                o, t, cx.small[:, SM_LG[l] + g:SM_LG[l] + g + 1],
                cx.small[:, SM_LB[l] + g:SM_LB[l] + g + 1], ALU.mult, ALU.add))
    for hp2 in range(2):
        wst = p.wv(cx.s_pe, cx.s_dv, cx.s_ac, cx.s_st, cx.s_ld)
        for kk in range(8):
            for side in range(2):
                gdma(p, cx, KB(FF_W1 + (kk * 22 + side * 11) * 256, 1408),
                     d[f"w1{l}"][kk * 128:(kk + 1) * 128,
                                 side * HIDP + hp2 * 1408:side * HIDP + hp2 * 1408 + 1408],
                     wait=wst)
        for kl in range(11):
            gdma(p, cx, KB(FF_W2 + kl * 2048, 1024),
                 d[f"w2{l}"][(hp2 * 11 + kl) * 128:(hp2 * 11 + kl) * 128 + 128, :], wait=wst)
        for cm in range(NCH):
            p.barrier()
            for i in range(11):
                gi = hp2 * 11 + i
                ab_ = cx.PS[:, (i % 2) * 1024:(i % 2) * 1024 + 512]
                gb_ = cx.PS[:, (i % 2) * 1024 + 512:(i % 2) * 1024 + 1024]
                wmm = p.wv(cx.s_dv, cx.s_ac, cx.s_st)
                for kk in range(8):
                    rhs = KB(FF_SHLN + kk * 4096, 2048)[:, cm * CH:(cm + 1) * CH]
                    pe(p, cx, lambda e, o=ab_, lt=KB(FF_W1 + (kk * 22 + i) * 256, 128), r=rhs, kk=kk:
                       e.matmul(o, lt, r, start=(kk == 0), stop=(kk == 7)), wait=wmm if kk == 0 else ())
                for kk in range(8):
                    rhs = KB(FF_SHLN + kk * 4096, 2048)[:, cm * CH:(cm + 1) * CH]
                    pe(p, cx, lambda e, o=gb_, lt=KB(FF_W1 + (kk * 22 + 11 + i) * 256, 128), r=rhs, kk=kk:
                       e.matmul(o, lt, r, start=(kk == 0), stop=(kk == 7)))
                GLB = KB(FF_GLB, 512)
                act(p, cx, lambda e, b=gb_, gi=gi: e.activation(
                    out=GLB, in_=b, func=ACTF.Gelu,
                    bias=cx.small[:, SM_B1[l] + 22 + gi:SM_B1[l] + 22 + gi + 1]),
                    wait=p.wv(cx.s_pe, cx.s_dv))
                hp_t = KB(FF_HP + i * 1024, 512)
                dve(p, cx, lambda e, o=hp_t, b=ab_, gi=gi: e.scalar_tensor_tensor(
                    o, b, cx.small[:, SM_B1[l] + gi:SM_B1[l] + gi + 1], GLB, ALU.add, ALU.mult),
                    wait=p.wv(cx.s_ac, cx.s_pe))
            for mp in range(2):
                for m4 in range(4):
                    m = mp * 4 + m4
                    bank = cx.PS[:, m4 * 512:(m4 + 1) * 512]
                    wmm = p.wv(cx.s_dv, cx.s_ac)
                    for kl in range(11):
                        pe(p, cx, lambda e, o=bank, lt=KB(FF_W2 + kl * 2048 + m * 256, 128),
                           r=KB(FF_HP + kl * 1024, 512), kl=kl:
                           e.matmul(o, lt, r, start=(kl == 0), stop=(kl == 10)),
                           wait=wmm if kl == 0 else ())
                for m4 in range(4):
                    m = mp * 4 + m4
                    bank = cx.PS[:, m4 * 512:(m4 + 1) * 512]
                    asl = _af(cx, m, cm * CH + 1, CH)
                    if hp2 == 0:
                        dve(p, cx, lambda e, o=asl, b=bank: e.tensor_tensor(o, b, o, ALU.add),
                            wait=p.wv(cx.s_pe))
                    else:
                        dve(p, cx, lambda e, o=asl, b=bank, m=m: e.scalar_tensor_tensor(
                            o, b, cx.small[:, SM_B2[l] + m:SM_B2[l] + m + 1], o, ALU.add, ALU.add),
                            wait=p.wv(cx.s_pe))
    if h == 0 and l == 0:
        for g in range(8):
            s0 = SM_AHALO + 16 + g
            dve(p, cx, lambda e, o=cx.small[:, s0:s0 + 1], i=_af(cx, g, T_HALF, 1):
                e.tensor_copy(o, i), wait=p.wv(cx.s_dv) if g == 0 else ())
    if h == 1 and l == 0:
        for g in range(8):
            s0 = SM_AHALO + 16 + g
            dve(p, cx, lambda e, o=_af(cx, g, 0, 1), i=cx.small[:, s0:s0 + 1]:
                e.tensor_copy(o, i), wait=p.wv(cx.s_dv, cx.s_pe) if g == 0 else ())


def stage_combine(p, cx, h):
    d = cx.dram
    p.cur_stage = "combine"
    p.barrier()
    for c in range(NCH):
        t0 = h * T_HALF + c * CH
        for g in range(8):
            xch = _reg(cx, FF_XCH + g * 2048, CH)
            ldma(p, cx, xch, d["Xbuf"][g * 128:(g + 1) * 128, t0:t0 + CH],
                 wait=p.wv(*p.allsems))
            dve(p, cx, lambda e, o=xch, a=_af(cx, g, c * CH + 1, CH): e.scalar_tensor_tensor(
                o, a, 0.5, o, ALU.mult, ALU.add), wait=p.wv(*p.allsems))
            gdma(p, cx, d["Xbuf"][g * 128:(g + 1) * 128, t0:t0 + CH], xch,
                 wait=p.wv(*p.allsems))
    p.emit("sync", lambda e: e.nop(), wait=p.wv(cx.s_st))



def stage_x_in(p, cx):
    d = cx.dram
    tm = FF_QH
    for c in range(T_TOT // 512):
        wld = p.wv(cx.s_pe)
        for s in range(4):
            ldma(p, cx, _reg(cx, tm + s * 4096, 1024),
                 d["hs"][c * 512 + s * 128: c * 512 + (s + 1) * 128, :], wait=wld)
        for g in range(8):
            bank = cx.PS[:, (g % 2) * 512:(g % 2) * 512 + 512]
            for s in range(4):
                src = _reg(cx, tm + s * 4096 + g * 512, 128)
                pe(p, cx, lambda e, o=bank[:, s * 128:(s + 1) * 128], i=src:
                   e.transpose(o, i, cx.ident_f[:, :]), wait=p.wv(cx.s_ld, cx.s_dv))
            dst = _reg(cx, FF_OB + g * 2048, 512)
            dve(p, cx, lambda e, o=dst, i=bank: e.tensor_copy(o, i),
                wait=p.wv(cx.s_pe, cx.s_st))
            gdma(p, cx, d["Xbuf"][g * 128:(g + 1) * 128, c * 512:(c + 1) * 512], dst,
                 wait=p.wv(cx.s_dv))
    p.emit("sync", lambda e: e.nop(), wait=p.wv(cx.s_st))


def stage_x_out(p, cx):
    d = cx.dram
    tm = FF_QH
    fm = FF_OB
    for c in range(T_TOT // 512):
        wg = p.wv(cx.s_st, cx.s_pe)
        for g in range(8):
            ldma(p, cx, _reg(cx, fm + g * 2048, 512),
                 d["Xbuf"][g * 128:(g + 1) * 128, c * 512:(c + 1) * 512], wait=wg)
        for s in range(4):
            bank = cx.PS[:, (s % 2) * 512:(s % 2) * 512 + 512]
            for g in range(8):
                src = _reg(cx, fm + g * 2048 + s * 512, 128)
                pe(p, cx, lambda e, o=bank[:, (g % 4) * 128:(g % 4 + 1) * 128], i=src:
                   e.transpose(o, i, cx.ident_f[:, :]), wait=p.wv(cx.s_ld, cx.s_dv))
                if g % 4 == 3:
                    dst = _reg(cx, tm + s * 4096 + (g - 3) * 512, 512)
                    dve(p, cx, lambda e, o=dst, i=bank: e.tensor_copy(o, i), wait=p.wv(cx.s_pe))
            gdma(p, cx, d["out"][c * 512 + s * 128: c * 512 + (s + 1) * 128, :],
                 _reg(cx, tm + s * 4096, 1024), wait=p.wv(cx.s_dv))
    p.emit("gpsimd", lambda e: e.nop(), wait=p.wv(cx.s_st))


def dump_af_to_xbuf(p, cx, h):
    d = cx.dram
    for g in range(8):
        src = cx.slab[:, (A_F // 4) + g * 2049 + 1:(A_F // 4) + g * 2049 + 1 + T_HALF]
        gdma(p, cx, d["Xbuf"][g * 128:(g + 1) * 128, h * T_HALF:(h + 1) * T_HALF], src,
             wait=p.wv(cx.s_dv, cx.s_pe, cx.s_ac))
    p.emit("sync", lambda e: e.nop(), wait=p.wv(cx.s_st))



def build_program(debug=None):
    nc = bass.Bass("TRN2", target_bir_lowering=False, debug=False, detect_race_conditions=False)
    stack = contextlib.ExitStack()
    with stack:
        p = Prog(nc, stack)
        dram = {}
        for nm, shp in _IN_SHAPES.items():
            dram[nm] = nc.dram_tensor(nm, shp, F32, kind="ExternalInput").ap()
        dram["out"] = nc.dram_tensor("out", [T_TOT, D_MODEL], F32, kind="ExternalOutput").ap()
        for nm, shp in [("Xbuf", [D_MODEL, T_TOT]), ("xin_s", [D_INNER, T_TOT]),
                        ("xt_s", [D_INNER, T_TOT]), ("z_s", [D_INNER, T_TOT])]:
            dram[nm] = nc.dram_tensor(nm, shp, F32).ap()
        dram["lnx_s"] = nc.dram_tensor("lnx_s", [D_MODEL, T_TOT], F32).ap()
        for nm in ("kcar", "vcar"):
            dram[nm] = nc.dram_tensor(nm, [2, 512, 512], BF16).ap()

        cx = setup(p, dram)
        init_consts(p, cx)
        stage_x_in(p, cx)
        if debug == "xcopy":
            pass
        elif debug == "ln1":
            for h in range(N_HALF):
                stage_ln1(p, cx, h)
                dump_af_to_xbuf(p, cx, h)
        elif debug == "consts":
            wd = p.wv(cx.s_dv, cx.s_ac, cx.s_pe, cx.s_gp, cx.s_st, cx.s_ld)
            gdma(p, cx, dram["Xbuf"][0:128, 0:5], cx.small[:, SM_DIMIDX:SM_DIMIDX + 5], wait=wd)
            gdma(p, cx, dram["Xbuf"][0:128, 5:9], cx.small[:, SM_TMP:SM_TMP + 4], wait=wd)
            p.emit("sync", lambda e: e.nop(), wait=p.wv(cx.s_st))
        elif debug and debug.startswith("tbl"):
            tabs = tables_chunk(p, cx, int(debug[3:] or 0))
            for i, k in enumerate(("csq", "snq", "csk", "snk")):
                gdma(p, cx, dram["Xbuf"][i * 128:(i + 1) * 128, 0:CH], tabs[k],
                     wait=p.wv(cx.s_dv, cx.s_ac, cx.s_pe, cx.s_gp))
            p.emit("sync", lambda e: e.nop(), wait=p.wv(cx.s_st))
        elif debug == "p1":
            stage_ln1(p, cx, 0)
            for g in range(8):
                p.emit("vector", lambda e, g=g: e.memset(_af(cx, g, 0, 1), 0.0),
                       wait=p.wv(cx.s_dv, cx.s_ac, cx.s_pe), inc=(cx.s_dv, 1))
            stage_attn_layer(p, cx, 0, 0, p1_only=True)
            wd = p.wv(cx.s_dv, cx.s_ac, cx.s_pe)
            for g in range(4):
                qh = _regb(cx, FF_QH + g * 4096, 2048)
                dve(p, cx, lambda e, g=g, i=qh: e.tensor_copy(_reg(cx, FF_TSC, 512), i[:, 0:512]), wait=wd)
                gdma(p, cx, dram["Xbuf"][g * 128:(g + 1) * 128, 0:512], _reg(cx, FF_TSC, 512),
                     wait=p.wv(cx.s_dv))
            for g in range(4):
                kf = _regb(cx, FF_KF + g * 5120, 2560)
                dve(p, cx, lambda e, g=g, i=kf: e.tensor_copy(_reg(cx, FF_TSC, 512), i[:, 512:1024]),
                    wait=p.wv(cx.s_st))
                gdma(p, cx, dram["Xbuf"][512 + g * 128:512 + (g + 1) * 128, 0:512], _reg(cx, FF_TSC, 512),
                     wait=p.wv(cx.s_dv))
            for s in range(4):
                vt = _regb(cx, FF_VT + (4 + s) * 1024, 512)
                dve(p, cx, lambda e, i=vt: e.tensor_copy(_reg(cx, FF_TSC, 512), i),
                    wait=p.wv(cx.s_st))
                gdma(p, cx, dram["Xbuf"][512 + s * 128:512 + (s + 1) * 128, 512:1024],
                     _reg(cx, FF_TSC, 512), wait=p.wv(cx.s_dv))
            p.emit("sync", lambda e: e.nop(), wait=p.wv(cx.s_st))
        elif debug == "attn0":
            for h in range(N_HALF):
                stage_ln1(p, cx, h)
                stage_ffn(p, cx, h, only="attn0")
                dump_af_to_xbuf(p, cx, h)
        elif debug == "l0":
            for h in range(N_HALF):
                stage_ln1(p, cx, h)
                stage_ffn(p, cx, h, only="l0")
                dump_af_to_xbuf(p, cx, h)
        elif debug == "mamba":
            for h in range(N_HALF):
                stage_mamba(p, cx, h)
        elif debug == "ffn1":
            for h in range(N_HALF):
                stage_ln1(p, cx, h)
                stage_ffn(p, cx, h)
                dump_af_to_xbuf(p, cx, h)
        else:
            full_chain(p, cx)
        stage_x_out(p, cx)
        p.build()
    return nc


def full_chain(p, cx):
    for h in range(N_HALF):
        stage_ln1(p, cx, h)
        stage_ffn(p, cx, h)
        stage_combine(p, cx, h)
    for h in range(N_HALF):
        stage_mamba(p, cx, h)
    for h in range(N_HALF):
        stage_ln1(p, cx, h)
        stage_ffn(p, cx, h)
        stage_combine(p, cx, h)


def kernel(**inputs):
    pre = _prep_inputs(inputs)
    hs_full = np.ascontiguousarray(np.asarray(inputs["L0_attn_hidden_states"], np.float32))
    nc = build_program()
    in_maps = []
    for c in range(8):
        m = dict(pre)
        m["hs"] = hs_full[c // 2]
        in_maps.append(m)
    res = run_bass_kernel_spmd(nc, in_maps, core_ids=list(range(8)))
    outs = [res.results[c]["out"] for c in (0, 2, 4, 6)]
    return np.stack(outs, axis=0)



NM3 = T_HALF // TCM


def stage_mamba(p, cx, h):
    d = cx.dram
    KB = lambda off, n: _regb(cx, off, n)
    RG = lambda off, n: _reg(cx, off, n)
    p.barrier()
    p.cur_stage = "m1"

    wu = p.wv(cx.s_pe, cx.s_dv, cx.s_ac, cx.s_st, cx.s_ld)
    for kk in range(8):
        gdma(p, cx, KB(MB_WIN + kk * 8192, 4096),
             d["inT"][kk * 128:(kk + 1) * 128, :], wait=wu)
    if h == 0:
        dve(p, cx, lambda e: e.memset(cx.small[:, SM_STATE:SM_STATE + 256], 0.0), wait=wu)
    for c in range(NCH):
        p.barrier()
        t0 = h * T_HALF + c * CH
        xch = [RG(MB_XCH + g * 2048, CH) for g in range(8)]
        xb = [KB(MB_XB + g * 1024, CH) for g in range(8)]
        wld = p.wv(cx.s_dv, cx.s_ac, cx.s_pe, cx.s_st)
        for g in range(8):
            ldma(p, cx, xch[g], d["Xbuf"][g * 128:(g + 1) * 128, t0:t0 + CH], wait=wld)
            gdma(p, cx, xb[g], d["Xbuf"][g * 128:(g + 1) * 128, t0:t0 + CH], wait=wld)
        ln_stats(p, cx, xb, MB_SQ)
        ln_rows(p, cx, mamba=True)
        wmr = p.wv(cx.s_pe, cx.s_st)
        for g in range(8):
            lnx = RG(MB_LNX + g * 2048, CH)
            lnm = KB(MB_LNM + g * 1024, CH)
            nwc = cx.small[:, SM_NW + g:SM_NW + g + 1]
            dve(p, cx, lambda e, x_=xch[g]: e.scalar_tensor_tensor(x_, cx.PS[:, 1024:1024 + CH],
                                                                   -1.0, x_, ALU.mult, ALU.add),
                wait=wmr if g == 0 else ())
            dve(p, cx, lambda e, o=lnx, x_=xch[g]: e.tensor_tensor(o, x_, cx.PS[:, 0:CH], ALU.mult))
            dve(p, cx, lambda e, x_=xch[g]: e.tensor_tensor(x_, x_, cx.PS[:, 1536:1536 + CH], ALU.mult))
            dve(p, cx, lambda e, o=lnm, x_=xch[g], w=nwc: e.tensor_scalar_mul(o, x_, w))
            gdma(p, cx, d["lnx_s"][g * 128:(g + 1) * 128, t0:t0 + CH], lnx, wait=p.wv(cx.s_dv))
        for m in range(32):
            bank = cx.PS[:, (m % 2) * 512:(m % 2) * 512 + 512]
            wmm = p.wv(cx.s_dv, cx.s_ac, cx.s_st)
            for kk in range(8):
                lhsT = KB(MB_WIN + kk * 8192 + m * 256, 128)
                pe(p, cx, lambda e, o=bank, lt=lhsT, r=KB(MB_LNM + kk * 1024, CH), kk=kk:
                   e.matmul(o, lt, r, start=(kk == 0), stop=(kk == 7)), wait=wmm if kk == 0 else ())
            slot = RG(MB_XZS + (m % 4) * 2048, CH)
            dve(p, cx, lambda e, o=slot, b=bank: e.tensor_copy(o, b),
                wait=p.wv(cx.s_pe, cx.s_st))
            dst = (d["xin_s"][(m % 16) * 128:(m % 16 + 1) * 128, t0:t0 + CH] if m < 16
                   else d["z_s"][(m - 16) * 128:(m - 15) * 128, t0:t0 + CH])
            gdma(p, cx, dst, slot, wait=p.wv(cx.s_dv))

    p.cur_stage = "m2"
    wu = p.wv(cx.s_pe, cx.s_dv, cx.s_ac, cx.s_st, cx.s_ld)
    p.emit("sync", lambda e: e.nop(), wait=p.wv(cx.s_st))
    ldma(p, cx, cx.small[:, SM_CW:SM_CW + 64], d["convw"][:, :], wait=wu)
    for kk in range(16):
        gdma(p, cx, KB(MB_WXP + kk * 224, 112), d["xpT"][kk * 128:(kk + 1) * 128, :], wait=wu)
    for c in range(NCH):
        p.barrier()
        t0 = h * T_HALF + c * CH
        wld = p.wv(cx.s_dv, cx.s_ac, cx.s_pe, cx.s_st)
        for g in range(16):
            tile = RG(MB_XIN + g * 2060, 515)
            if t0 == 0:
                dve(p, cx, lambda e, o=tile[:, 0:3]: e.memset(o, 0.0), wait=wld if g == 0 else ())
                ldma(p, cx, tile[:, 3:515], d["xin_s"][g * 128:(g + 1) * 128, 0:CH], wait=wld)
            else:
                ldma(p, cx, tile[:, 0:515], d["xin_s"][g * 128:(g + 1) * 128, t0 - 3:t0 + CH],
                     wait=wld)
        for g in range(16):
            tile = RG(MB_XIN + g * 2060, 515)
            xt = RG(MB_XT + g * 2048, CH)
            xtb = KB(MB_XTB + g * 1024, CH)
            wcv = p.wv(cx.s_ld, cx.s_dv, cx.s_ac, cx.s_pe, cx.s_st)
            dve(p, cx, lambda e, o=xt, i=tile, w=cx.small[:, SM_CW + g:SM_CW + g + 1]:
                e.tensor_scalar_mul(o, i[:, 0:CH], w), wait=wcv)
            for k in range(1, 4):
                dve(p, cx, lambda e, o=xt, i=tile, k=k,
                    w=cx.small[:, SM_CW + k * 16 + g:SM_CW + k * 16 + g + 1]:
                    e.scalar_tensor_tensor(o, i[:, k:k + CH], w, o, ALU.mult, ALU.add))
            act(p, cx, lambda e, o=xt, g=g: e.activation(
                out=o, in_=o, func=ACTF.Silu, bias=cx.small[:, SM_CB + g:SM_CB + g + 1]),
                wait=p.wv(cx.s_dv))
            dve(p, cx, lambda e, o=xtb, i=xt: e.tensor_copy(o, i), wait=p.wv(cx.s_ac))
            gdma(p, cx, d["xt_s"][g * 128:(g + 1) * 128, t0:t0 + CH], xt, wait=p.wv(cx.s_dv))
        bank = cx.PS[:, 1024:1536]
        wmm = p.wv(cx.s_dv, cx.s_ac)
        for kk in range(16):
            pe(p, cx, lambda e, lt=KB(MB_WXP + kk * 224, 112), r=KB(MB_XTB + kk * 1024, CH), kk=kk:
               e.matmul(cx.PS[0:112, 1024:1536], lt, r, start=(kk == 0), stop=(kk == 15)),
               wait=wmm if kk == 0 else ())
        dve(p, cx, lambda e, o=RG(MB_XP, 2048)[0:112, c * CH:(c + 1) * CH]:
            e.tensor_copy(o, cx.PS[0:112, 1024:1536]), wait=p.wv(cx.s_pe))

    p.cur_stage = "m3"
    wu = p.wv(cx.s_pe, cx.s_dv, cx.s_ac, cx.s_st, cx.s_ld)
    for kk in range(16):
        gdma(p, cx, KB(M3_WOUT + kk * 2048, 1024), d["outT"][kk * 128:(kk + 1) * 128, :], wait=wu)
    gdma(p, cx, KB(M3_WDT, 2048)[0:64, :], d["dtT"][0:64, :], wait=wu)
    dve(p, cx, lambda e: e.tensor_copy(KB(M3_DTRB, 2048)[0:64, :], RG(MB_XP, 2048)[0:64, :]),
        wait=wu)
    for c8 in range(NM3):
        p.barrier()
        t0 = h * T_HALF + c8 * TCM
        cs = c8 * TCM
        seln = RG(M3_DA, 2048)
        ldma(p, cx, seln, d["seln"][:, :], wait=p.wv(cx.s_dv, cx.s_ac, cx.s_pe, cx.s_st))
        for n in range(16):
            wmm = p.wv(cx.s_dv, cx.s_ac, cx.s_ld)
            pe(p, cx, lambda e, lt=seln[64:80, n * 128:(n + 1) * 128],
               r=RG(MB_XP, 2048)[64:80, cs:cs + TCM]:
               e.matmul(cx.PS[:, 2048:2048 + TCM], lt, r, start=True, stop=True), wait=wmm)
            dve(p, cx, lambda e, o=RG(M3_BBC + n * 1024, TCM): e.tensor_copy(o, cx.PS[:, 2048:2048 + TCM]),
                wait=p.wv(cx.s_pe))
            pe(p, cx, lambda e, lt=seln[96:112, n * 128:(n + 1) * 128],
               r=RG(MB_XP, 2048)[96:112, cs:cs + TCM]:
               e.matmul(cx.PS[:, 2560:2560 + TCM], lt, r, start=True, stop=True,
                        tile_position=(96, 0)))
            dve(p, cx, lambda e, o=RG(M3_CBC + n * 1024, TCM): e.tensor_copy(o, cx.PS[:, 2560:2560 + TCM]),
                wait=p.wv(cx.s_pe))
        for g in range(16):
            DT = RG(M3_DT, TCM)
            UM = RG(M3_UM, TCM)
            XTG = RG(M3_XTG, TCM)
            ZG = RG(M3_ZG, TCM)
            SZG = RG(M3_SZG, TCM)
            DA = RG(M3_DA, 16 * TCM)
            BH = RG(M3_BH, 16 * TCM)
            wld = p.wv(cx.s_dv, cx.s_ac, cx.s_st)
            ldma(p, cx, XTG, d["xt_s"][g * 128:(g + 1) * 128, t0:t0 + TCM], wait=wld)
            ldma(p, cx, ZG, d["z_s"][g * 128:(g + 1) * 128, t0:t0 + TCM], wait=wld)
            wmm = p.wv(cx.s_dv, cx.s_ac)
            pe(p, cx, lambda e, lt=KB(M3_WDT + g * 256, 128)[0:64, :], r=KB(M3_DTRB, 2048)[0:64, cs:cs + TCM]:
               e.matmul(cx.PS[:, 1024:1024 + TCM], lt, r, start=True, stop=True), wait=wmm)
            act(p, cx, lambda e, g=g: e.activation(out=RG(M3_SZG, TCM), in_=cx.PS[:, 1024:1024 + TCM],
                                                   func=ACTF.Exp,
                                                   bias=cx.small[:, SM_DTB + g:SM_DTB + g + 1]),
                wait=p.wv(cx.s_pe))
            act(p, cx, lambda e: e.activation(out=DT, in_=RG(M3_SZG, TCM), func=ACTF.Ln,
                                              bias=ccol(p, cx, 1.0)))
            dve(p, cx, lambda e, o=UM, i=XTG: e.tensor_tensor(o, DT, i, ALU.mult),
                wait=p.wv(cx.s_ac, cx.s_ld))
            for n in range(16):
                acol = cx.small[:, SM_ANEG + g * 16 + n:SM_ANEG + g * 16 + n + 1]
                act(p, cx, lambda e, o=DA[:, n * TCM:(n + 1) * TCM], a=acol:
                    e.activation(out=o, in_=DT, func=ACTF.Exp, scale=a),
                    wait=p.wv(cx.s_dv) if n == 0 else ())
            for n in range(16):
                dve(p, cx, lambda e, o=BH[:, n * TCM:(n + 1) * TCM], b=RG(M3_BBC + n * 1024, TCM):
                    e.tensor_tensor(o, UM, b, ALU.mult), wait=p.wv(cx.s_ac) if n == 0 else ())
            for n in range(16):
                st = cx.small[:, SM_STATE + g * 16 + n:SM_STATE + g * 16 + n + 1]
                sl = BH[:, n * TCM:(n + 1) * TCM]
                dve(p, cx, lambda e, o=sl, a=DA[:, n * TCM:(n + 1) * TCM], b=sl, s=st:
                    e.tensor_tensor_scan(o, a, b, s, ALU.mult, ALU.add))
            dve(p, cx, lambda e, o=cx.small[:, SM_STATE + g * 16:SM_STATE + (g + 1) * 16],
                i=BH[:, TCM - 1::TCM]: e.tensor_copy(o, i))
            dav = DA.rearrange("p (n t) -> p n t", n=16)
            da_t = DA.rearrange("p (t n) -> p t n", n=16)
            for n in range(16):
                dve(p, cx, lambda e, o=da_t[:, :, n], hh_=BH[:, n * TCM:(n + 1) * TCM],
                    cc=RG(M3_CBC + n * 1024, TCM): e.tensor_tensor(o, hh_, cc, ALU.mult))
            dve(p, cx, lambda e: e.tensor_reduce(UM, da_t, AX.X, ALU.add))
            dve(p, cx, lambda e, i=XTG, dc=cx.small[:, SM_DVEC + g:SM_DVEC + g + 1]:
                e.scalar_tensor_tensor(UM, i, dc, UM, ALU.mult, ALU.add))
            act(p, cx, lambda e, i=ZG: e.activation(out=SZG, in_=i, func=ACTF.Silu),
                wait=p.wv(cx.s_dv))
            dve(p, cx, lambda e, o=KB(M3_YSB + g * 512, TCM): e.tensor_tensor(o, UM, SZG, ALU.mult),
                wait=p.wv(cx.s_ac))
        for m in range(8):
            bank = cx.PS[:, (m % 2) * 512:(m % 2) * 512 + 512][:, 0:TCM]
            wmm = p.wv(cx.s_dv, cx.s_ac, cx.s_ld)
            for kk in range(16):
                pe(p, cx, lambda e, o=bank, lt=KB(M3_WOUT + kk * 2048 + m * 256, 128),
                   r=KB(M3_YSB + kk * 512, TCM), kk=kk:
                   e.matmul(o, lt, r, start=(kk == 0), stop=(kk == 15)),
                   wait=wmm if kk == 0 else ())
            xrm = RG(M3_DA + m * 1024, TCM)
            lnxc = RG(M3_DA + 8192 + m * 1024, TCM)
            wrd = p.wv(cx.s_dv, cx.s_ac, cx.s_st)
            ldma(p, cx, xrm, d["Xbuf"][m * 128:(m + 1) * 128, t0:t0 + TCM], wait=wrd)
            ldma(p, cx, lnxc, d["lnx_s"][m * 128:(m + 1) * 128, t0:t0 + TCM], wait=wrd)
            dve(p, cx, lambda e, o=lnxc, b=bank: e.tensor_tensor(o, b, o, ALU.add),
                wait=p.wv(cx.s_pe, cx.s_ld))
            dve(p, cx, lambda e, o=xrm, i=lnxc: e.tensor_tensor(o, i, o, ALU.add))
            gdma(p, cx, d["Xbuf"][m * 128:(m + 1) * 128, t0:t0 + TCM], xrm, wait=p.wv(cx.s_dv))
    p.emit("sync", lambda e: e.nop(), wait=p.wv(cx.s_st))


# version 09 (final)
# speedup vs baseline: 1.1177x; 1.0803x over previous
import contextlib
import math
import numpy as np
import concourse.bass as bass
import concourse.mybir as mybir
from concourse.bass_utils import run_bass_kernel_spmd

F32 = mybir.dt.float32
BF16 = mybir.dt.bfloat16
I32 = mybir.dt.int32
ALU = mybir.AluOpType
ACTF = mybir.ActivationFunctionType
AX = mybir.AxisListType

D_MODEL = 1024
D_INNER = 2048
D_STATE = 16
DT_RANK = 64
HEADS = 8
WINDOW = 512
HID = 2730
HIDP = 2816
QK_SCALE = 8.0
T_TOT = 4096
NEG_BIG = -1e38
DM_T = 8
DI_T = 16
N_HALF = 2
T_HALF = 2048
CH = 512
NCH = T_HALF // CH
TCM = 256
PI = math.pi
TWO_PI_HI = np.float32(2 * PI)
TWO_PI_LO = np.float32(2 * PI - float(np.float32(2 * PI)))
INV_2PI = np.float32(1.0 / (2 * PI))
PI_HALF = np.float32(PI / 2)


class _Sem:
    __slots__ = ("h", "name", "n", "emitted")

    def __init__(self, h, name):
        self.h, self.name, self.n, self.emitted = h, name, 0, 0


class Prog:
    ENGINES = ("sync", "scalar", "vector", "gpsimd", "tensor")

    def __init__(self, nc, stack):
        self.nc, self.stack = nc, stack
        self.prog = {e: [] for e in self.ENGINES}
        self._semid = 0

    def sem(self, name=None):
        name = name or f"s{self._semid}"
        self._semid += 1
        h = self.stack.enter_context(self.nc.semaphore(name))
        return _Sem(h, name)

    def sbuf(self, name, shape, dt):
        return self.stack.enter_context(self.nc.sbuf_tensor(name, shape, dt))

    def psum(self, name, shape, dt=F32):
        return self.stack.enter_context(self.nc.psum_tensor(name, shape, dt))

    SAFE = False
    BARRIERS = False
    SAFE_STAGES = set()
    cur_stage = ""

    def emit(self, eng, fn, wait=None, inc=None):
        if (Prog.SAFE or (self.cur_stage in Prog.SAFE_STAGES)) and getattr(self, "allsems", None):
            wait = [(s, s.n) for s in self.allsems]
        if inc is not None:
            sem, k = inc
            sem.n += k
            inc = (sem, sem.n)
        self.prog[eng].append((fn, list(wait or []), inc))

    def wv(self, *sems):
        return [(s, s.n) for s in sems]

    def barrier(self):
        if not Prog.BARRIERS:
            return
        w = [(s, s.n) for s in self.allsems]
        for eng in self.ENGINES:
            self.prog[eng].append(((lambda e: e.nop()), list(w), None))

    def build(self):
        with self.nc.Block() as block:
            for eng_name in self.ENGINES:
                seq = self.prog[eng_name]
                if not seq:
                    continue

                def body(eng, seq=seq):
                    for fn, waits, inc in seq:
                        for s, v in waits:
                            if v > 0:
                                eng.wait_ge(s.h, v)
                        inst = fn(eng)
                        if inc is not None:
                            s, v = inc
                            inst.then_inc(s.h, v - s.emitted)
                            s.emitted = v

                getattr(block, eng_name)(body)



def _colvec(v):
    v = np.asarray(v, np.float32).reshape(-1)
    return np.ascontiguousarray(v.reshape(-1, 128).T)


def _prep_inputs(ix):
    f = np.float32
    pre = {}
    for l in range(2):
        pre[f"wqkv{l}"] = np.ascontiguousarray(ix["L0_ffn_wqkv"][l], f)
        pre[f"wo{l}"] = np.ascontiguousarray(ix["L0_ffn_wo"][l], f)
        w1 = np.asarray(ix["L0_ffn_w1"][l], f)
        w1p = np.zeros((D_MODEL, 2 * HIDP), f)
        w1p[:, :HID] = w1[:, :HID]
        w1p[:, HIDP:HIDP + HID] = w1[:, HID:]
        pre[f"w1{l}"] = w1p
        w2 = np.asarray(ix["L0_ffn_w2"][l], f)
        w2p = np.zeros((HIDP, D_MODEL), f)
        w2p[:HID] = w2
        pre[f"w2{l}"] = w2p
        b1 = np.asarray(ix["L0_ffn_b1"][l], f)
        b1p = np.zeros((2 * HIDP,), f)
        b1p[:HID] = b1[:HID]
        b1p[HIDP:HIDP + HID] = b1[HID:]
        pre[f"b1{l}"] = _colvec(b1p)
        pre[f"b2{l}"] = _colvec(ix["L0_ffn_b2"][l])
        pre[f"lg{l}"] = _colvec(ix["L0_ffn_ln_g"][l])
        pre[f"lb{l}"] = _colvec(ix["L0_ffn_ln_b"][l])
        pre[f"qs{l}"] = np.ascontiguousarray(np.tile(np.asarray(ix["L0_ffn_q_scale"][l], f), 2).reshape(128, 1))
        pre[f"ks{l}"] = np.ascontiguousarray(np.tile(np.asarray(ix["L0_ffn_k_scale"][l], f), 2).reshape(128, 1))
    pre["inT"] = np.ascontiguousarray(np.asarray(ix["L0_attn_in_proj_w"], f).T)
    pre["outT"] = np.ascontiguousarray(np.asarray(ix["L0_attn_out_proj_w"], f).T)
    xpT = np.asarray(ix["L0_attn_x_proj_w"], f).T
    xpp = np.zeros((D_INNER, 112), f)
    xpp[:, 0:80] = xpT[:, 0:80]
    xpp[:, 96:112] = xpT[:, 80:96]
    pre["xpT"] = xpp
    pre["dtT"] = np.ascontiguousarray(np.asarray(ix["L0_attn_dt_proj_w"], f).T)
    cw = np.asarray(ix["L0_attn_conv_w"], f)[:, 0, :]
    pre["convw"] = np.ascontiguousarray(
        np.concatenate([_colvec(cw[:, k]) for k in range(4)], axis=1))
    pre["nw"] = _colvec(ix["L0_attn_norm_weight"])
    pre["cb"] = _colvec(ix["L0_attn_conv_b"])
    pre["dtb"] = _colvec(ix["L0_attn_dt_proj_b"])
    pre["Dvec"] = _colvec(ix["L0_attn_D"])
    al = np.asarray(ix["L0_attn_A_log"], f).reshape(DI_T, 128, D_STATE)
    pre["AlogT"] = np.ascontiguousarray(al.transpose(1, 0, 2).reshape(128, DI_T * D_STATE))
    pre["dimidx"] = np.ascontiguousarray((2.0 * (np.arange(128) % 32)).reshape(128, 1).astype(f))
    pre["ident"] = np.ascontiguousarray(np.eye(128, dtype=f))
    R = np.zeros((128, 128), f)
    for m in range(128):
        if m % 64 < 32:
            R[(m + 32), m] = -1.0
        else:
            R[(m - 32), m] = 1.0
    pre["rotm"] = R
    bm = np.zeros((128, 128), f)
    bm[:64, :64] = 1.0
    bm[64:, 64:] = 1.0
    pre["headmask"] = bm
    sel = np.zeros((128, 16 * 128), f)
    for n in range(16):
        sel[64 + n, n * 128:(n + 1) * 128] = 1.0
        sel[96 + n, n * 128:(n + 1) * 128] = 1.0
    pre["seln"] = sel
    pre["ten_thousand"] = np.full((1, 1), 10000.0, f)
    dimidx = (2.0 * (np.arange(128) % 32)).astype(np.float64)
    l10k = np.log(np.float64(10000.0))
    invf = np.exp(dimidx * np.float64(-1.0 / 64.0) * l10k)
    t_all = np.arange(T_TOT, dtype=np.float64)
    fr = invf[:, None] * t_all[None, :]
    c = np.float64(np.float32(1.0 / (2.0 * math.pi)))
    pre["rotk1"] = np.floor(fr * c + 0.5).astype(f)
    pre["rotk2"] = np.floor(fr * c + 0.75).astype(f)
    return pre


_IN_SHAPES = {
    "hs": [T_TOT, D_MODEL],
    "inT": [D_MODEL, 2 * D_INNER], "outT": [D_INNER, D_MODEL],
    "xpT": [D_INNER, 112], "dtT": [DT_RANK, D_INNER], "convw": [128, 64],
    "nw": [128, 8], "cb": [128, 16], "dtb": [128, 16], "Dvec": [128, 16],
    "AlogT": [128, DI_T * D_STATE], "dimidx": [128, 1], "ident": [128, 128],
    "rotm": [128, 128], "headmask": [128, 128], "seln": [128, 16 * 128],
    "ten_thousand": [1, 1],
    "rotk1": [128, T_TOT], "rotk2": [128, T_TOT],
}
for _l in range(2):
    _IN_SHAPES.update({
        f"wqkv{_l}": [D_MODEL, 1536], f"wo{_l}": [512, D_MODEL],
        f"w1{_l}": [D_MODEL, 2 * HIDP], f"w2{_l}": [HIDP, D_MODEL],
        f"b1{_l}": [128, 44], f"b2{_l}": [128, 8], f"lg{_l}": [128, 8],
        f"lb{_l}": [128, 8], f"qs{_l}": [128, 1], f"ks{_l}": [128, 1],
    })



class Ctx:
    pass


def _reg(cx, byte_off, n_f32):
    assert byte_off % 4 == 0
    c = byte_off // 4
    assert c + n_f32 <= cx.SLAB_COLS, (byte_off, n_f32)
    return cx.slab[:, c:c + n_f32]


def _regb(cx, byte_off, n_bf16):
    assert byte_off % 4 == 0
    c = byte_off // 4
    assert c * 4 + n_bf16 * 2 <= cx.SLAB_COLS * 4
    return cx.slab[:, c:c + (n_bf16 + 1) // 2].bitcast(BF16)[:, :n_bf16]


def setup(p, dram):
    cx = Ctx()
    cx.dram = dram
    cx.ident_f = p.sbuf("ident_f", [128, 128], F32)
    cx.ident_b = p.sbuf("ident_b", [128, 128], BF16)
    cx.rot_b = p.sbuf("rot_b", [128, 128], BF16)
    cx.hmask_b = p.sbuf("hmask_b", [128, 128], BF16)
    cx.triu_b = p.sbuf("triu_b", [128, 128], BF16)
    cx.tril_b = p.sbuf("tril_b", [128, 128], BF16)
    cx.negb_b = p.sbuf("negb_b", [128, 128], BF16)
    cx.ones_b = p.sbuf("ones_b", [128, 1], BF16)
    cx.small = p.sbuf("small", [128, 3072], F32)
    cx.SLAB_COLS = 48592
    cx.slab = p.sbuf("slab", [128, cx.SLAB_COLS], F32)
    cx.PS = p.psum("PS", [128, 3072], F32)
    cx.PSB = p.psum("PSB", [128, 2048], BF16)
    cx.s_ld = p.sem("ld")
    cx.s_st = p.sem("st")
    cx.s_ld2 = p.sem("ld2")
    cx.s_pe = p.sem("pe")
    cx.s_dv = p.sem("dv")
    cx.s_ac = p.sem("ac")
    cx.s_gp = p.sem("gp")
    cx.consts = {}
    cx.n_consts = 0
    p.allsems = [cx.s_ld, cx.s_st, cx.s_ld2, cx.s_pe, cx.s_dv, cx.s_ac, cx.s_gp]
    return cx


SM_B1 = {0: 0, 1: 44}
SM_B2 = {0: 88, 1: 96}
SM_LG = {0: 104, 1: 112}
SM_LB = {0: 120, 1: 128}
SM_QS = {0: 136, 1: 137}
SM_KS = {0: 138, 1: 139}
SM_NW = 140
SM_CB = 148
SM_DTB = 164
SM_DVEC = 180
SM_ANEG = 196
SM_DIMIDX = 452
SM_INVF = 453
SM_LNB = 454
SM_L10K = 455
SM_BASE = 456
SM_MX = 460
SM_NMX = 461
SM_RS = 462
SM_RC = 463
SM_STATE = 464
SM_CONVH = 736
SM_AHALO = 800
SM_ONESR = 840
SM_STAT = 1024
SM_STAT2 = 1536
SM_TMP = 2048


SLAB_BYTES = 48592 * 4

A_F = 0
FF_ABF = 65568
FF_QH = 73856
FF_KF = 90240
FF_VT = 110720
FF_SQB = 131200
FF_TBL = 135296
FF_TSC = 143488
FF_RRS = FF_TSC
FF_Q2 = FF_TSC + 2048
FF_UB = FF_TSC + 4096
FF_WSCR = FF_TSC + 6144
FF_WQ = 151680
FF_WO = 176256
FF_OB = 131200
FF_XCH = FF_QH
FF_XB = FF_VT
FF_PSC = 184448
FF_PT = 187008

FF_SHLN = 65568
FF_W1 = 98336
FF_W2 = 143392
FF_HP = 165920
FF_GLB = 177184
FF_ABF2 = 98336
FF_TMP45 = 106624
FF_SQ45 = 123008

MB_LNM = 65568
MB_SQ = 73760
MB_XCH = 77856
MB_XB = 94240
MB_WIN = 102432
MB_XZS = 167968
MB_LNX = 176160
MB_XIN = 65568
MB_XT = 98560
MB_XTB = 131328
MB_WXP = 147712
M3_DA = 65568
M3_BH = 81952
M3_DT = 98336
M3_UM = 99360
M3_XTG = 100384
M3_ZG = 101408
M3_SZG = 102432
M3_YG = M3_UM
M3_YSB = 103456
M3_WDT = 111648
M3_DTRB = 115744
M3_WOUT = 119840
M3_BBC = 152608
M3_CBC = 168992
MB_XP = 185376

assert FF_PT + 1280 <= SLAB_BYTES
assert FF_GLB + 1024 <= SLAB_BYTES
assert MB_XZS + 8192 <= SLAB_BYTES
assert MB_XP + 8192 <= SLAB_BYTES



def _allsems(cx):
    return (cx.s_ld, cx.s_st, cx.s_ld2, cx.s_pe, cx.s_dv, cx.s_ac, cx.s_gp)


def barrier_wait(p, cx):
    return p.wv(*_allsems(cx))


def dve(p, cx, fn, wait=()):
    p.emit("vector", fn, wait=wait, inc=(cx.s_dv, 1))


def vnop(p, cx):
    p.emit("vector", lambda e: e.nop())


def act(p, cx, fn, wait=()):
    p.emit("scalar", fn, wait=wait, inc=(cx.s_ac, 1))


def pe(p, cx, fn, wait=()):
    p.emit("tensor", fn, wait=wait, inc=(cx.s_pe, 1))


def ldma(p, cx, out, in_, wait=()):
    p.emit("sync", lambda e, o=out, i=in_: e.dma_start(out=o, in_=i), wait=wait, inc=(cx.s_ld, 16))


def gdma(p, cx, out, in_, wait=()):
    p.emit("gpsimd", lambda e, o=out, i=in_: e.dma_start(out=o, in_=i), wait=wait, inc=(cx.s_st, 16))


SM_CONST = 968
SM_CW = 2100


def ccol(p, cx, val):
    val = float(val)
    if val in cx.consts:
        return cx.consts[val]
    idx = SM_CONST + cx.n_consts
    cx.n_consts += 1
    assert idx < 3072
    ap = cx.small[:, idx:idx + 1]
    p.emit("vector", lambda e, a=ap, v=val: e.memset(a, v), inc=(cx.s_dv, 1))
    cx.consts[val] = ap
    return ap


def sm(cx, col, n=1, row0=False):
    t = cx.small[0:1, col:col + n] if row0 else cx.small[:, col:col + n]
    return t



def init_consts(p, cx):
    d = cx.dram
    for v in (1e-5, 1.0, -0.5, -1.0, 0.5):
        ccol(p, cx, v)
    B = lambda: barrier_wait(p, cx)
    ldma(p, cx, cx.ident_f[:, :], d["ident"][:, :])
    gdma(p, cx, cx.ident_b[:, :], d["ident"][:, :])
    gdma(p, cx, cx.rot_b[:, :], d["rotm"][:, :])
    gdma(p, cx, cx.hmask_b[:, :], d["headmask"][:, :])
    for l in range(2):
        ldma(p, cx, cx.small[:, SM_B1[l]:SM_B1[l] + 44], d[f"b1{l}"][:, :])
        ldma(p, cx, cx.small[:, SM_B2[l]:SM_B2[l] + 8], d[f"b2{l}"][:, :])
        ldma(p, cx, cx.small[:, SM_LG[l]:SM_LG[l] + 8], d[f"lg{l}"][:, :])
        ldma(p, cx, cx.small[:, SM_LB[l]:SM_LB[l] + 8], d[f"lb{l}"][:, :])
        ldma(p, cx, cx.small[:, SM_QS[l]:SM_QS[l] + 1], d[f"qs{l}"][:, :])
        ldma(p, cx, cx.small[:, SM_KS[l]:SM_KS[l] + 1], d[f"ks{l}"][:, :])
    ldma(p, cx, cx.small[:, SM_NW:SM_NW + 8], d["nw"][:, :])
    ldma(p, cx, cx.small[:, SM_CB:SM_CB + 16], d["cb"][:, :])
    ldma(p, cx, cx.small[:, SM_DTB:SM_DTB + 16], d["dtb"][:, :])
    ldma(p, cx, cx.small[:, SM_DVEC:SM_DVEC + 16], d["Dvec"][:, :])
    ldma(p, cx, cx.small[:, SM_DIMIDX:SM_DIMIDX + 1], d["dimidx"][:, :])
    ldma(p, cx, cx.small[:, SM_TMP:SM_TMP + 256], d["AlogT"][:, :])
    ldma(p, cx, cx.small[0:1, SM_TMP + 384:SM_TMP + 385], d["ten_thousand"][:, :])
    act(p, cx, lambda e: e.activation(out=sm(cx, SM_TMP, 256), in_=sm(cx, SM_TMP, 256),
                                      func=ACTF.Exp), wait=B())
    dve(p, cx, lambda e: e.tensor_scalar_mul(sm(cx, SM_ANEG, 256), sm(cx, SM_TMP, 256), -1.0),
        wait=B())
    dve(p, cx, lambda e: e.memset(cx.ones_b[:, :], 1.0), wait=B())
    dve(p, cx, lambda e: e.memset(cx.negb_b[:, :], NEG_BIG))
    dve(p, cx, lambda e: e.memset(cx.small[0:1, SM_ONESR:SM_ONESR + 128], 1.0))
    p.emit("gpsimd", lambda e: e.iota(sm(cx, SM_TMP, 128), [[1, 128]], channel_multiplier=0,
                                      allow_small_or_imprecise_dtypes=True), wait=B(), inc=(cx.s_gp, 1))
    p.emit("gpsimd", lambda e: e.iota(sm(cx, SM_TMP + 128, 128), [[0, 128]], channel_multiplier=1,
                                      allow_small_or_imprecise_dtypes=True), inc=(cx.s_gp, 1))
    dve(p, cx, lambda e: e.tensor_tensor(sm(cx, SM_TMP + 256, 128), sm(cx, SM_TMP, 128),
                                         sm(cx, SM_TMP + 128, 128), ALU.is_lt), wait=B())
    dve(p, cx, lambda e: e.tensor_scalar_mul(cx.triu_b[:, :], sm(cx, SM_TMP + 256, 128), NEG_BIG),
        wait=B())
    dve(p, cx, lambda e: e.tensor_tensor(sm(cx, SM_TMP + 256, 128), sm(cx, SM_TMP, 128),
                                         sm(cx, SM_TMP + 128, 128), ALU.is_gt), wait=B())
    dve(p, cx, lambda e: e.tensor_scalar_mul(cx.tril_b[:, :], sm(cx, SM_TMP + 256, 128), NEG_BIG),
        wait=B())
    act(p, cx, lambda e: e.activation(out=cx.small[0:1, SM_TMP + 385:SM_TMP + 386],
                                      in_=cx.small[0:1, SM_TMP + 384:SM_TMP + 385], func=ACTF.Ln),
        wait=B())
    pe(p, cx, lambda e: e.matmul(cx.PS[:, 0:1], cx.small[0:1, SM_ONESR:SM_ONESR + 128],
                                 cx.small[0:1, SM_TMP + 385:SM_TMP + 386], start=True, stop=True),
       wait=B())
    dve(p, cx, lambda e: e.tensor_copy(sm(cx, SM_L10K), cx.PS[:, 0:1]), wait=B())
    vnop(p, cx)
    dve(p, cx, lambda e: e.tensor_scalar_mul(sm(cx, SM_TMP), sm(cx, SM_DIMIDX), -1.0 / 64.0),
        wait=B())
    vnop(p, cx)
    dve(p, cx, lambda e: e.tensor_tensor(sm(cx, SM_TMP + 1, 1), sm(cx, SM_TMP), sm(cx, SM_L10K),
                                         ALU.mult))
    vnop(p, cx)
    act(p, cx, lambda e: e.activation(out=sm(cx, SM_INVF), in_=sm(cx, SM_TMP + 1, 1),
                                      func=ACTF.Exp), wait=B())
    dve(p, cx, lambda e: e.tensor_scalar(sm(cx, SM_BASE), sm(cx, SM_DIMIDX), 25.6, 1.0 / 89.6,
                                         ALU.add, ALU.mult), wait=B())
    act(p, cx, lambda e: e.activation(out=sm(cx, SM_LNB), in_=sm(cx, SM_BASE), func=ACTF.Ln),
        wait=B())
    p.emit("sync", lambda e: e.nop(), wait=B())



def ln_stats(p, cx, xb_tiles, sq_slot_off, shifted=False):
    for g in range(8):
        pe(p, cx, lambda e, g=g: e.matmul(cx.PS[0:1, 2048:2048 + CH], cx.ones_b[:, 0:1],
                                          xb_tiles[g], start=(g == 0), stop=(g == 7)),
           wait=p.wv(cx.s_st, cx.s_dv, cx.s_ld))
        slot = _regb(cx, sq_slot_off + (g % 4) * 1024, CH)
        act(p, cx, lambda e, s=slot, g=g: e.activation(out=s, in_=xb_tiles[g], func=ACTF.Square),
            wait=p.wv(cx.s_pe, cx.s_st, cx.s_ld))
        pe(p, cx, lambda e, s=slot, g=g: e.matmul(cx.PS[0:1, 2560:2560 + CH], cx.ones_b[:, 0:1],
                                                  s, start=(g == 0), stop=(g == 7)),
           wait=p.wv(cx.s_ac))


def ln_rows(p, cx, mamba=False):
    S = 1024
    m_row = cx.small[0:1, S:S + CH]
    r_row = cx.small[0:1, S + 512:S + 512 + CH]
    t1 = cx.small[0:1, S + 1024:S + 1024 + CH]
    w = p.wv(cx.s_pe)
    dve(p, cx, lambda e: e.tensor_scalar_mul(m_row, cx.PS[0:1, 2048:2048 + CH], 1.0 / 1024.0), wait=w)
    dve(p, cx, lambda e: e.tensor_scalar_mul(t1, cx.PS[0:1, 2560:2560 + CH], 1.0 / 1024.0))
    dve(p, cx, lambda e: e.tensor_tensor(r_row, m_row, m_row, ALU.mult))
    dve(p, cx, lambda e: e.tensor_tensor(t1, t1, r_row, ALU.subtract))
    w = p.wv(cx.s_dv)
    act(p, cx, lambda e: e.activation(out=t1, in_=t1, func=ACTF.Ln,
                                      bias=ccol(p, cx, 1e-5)[0:1, :]), wait=w)
    if not mamba:
        act(p, cx, lambda e: e.activation(out=r_row, in_=t1, func=ACTF.Exp,
                                          scale=ccol(p, cx, -0.5)[0:1, :]))
    else:
        r1_row = cx.small[0:1, S + 1536:S + 1536 + CH]
        act(p, cx, lambda e: e.activation(out=r1_row, in_=t1, func=ACTF.Exp,
                                          scale=ccol(p, cx, -0.5)[0:1, :]))
        act(p, cx, lambda e: e.activation(out=r_row, in_=t1, func=ACTF.Exp,
                                          scale=ccol(p, cx, -1.0)[0:1, :]))
        dve(p, cx, lambda e: e.tensor_scalar(r_row, r_row, -1e-5, 1.0, ALU.mult, ALU.add),
            wait=p.wv(cx.s_ac))
        act(p, cx, lambda e: e.activation(out=r_row, in_=r_row, func=ACTF.Ln,
                                          bias=ccol(p, cx, 1e-5)[0:1, :]), wait=p.wv(cx.s_dv))
        dve(p, cx, lambda e: e.tensor_tensor(t1, t1, r_row, ALU.add), wait=p.wv(cx.s_ac))
        act(p, cx, lambda e: e.activation(out=r_row, in_=t1, func=ACTF.Exp,
                                          scale=ccol(p, cx, -0.5)[0:1, :]), wait=p.wv(cx.s_dv))
    w = p.wv(cx.s_ac, cx.s_dv)
    pe(p, cx, lambda e: e.matmul(cx.PS[:, 1024:1024 + CH], cx.small[0:1, SM_ONESR:SM_ONESR + 128],
                                 m_row, start=True, stop=True), wait=w)
    pe(p, cx, lambda e: e.matmul(cx.PS[:, 1536:1536 + CH], cx.small[0:1, SM_ONESR:SM_ONESR + 128],
                                 r_row, start=True, stop=True))
    if mamba:
        r1_row = cx.small[0:1, S + 1536:S + 1536 + CH]
        pe(p, cx, lambda e: e.matmul(cx.PS[:, 0:CH], cx.small[0:1, SM_ONESR:SM_ONESR + 128],
                                     r1_row, start=True, stop=True))


def stage_ln1(p, cx, h, mamba=False, dst="a"):
    d = cx.dram
    p.cur_stage = "ln1"
    for c in range(NCH):
        p.barrier()
        t0 = h * T_HALF + c * CH
        xch = [_reg(cx, (FF_XCH if not mamba else MB_XCH) + g * 2048, CH) for g in range(8)]
        xb = [_regb(cx, (FF_XB if not mamba else MB_XB) + g * 1024, CH) for g in range(8)]
        wld = p.wv(cx.s_dv, cx.s_ac, cx.s_pe, cx.s_st)
        for g in range(8):
            ldma(p, cx, xch[g], d["Xbuf"][g * 128:(g + 1) * 128, t0:t0 + CH], wait=wld)
            gdma(p, cx, xb[g], d["Xbuf"][g * 128:(g + 1) * 128, t0:t0 + CH], wait=wld)
        ln_stats(p, cx, xb, (FF_SQB if not mamba else MB_LNM))
        ln_rows(p, cx, mamba=mamba)
        wmr = p.wv(cx.s_pe)
        for g in range(8):
            dve(p, cx, lambda e, g=g: e.scalar_tensor_tensor(xch[g], cx.PS[:, 1024:1024 + CH],
                                                             -1.0, xch[g], ALU.mult, ALU.add),
                wait=wmr)
            if not mamba:
                dst_ap = cx.slab[:, (A_F // 4) + g * 2049 + c * CH + 1: (A_F // 4) + g * 2049 + c * CH + 1 + CH]
                dve(p, cx, lambda e, g=g, o=dst_ap: e.tensor_tensor(o, xch[g], cx.PS[:, 1536:1536 + CH],
                                                                    ALU.mult))
            else:
                dve(p, cx, lambda e, g=g: e.tensor_tensor(xch[g], xch[g], cx.PS[:, 1536:1536 + CH],
                                                          ALU.mult))
                lnm = _regb(cx, MB_LNM + g * 1024, CH)
                dve(p, cx, lambda e, g=g, o=lnm: e.tensor_scalar_mul(o, xch[g],
                                                                     cx.small[:, SM_NW + g:SM_NW + g + 1]))



def tables_chunk(p, cx, t0):
    FR = _reg(cx, FF_TSC, CH)
    KF_ = _reg(cx, FF_TSC + 2048, CH)
    S1 = _reg(cx, FF_TSC + 4096, CH)
    S2 = _reg(cx, FF_TSC + 6144, CH)
    cs = _reg(cx, FF_TBL, CH)
    sn = _reg(cx, FF_TBL + 2048, CH)
    scq = _reg(cx, FF_TBL + 4096, CH)
    sck = _reg(cx, FF_TBL + 6144, CH)
    w = p.wv(cx.s_dv, cx.s_ac, cx.s_pe)
    p.emit("gpsimd", lambda e: e.iota(FR, [[1, CH]], base=t0, channel_multiplier=0,
                                      allow_small_or_imprecise_dtypes=True), wait=w, inc=(cx.s_gp, 1))
    wg = p.wv(cx.s_gp)
    dve(p, cx, lambda e: e.tensor_scalar_mul(FR, FR, sm(cx, SM_INVF)), wait=wg)
    ldma(p, cx, KF_, cx.dram["rotk1"][:, t0:t0 + CH], wait=p.wv(cx.s_dv, cx.s_ac))
    dve(p, cx, lambda e: e.scalar_tensor_tensor(S1, KF_, -float(TWO_PI_HI), FR, ALU.mult, ALU.add),
        wait=p.wv(cx.s_ld))
    dve(p, cx, lambda e: e.scalar_tensor_tensor(S1, KF_, -float(TWO_PI_LO), S1, ALU.mult, ALU.add))
    act(p, cx, lambda e: e.activation(out=sn, in_=S1, func=ACTF.Sin), wait=p.wv(cx.s_dv))
    ldma(p, cx, KF_, cx.dram["rotk2"][:, t0:t0 + CH], wait=p.wv(cx.s_dv, cx.s_ac))
    dve(p, cx, lambda e: e.scalar_tensor_tensor(S2, KF_, -float(TWO_PI_HI), FR, ALU.mult, ALU.add),
        wait=p.wv(cx.s_ld))
    dve(p, cx, lambda e: e.scalar_tensor_tensor(S2, KF_, -float(TWO_PI_LO), S2, ALU.mult, ALU.add))
    dve(p, cx, lambda e: e.tensor_scalar_add(S2, S2, float(PI_HALF)))
    act(p, cx, lambda e: e.activation(out=cs, in_=S2, func=ACTF.Sin), wait=p.wv(cx.s_dv))
    p.emit("gpsimd", lambda e: e.iota(S1, [[1, CH]], base=t0 - 2048, channel_multiplier=0,
                                      allow_small_or_imprecise_dtypes=True),
           wait=p.wv(cx.s_dv), inc=(cx.s_gp, 1))
    dve(p, cx, lambda e: e.tensor_scalar_mul(S1, S1, 1.0 / 256.0), wait=p.wv(cx.s_gp))
    dve(p, cx, lambda e: e.tensor_scalar_mul(S1, S1, sm(cx, SM_LNB)))
    act(p, cx, lambda e: e.activation(out=scq, in_=S1, func=ACTF.Exp), wait=p.wv(cx.s_dv))
    act(p, cx, lambda e: e.activation(out=sck, in_=S1, func=ACTF.Exp, scale=ccol(p, cx, -1.0)))
    dve(p, cx, lambda e: e.tensor_scalar_mul(scq, scq, 8.0), wait=p.wv(cx.s_ac))
    w = p.wv(cx.s_ac)
    csq = cs
    snq = sn
    csk = _reg(cx, FF_TSC, CH)
    snk = _reg(cx, FF_TSC + 2048, CH)
    dve(p, cx, lambda e: e.tensor_tensor(csk, cs, sck, ALU.mult), wait=w)
    dve(p, cx, lambda e: e.tensor_tensor(snk, sn, sck, ALU.mult))
    dve(p, cx, lambda e: e.tensor_tensor(csq, cs, scq, ALU.mult))
    dve(p, cx, lambda e: e.tensor_tensor(snq, sn, scq, ALU.mult))
    dve(p, cx, lambda e: e.tensor_copy(scq, csk))
    dve(p, cx, lambda e: e.tensor_copy(sck, snk))
    return dict(csq=csq, snq=snq, csk=scq, snk=sck)




def _af(cx, g, c0, n):
    return cx.slab[:, (A_F // 4) + g * 2049 + c0:(A_F // 4) + g * 2049 + c0 + n]


def stage_ffn(p, cx, h, only=None):
    d = cx.dram
    p.cur_stage = "ffn"
    p.barrier()
    wdv = p.wv(cx.s_dv, cx.s_ac, cx.s_pe, cx.s_st, cx.s_ld)
    if h == 0:
        for g in range(8):
            dve(p, cx, lambda e, g=g: e.memset(_af(cx, g, 0, 1), 0.0), wait=wdv if g == 0 else ())
        for g in range(8):
            dve(p, cx, lambda e, o=cx.small[:, SM_AHALO + g:SM_AHALO + g + 1],
                i=_af(cx, g, T_HALF, 1): e.tensor_copy(o, i))
    else:
        for g in range(8):
            dve(p, cx, lambda e, o=_af(cx, g, 0, 1),
                i=cx.small[:, SM_AHALO + g:SM_AHALO + g + 1]: e.tensor_copy(o, i),
                wait=wdv if g == 0 else ())
    for l in range(2):
        stage_attn_layer(p, cx, h, l)
        if only == "attn0":
            return
        stage_mlp_layer(p, cx, h, l)
        if only == "l0":
            return


def stage_attn_layer(p, cx, h, l, p1_only=False):
    d = cx.dram
    p.cur_stage = "attn"
    KB = lambda off, n: _regb(cx, off, n)
    wub = p.wv(cx.s_pe, cx.s_dv, cx.s_ac, cx.s_st, cx.s_ld)
    for kk in range(8):
        gdma(p, cx, KB(FF_WQ + kk * 3072, 1536),
             d[f"wqkv{l}"][kk * 128:(kk + 1) * 128, :], wait=wub)
    for kk in range(4):
        gdma(p, cx, KB(FF_WO + kk * 2048, 1024),
             d[f"wo{l}"][kk * 128:(kk + 1) * 128, :], wait=wub)
    wv0 = p.wv(cx.s_pe, cx.s_ac, cx.s_dv, cx.s_st)
    if h == 0:
        dve(p, cx, lambda e: e.memset(KB(FF_KF, 512), 0.0), wait=wv0)
        dve(p, cx, lambda e: e.memset(KB(FF_KF + 5120, 512), 0.0))
        dve(p, cx, lambda e: e.memset(KB(FF_KF + 10240, 512), 0.0))
        dve(p, cx, lambda e: e.memset(KB(FF_KF + 15360, 512), 0.0))
        for t4 in range(4):
            dve(p, cx, lambda e, t4=t4: e.memset(KB(FF_VT + t4 * 1024, 512), 0.0))
    else:
        for g in range(4):
            ldma(p, cx, KB(FF_KF + g * 5120, 512),
                 d["kcar"][l, g * 128:(g + 1) * 128, :], wait=wv0)
        for t4 in range(4):
            ldma(p, cx, KB(FF_VT + t4 * 1024, 512),
                 d["vcar"][l, t4 * 128:(t4 + 1) * 128, :], wait=wv0)
    for c in range(NCH):
        p.barrier()
        t0g = h * T_HALF + c * CH
        tabs = tables_chunk(p, cx, t0g)
        wab = p.wv(cx.s_pe, cx.s_ac)
        for g in range(8):
            dve(p, cx, lambda e, o=KB(FF_ABF + g * 1032, 513), a=_af(cx, g, c * CH, 513):
                e.tensor_copy(o, a), wait=wab)
        wqb = {0: p.wv(cx.s_dv, cx.s_ac, cx.s_st), 1: p.wv(cx.s_dv, cx.s_ac, cx.s_st)}
        for m in range(12):
            bank = cx.PS[:, (m % 2) * 512:(m % 2) * 512 + 512]
            wmm = wqb[m % 2]
            for kk in range(8):
                rhs = (KB(FF_ABF + kk * 1032, 513)[:, 1:513] if kk < 4
                       else KB(FF_ABF + kk * 1032, 513)[:, 0:512])
                lhsT = KB(FF_WQ + kk * 3072 + m * 256, 128)
                pe(p, cx, lambda e, o=bank, lt=lhsT, r=rhs, kk=kk:
                   e.matmul(o, lt, r, start=(kk == 0), stop=(kk == 7)), wait=wmm if kk == 0 else ())
            if m < 8:
                isq = m < 4
                tbl_cs = tabs["csq" if isq else "csk"]
                tbl_sn = tabs["snq" if isq else "snk"]
                scol = cx.small[:, SM_QS[l]:SM_QS[l] + 1] if isq else cx.small[:, SM_KS[l]:SM_KS[l] + 1]
                slot = KB(FF_SQB + (m % 4) * 1024, 512)
                act(p, cx, lambda e, s=slot, b=bank: e.activation(out=s, in_=b, func=ACTF.Square),
                    wait=p.wv(cx.s_pe))
                pe(p, cx, lambda e, s=slot: e.matmul(cx.PS[:, 1024:1536], cx.hmask_b[:, :], s,
                                                     start=True, stop=True), wait=p.wv(cx.s_ac, cx.s_dv))
                RRS = _reg(cx, FF_RRS, 512)
                WSC = _reg(cx, FF_WSCR, 512)
                Q2 = _reg(cx, FF_Q2, 512)
                UB = KB(FF_UB, 512)
                dve(p, cx, lambda e: e.tensor_scalar_max(RRS, cx.PS[:, 1024:1536], 1e-24),
                    wait=p.wv(cx.s_pe))
                act(p, cx, lambda e: e.activation(out=WSC, in_=RRS, func=ACTF.Ln), wait=p.wv(cx.s_dv))
                act(p, cx, lambda e: e.activation(out=RRS, in_=WSC, func=ACTF.Exp,
                                                  scale=ccol(p, cx, -0.5)))
                dve(p, cx, lambda e, b=bank, sc=scol: e.scalar_tensor_tensor(Q2, b, sc, RRS,
                                                                             ALU.mult, ALU.mult),
                    wait=p.wv(cx.s_ac))
                dve(p, cx, lambda e, t=tbl_sn: e.tensor_tensor(UB, Q2, t, ALU.mult),
                    wait=p.wv(cx.s_pe))
                pe(p, cx, lambda e: e.matmul(cx.PS[:, 1536:2048], cx.rot_b[:, :], UB,
                                             start=True, stop=True), wait=p.wv(cx.s_dv))
                dve(p, cx, lambda e, t=tbl_cs: e.tensor_tensor(WSC, Q2, t, ALU.mult))
                if isq:
                    dst = KB(FF_QH + m * 4096, 2048)[:, c * CH:(c + 1) * CH]
                else:
                    dst = KB(FF_KF + (m - 4) * 5120, 2560)[:, 512 + c * CH:512 + (c + 1) * CH]
                dve(p, cx, lambda e, o=dst: e.tensor_tensor(o, WSC, cx.PS[:, 1536:2048], ALU.add),
                    wait=p.wv(cx.s_pe, cx.s_st))
                wqb[m % 2] = p.wv(cx.s_dv, cx.s_ac, cx.s_st)
            else:
                mv = m - 8
                slot = KB(FF_SQB + (mv % 4) * 1024, 512)
                dve(p, cx, lambda e, s=slot, b=bank: e.tensor_copy(s, b), wait=p.wv(cx.s_pe))
                for b4 in range(4):
                    pe(p, cx, lambda e, s=slot, b4=b4,
                       o=cx.PSB[:, (m % 2) * 512 + b4 * 128:(m % 2) * 512 + (b4 + 1) * 128]:
                       e.transpose(o, s[:, b4 * 128:(b4 + 1) * 128], cx.ident_b[:, :]),
                       wait=p.wv(cx.s_dv))
                for b4 in range(4):
                    dst = KB(FF_VT + (4 + c * 4 + b4) * 1024, 512)[:, mv * 128:(mv + 1) * 128]
                    dve(p, cx, lambda e, o=dst,
                        i=cx.PSB[:, (m % 2) * 512 + b4 * 128:(m % 2) * 512 + (b4 + 1) * 128]:
                        e.tensor_copy(o, i), wait=p.wv(cx.s_pe, cx.s_st))
                wqb[m % 2] = p.wv(cx.s_dv, cx.s_ac, cx.s_st)
    if p1_only:
        return
    if h == 0:
        wkv = p.wv(cx.s_dv, cx.s_ac, cx.s_pe)
        for g in range(4):
            gdma(p, cx, d["kcar"][l, g * 128:(g + 1) * 128, :],
                 KB(FF_KF + g * 5120, 2560)[:, T_HALF:T_HALF + 512], wait=wkv)
        for t4 in range(4):
            gdma(p, cx, d["vcar"][l, t4 * 128:(t4 + 1) * 128, :],
                 KB(FF_VT + (16 + t4) * 1024, 512), wait=wkv)
    p.cur_stage = "p2"
    for c in range(NCH):
        p.barrier()
        for s4 in range(4):
            qt = c * 4 + s4
            for hp in range(4):
                qh_t = KB(FF_QH + hp * 4096, 2048)
                kf_t = KB(FF_KF + hp * 5120, 2560)
                wS = {}
                for hh in range(2):
                    sb = hh * 1024
                    lhsT = qh_t[hh * 64:(hh + 1) * 64, qt * 128:qt * 128 + 128]
                    wmm = p.wv(cx.s_ac, cx.s_dv)
                    pe(p, cx, lambda e, lt=lhsT, r=kf_t[hh * 64:(hh + 1) * 64, qt * 128:qt * 128 + 512], sb=sb:
                       e.matmul(cx.PS[:, sb:sb + 512], lt, r, start=True, stop=False), wait=wmm)
                    if h == 0 and qt < 4:
                        for bb in range(4 - qt):
                            pe(p, cx, lambda e, sb=sb, bb=bb:
                               e.matmul(cx.PS[:, sb + bb * 128:sb + (bb + 1) * 128], cx.ident_b[:, :],
                                        cx.negb_b[:, :], start=False, stop=False))
                    pe(p, cx, lambda e, sb=sb: e.matmul(cx.PS[:, sb:sb + 128], cx.ident_b[:, :],
                                                        cx.triu_b[:, :], start=False, stop=True))
                    pe(p, cx, lambda e, lt=lhsT, r=kf_t[hh * 64:(hh + 1) * 64, qt * 128 + 512:qt * 128 + 640], sb=sb:
                       e.matmul(cx.PS[:, sb + 512:sb + 640], lt, r, start=True, stop=False))
                    pe(p, cx, lambda e, sb=sb: e.matmul(cx.PS[:, sb + 512:sb + 640], cx.ident_b[:, :],
                                                        cx.tril_b[:, :], start=False, stop=True))
                    wS[hh] = p.wv(cx.s_pe)
                for hh in range(2):
                    sb = hh * 1024
                    sl = (qt * 8 + hp * 2 + hh) % 4
                    NMX = cx.small[:, 2052 + sl:2053 + sl]
                    RS = cx.small[:, 2056 + sl:2057 + sl]
                    RC = cx.small[:, 2060 + sl:2061 + sl]
                    dve(p, cx, lambda e, sb=sb, o=NMX: e.tensor_reduce(o, cx.PS[:, sb:sb + 640],
                                                                       AX.X, ALU.max, negate=True),
                        wait=wS[hh])
                    PSC = KB(FF_PSC + hh * 1280, 640)
                    act(p, cx, lambda e, sb=sb, o=PSC, b=NMX, a=RS:
                        e.activation(out=o, in_=cx.PS[:, sb:sb + 640], func=ACTF.Exp, bias=b,
                                     accum_out=a), wait=p.wv(cx.s_dv) + wS[hh])
                    dve(p, cx, lambda e, o=RC, i=RS: e.reciprocal(o, i), wait=p.wv(cx.s_ac))
                    vnop(p, cx)
                    dve(p, cx, lambda e, o=PSC, i=RC: e.tensor_scalar_mul(o, o, i))
                    for b5 in range(5):
                        pe(p, cx, lambda e, i=PSC, b5=b5, hh=hh:
                           e.transpose(cx.PSB[:, hh * 1024 + b5 * 128:hh * 1024 + (b5 + 1) * 128],
                                       i[:, b5 * 128:(b5 + 1) * 128], cx.ident_b[:, :]),
                           wait=p.wv(cx.s_dv))
                    PT = KB(FF_PT + hh * 1280, 640)
                    dve(p, cx, lambda e, o=PT, hh=hh: e.tensor_copy(o, cx.PSB[:, hh * 1024:hh * 1024 + 640]),
                        wait=p.wv(cx.s_pe))
                    for b5 in range(5):
                        vt_t = KB(FF_VT + (qt + b5) * 1024, 512)
                        lhsT_v = vt_t[:, hp * 128 + hh * 64:hp * 128 + hh * 64 + 64]
                        pe(p, cx, lambda e, lt=lhsT_v, r=PT[:, b5 * 128:(b5 + 1) * 128], hh=hh, b5=b5:
                           e.matmul(cx.PS[hh * 64:(hh + 1) * 64, 2048:2176], lt, r,
                                    start=(b5 == 0), stop=(b5 == 4)),
                           wait=p.wv(cx.s_dv) if b5 == 0 else ())
                ob = KB(FF_OB + hp * 4096, 2048)
                dve(p, cx, lambda e, o=ob[:, qt * 128:(qt + 1) * 128]: e.tensor_copy(o, cx.PS[:, 2048:2176]),
                    wait=p.wv(cx.s_pe))
    p.cur_stage = "wo"
    p.barrier()
    for c in range(NCH):
        for m in range(8):
            bank = cx.PS[:, (m % 2) * 512:(m % 2) * 512 + 512]
            wmm = p.wv(cx.s_dv, cx.s_ac)
            for kk in range(4):
                lhsT = KB(FF_WO + kk * 2048 + m * 256, 128)
                rhs = KB(FF_OB + kk * 4096, 2048)[:, c * CH:(c + 1) * CH]
                pe(p, cx, lambda e, o=bank, lt=lhsT, r=rhs, kk=kk:
                   e.matmul(o, lt, r, start=(kk == 0), stop=(kk == 3)), wait=wmm if kk == 0 else ())
            asl = _af(cx, m, c * CH + 1, CH)
            dve(p, cx, lambda e, o=asl, b=bank: e.tensor_tensor(o, b, o, ALU.add), wait=p.wv(cx.s_pe))
    if h == 0:
        for g in range(8):
            s0 = SM_AHALO + (2 * l + 1) * 8 + g
            dve(p, cx, lambda e, o=cx.small[:, s0:s0 + 1], i=_af(cx, g, T_HALF, 1):
                e.tensor_copy(o, i), wait=p.wv(cx.s_dv) if g == 0 else ())


def stage_mlp_layer(p, cx, h, l):
    d = cx.dram
    p.cur_stage = "mlp"
    KB = lambda off, n: _regb(cx, off, n)
    if h == 1:
        for g in range(8):
            s0 = SM_AHALO + (2 * l + 1) * 8 + g
            dve(p, cx, lambda e, o=_af(cx, g, 0, 1), i=cx.small[:, s0:s0 + 1]:
                e.tensor_copy(o, i), wait=p.wv(cx.s_dv, cx.s_pe) if g == 0 else ())
    p.barrier()
    for cm in range(NCH):
        wab = p.wv(cx.s_pe, cx.s_ac, cx.s_st)
        ab2 = [KB(FF_ABF2 + g * 1032, 513) for g in range(8)]
        for g in range(8):
            dve(p, cx, lambda e, o=ab2[g], a=_af(cx, g, cm * CH, 513):
                e.tensor_copy(o, a), wait=wab)
        sh_ap = [ab2[g][:, 1:513] if g < 4 else ab2[g][:, 0:512] for g in range(8)]
        ln_stats(p, cx, sh_ap, FF_SQ45)
        ln_rows(p, cx)
        wmr = p.wv(cx.s_pe)
        for g in range(8):
            tmp = _reg(cx, FF_TMP45 + g * 2048, 512)
            src = (_af(cx, g, cm * CH + 1, CH) if g < 4 else _af(cx, g, cm * CH, CH))
            dve(p, cx, lambda e, t=tmp, s=src: e.scalar_tensor_tensor(t, cx.PS[:, 1024:1536],
                                                                      -1.0, s, ALU.mult, ALU.add),
                wait=wmr if g == 0 else ())
            dve(p, cx, lambda e, t=tmp: e.tensor_tensor(t, t, cx.PS[:, 1536:2048], ALU.mult))
            dst = KB(FF_SHLN + g * 4096, 2048)[:, cm * CH:(cm + 1) * CH]
            dve(p, cx, lambda e, o=dst, t=tmp, g=g: e.tensor_scalar(
                o, t, cx.small[:, SM_LG[l] + g:SM_LG[l] + g + 1],
                cx.small[:, SM_LB[l] + g:SM_LB[l] + g + 1], ALU.mult, ALU.add))
    for hp2 in range(2):
        wst = p.wv(cx.s_pe, cx.s_dv, cx.s_ac, cx.s_st, cx.s_ld)
        for kk in range(8):
            for side in range(2):
                gdma(p, cx, KB(FF_W1 + (kk * 22 + side * 11) * 256, 1408),
                     d[f"w1{l}"][kk * 128:(kk + 1) * 128,
                                 side * HIDP + hp2 * 1408:side * HIDP + hp2 * 1408 + 1408],
                     wait=wst)
        for kl in range(11):
            gdma(p, cx, KB(FF_W2 + kl * 2048, 1024),
                 d[f"w2{l}"][(hp2 * 11 + kl) * 128:(hp2 * 11 + kl) * 128 + 128, :], wait=wst)
        for cm in range(NCH):
            p.barrier()
            wbank = {0: p.wv(cx.s_dv, cx.s_ac, cx.s_st), 1: p.wv(cx.s_dv, cx.s_ac, cx.s_st)}
            for i in range(11):
                gi = hp2 * 11 + i
                ab_ = cx.PS[:, (i % 2) * 1024:(i % 2) * 1024 + 512]
                gb_ = cx.PS[:, (i % 2) * 1024 + 512:(i % 2) * 1024 + 1024]
                wmm = wbank[i % 2]
                for kk in range(8):
                    rhs = KB(FF_SHLN + kk * 4096, 2048)[:, cm * CH:(cm + 1) * CH]
                    pe(p, cx, lambda e, o=ab_, lt=KB(FF_W1 + (kk * 22 + i) * 256, 128), r=rhs, kk=kk:
                       e.matmul(o, lt, r, start=(kk == 0), stop=(kk == 7)), wait=wmm if kk == 0 else ())
                for kk in range(8):
                    rhs = KB(FF_SHLN + kk * 4096, 2048)[:, cm * CH:(cm + 1) * CH]
                    pe(p, cx, lambda e, o=gb_, lt=KB(FF_W1 + (kk * 22 + 11 + i) * 256, 128), r=rhs, kk=kk:
                       e.matmul(o, lt, r, start=(kk == 0), stop=(kk == 7)))
                GLB = KB(FF_GLB, 512)
                act(p, cx, lambda e, b=gb_, gi=gi: e.activation(
                    out=GLB, in_=b, func=ACTF.Gelu,
                    bias=cx.small[:, SM_B1[l] + 22 + gi:SM_B1[l] + 22 + gi + 1]),
                    wait=p.wv(cx.s_pe, cx.s_dv))
                hp_t = KB(FF_HP + i * 1024, 512)
                dve(p, cx, lambda e, o=hp_t, b=ab_, gi=gi: e.scalar_tensor_tensor(
                    o, b, cx.small[:, SM_B1[l] + gi:SM_B1[l] + gi + 1], GLB, ALU.add, ALU.mult),
                    wait=p.wv(cx.s_ac, cx.s_pe))
                wbank[i % 2] = p.wv(cx.s_dv, cx.s_ac, cx.s_st)
            for mp in range(2):
                for m4 in range(4):
                    m = mp * 4 + m4
                    bank = cx.PS[:, m4 * 512:(m4 + 1) * 512]
                    wmm = p.wv(cx.s_dv, cx.s_ac)
                    for kl in range(11):
                        pe(p, cx, lambda e, o=bank, lt=KB(FF_W2 + kl * 2048 + m * 256, 128),
                           r=KB(FF_HP + kl * 1024, 512), kl=kl:
                           e.matmul(o, lt, r, start=(kl == 0), stop=(kl == 10)),
                           wait=wmm if kl == 0 else ())
                for m4 in range(4):
                    m = mp * 4 + m4
                    bank = cx.PS[:, m4 * 512:(m4 + 1) * 512]
                    asl = _af(cx, m, cm * CH + 1, CH)
                    if hp2 == 0:
                        dve(p, cx, lambda e, o=asl, b=bank: e.tensor_tensor(o, b, o, ALU.add),
                            wait=p.wv(cx.s_pe))
                    else:
                        dve(p, cx, lambda e, o=asl, b=bank, m=m: e.scalar_tensor_tensor(
                            o, b, cx.small[:, SM_B2[l] + m:SM_B2[l] + m + 1], o, ALU.add, ALU.add),
                            wait=p.wv(cx.s_pe))
    if h == 0 and l == 0:
        for g in range(8):
            s0 = SM_AHALO + 16 + g
            dve(p, cx, lambda e, o=cx.small[:, s0:s0 + 1], i=_af(cx, g, T_HALF, 1):
                e.tensor_copy(o, i), wait=p.wv(cx.s_dv) if g == 0 else ())
    if h == 1 and l == 0:
        for g in range(8):
            s0 = SM_AHALO + 16 + g
            dve(p, cx, lambda e, o=_af(cx, g, 0, 1), i=cx.small[:, s0:s0 + 1]:
                e.tensor_copy(o, i), wait=p.wv(cx.s_dv, cx.s_pe) if g == 0 else ())


def stage_combine(p, cx, h):
    d = cx.dram
    p.cur_stage = "combine"
    p.barrier()
    for c in range(NCH):
        t0 = h * T_HALF + c * CH
        for g in range(8):
            xch = _reg(cx, FF_XCH + g * 2048, CH)
            ldma(p, cx, xch, d["Xbuf"][g * 128:(g + 1) * 128, t0:t0 + CH],
                 wait=p.wv(*p.allsems))
            dve(p, cx, lambda e, o=xch, a=_af(cx, g, c * CH + 1, CH): e.scalar_tensor_tensor(
                o, a, 0.5, o, ALU.mult, ALU.add), wait=p.wv(*p.allsems))
            gdma(p, cx, d["Xbuf"][g * 128:(g + 1) * 128, t0:t0 + CH], xch,
                 wait=p.wv(*p.allsems))
    p.emit("sync", lambda e: e.nop(), wait=p.wv(cx.s_st))



def stage_x_in(p, cx):
    d = cx.dram
    tm = FF_QH
    for c in range(T_TOT // 512):
        wld = p.wv(cx.s_pe)
        for s in range(4):
            ldma(p, cx, _reg(cx, tm + s * 4096, 1024),
                 d["hs"][c * 512 + s * 128: c * 512 + (s + 1) * 128, :], wait=wld)
        for g in range(8):
            bank = cx.PS[:, (g % 2) * 512:(g % 2) * 512 + 512]
            for s in range(4):
                src = _reg(cx, tm + s * 4096 + g * 512, 128)
                pe(p, cx, lambda e, o=bank[:, s * 128:(s + 1) * 128], i=src:
                   e.transpose(o, i, cx.ident_f[:, :]), wait=p.wv(cx.s_ld, cx.s_dv))
            dst = _reg(cx, FF_OB + g * 2048, 512)
            dve(p, cx, lambda e, o=dst, i=bank: e.tensor_copy(o, i),
                wait=p.wv(cx.s_pe, cx.s_st))
            gdma(p, cx, d["Xbuf"][g * 128:(g + 1) * 128, c * 512:(c + 1) * 512], dst,
                 wait=p.wv(cx.s_dv))
    p.emit("sync", lambda e: e.nop(), wait=p.wv(cx.s_st))


def stage_x_out(p, cx):
    d = cx.dram
    tm = FF_QH
    fm = FF_OB
    for c in range(T_TOT // 512):
        wg = p.wv(cx.s_st, cx.s_pe)
        for g in range(8):
            ldma(p, cx, _reg(cx, fm + g * 2048, 512),
                 d["Xbuf"][g * 128:(g + 1) * 128, c * 512:(c + 1) * 512], wait=wg)
        for s in range(4):
            bank = cx.PS[:, (s % 2) * 512:(s % 2) * 512 + 512]
            for g in range(8):
                src = _reg(cx, fm + g * 2048 + s * 512, 128)
                pe(p, cx, lambda e, o=bank[:, (g % 4) * 128:(g % 4 + 1) * 128], i=src:
                   e.transpose(o, i, cx.ident_f[:, :]), wait=p.wv(cx.s_ld, cx.s_dv))
                if g % 4 == 3:
                    dst = _reg(cx, tm + s * 4096 + (g - 3) * 512, 512)
                    dve(p, cx, lambda e, o=dst, i=bank: e.tensor_copy(o, i), wait=p.wv(cx.s_pe))
            gdma(p, cx, d["out"][c * 512 + s * 128: c * 512 + (s + 1) * 128, :],
                 _reg(cx, tm + s * 4096, 1024), wait=p.wv(cx.s_dv))
    p.emit("gpsimd", lambda e: e.nop(), wait=p.wv(cx.s_st))


def dump_af_to_xbuf(p, cx, h):
    d = cx.dram
    for g in range(8):
        src = cx.slab[:, (A_F // 4) + g * 2049 + 1:(A_F // 4) + g * 2049 + 1 + T_HALF]
        gdma(p, cx, d["Xbuf"][g * 128:(g + 1) * 128, h * T_HALF:(h + 1) * T_HALF], src,
             wait=p.wv(cx.s_dv, cx.s_pe, cx.s_ac))
    p.emit("sync", lambda e: e.nop(), wait=p.wv(cx.s_st))



def build_program(debug=None):
    nc = bass.Bass("TRN2", target_bir_lowering=False, debug=False, detect_race_conditions=False)
    stack = contextlib.ExitStack()
    with stack:
        p = Prog(nc, stack)
        dram = {}
        for nm, shp in _IN_SHAPES.items():
            dram[nm] = nc.dram_tensor(nm, shp, F32, kind="ExternalInput").ap()
        dram["out"] = nc.dram_tensor("out", [T_TOT, D_MODEL], F32, kind="ExternalOutput").ap()
        for nm, shp in [("Xbuf", [D_MODEL, T_TOT]), ("xin_s", [D_INNER, T_TOT]),
                        ("xt_s", [D_INNER, T_TOT]), ("z_s", [D_INNER, T_TOT])]:
            dram[nm] = nc.dram_tensor(nm, shp, F32).ap()
        dram["lnx_s"] = nc.dram_tensor("lnx_s", [D_MODEL, T_TOT], F32).ap()
        for nm in ("kcar", "vcar"):
            dram[nm] = nc.dram_tensor(nm, [2, 512, 512], BF16).ap()

        cx = setup(p, dram)
        init_consts(p, cx)
        stage_x_in(p, cx)
        if debug == "xcopy":
            pass
        elif debug == "ln1":
            for h in range(N_HALF):
                stage_ln1(p, cx, h)
                dump_af_to_xbuf(p, cx, h)
        elif debug == "consts":
            wd = p.wv(cx.s_dv, cx.s_ac, cx.s_pe, cx.s_gp, cx.s_st, cx.s_ld)
            gdma(p, cx, dram["Xbuf"][0:128, 0:5], cx.small[:, SM_DIMIDX:SM_DIMIDX + 5], wait=wd)
            gdma(p, cx, dram["Xbuf"][0:128, 5:9], cx.small[:, SM_TMP:SM_TMP + 4], wait=wd)
            p.emit("sync", lambda e: e.nop(), wait=p.wv(cx.s_st))
        elif debug and debug.startswith("tbl"):
            tabs = tables_chunk(p, cx, int(debug[3:] or 0))
            for i, k in enumerate(("csq", "snq", "csk", "snk")):
                gdma(p, cx, dram["Xbuf"][i * 128:(i + 1) * 128, 0:CH], tabs[k],
                     wait=p.wv(cx.s_dv, cx.s_ac, cx.s_pe, cx.s_gp))
            p.emit("sync", lambda e: e.nop(), wait=p.wv(cx.s_st))
        elif debug == "p1":
            stage_ln1(p, cx, 0)
            for g in range(8):
                p.emit("vector", lambda e, g=g: e.memset(_af(cx, g, 0, 1), 0.0),
                       wait=p.wv(cx.s_dv, cx.s_ac, cx.s_pe), inc=(cx.s_dv, 1))
            stage_attn_layer(p, cx, 0, 0, p1_only=True)
            wd = p.wv(cx.s_dv, cx.s_ac, cx.s_pe)
            for g in range(4):
                qh = _regb(cx, FF_QH + g * 4096, 2048)
                dve(p, cx, lambda e, g=g, i=qh: e.tensor_copy(_reg(cx, FF_TSC, 512), i[:, 0:512]), wait=wd)
                gdma(p, cx, dram["Xbuf"][g * 128:(g + 1) * 128, 0:512], _reg(cx, FF_TSC, 512),
                     wait=p.wv(cx.s_dv))
            for g in range(4):
                kf = _regb(cx, FF_KF + g * 5120, 2560)
                dve(p, cx, lambda e, g=g, i=kf: e.tensor_copy(_reg(cx, FF_TSC, 512), i[:, 512:1024]),
                    wait=p.wv(cx.s_st))
                gdma(p, cx, dram["Xbuf"][512 + g * 128:512 + (g + 1) * 128, 0:512], _reg(cx, FF_TSC, 512),
                     wait=p.wv(cx.s_dv))
            for s in range(4):
                vt = _regb(cx, FF_VT + (4 + s) * 1024, 512)
                dve(p, cx, lambda e, i=vt: e.tensor_copy(_reg(cx, FF_TSC, 512), i),
                    wait=p.wv(cx.s_st))
                gdma(p, cx, dram["Xbuf"][512 + s * 128:512 + (s + 1) * 128, 512:1024],
                     _reg(cx, FF_TSC, 512), wait=p.wv(cx.s_dv))
            p.emit("sync", lambda e: e.nop(), wait=p.wv(cx.s_st))
        elif debug == "attn0":
            for h in range(N_HALF):
                stage_ln1(p, cx, h)
                stage_ffn(p, cx, h, only="attn0")
                dump_af_to_xbuf(p, cx, h)
        elif debug == "l0":
            for h in range(N_HALF):
                stage_ln1(p, cx, h)
                stage_ffn(p, cx, h, only="l0")
                dump_af_to_xbuf(p, cx, h)
        elif debug == "mamba":
            for h in range(N_HALF):
                stage_mamba(p, cx, h)
        elif debug == "ffn1":
            for h in range(N_HALF):
                stage_ln1(p, cx, h)
                stage_ffn(p, cx, h)
                dump_af_to_xbuf(p, cx, h)
        else:
            full_chain(p, cx)
        stage_x_out(p, cx)
        p.build()
    return nc


def full_chain(p, cx):
    for h in range(N_HALF):
        stage_ln1(p, cx, h)
        stage_ffn(p, cx, h)
        stage_combine(p, cx, h)
    for h in range(N_HALF):
        stage_mamba(p, cx, h)
    for h in range(N_HALF):
        stage_ln1(p, cx, h)
        stage_ffn(p, cx, h)
        stage_combine(p, cx, h)


def kernel(**inputs):
    pre = _prep_inputs(inputs)
    hs_full = np.ascontiguousarray(np.asarray(inputs["L0_attn_hidden_states"], np.float32))
    nc = build_program()
    in_maps = []
    for c in range(8):
        m = dict(pre)
        m["hs"] = hs_full[c // 2]
        in_maps.append(m)
    res = run_bass_kernel_spmd(nc, in_maps, core_ids=list(range(8)))
    outs = [res.results[c]["out"] for c in (0, 2, 4, 6)]
    return np.stack(outs, axis=0)



NM3 = T_HALF // TCM


def stage_mamba(p, cx, h):
    d = cx.dram
    KB = lambda off, n: _regb(cx, off, n)
    RG = lambda off, n: _reg(cx, off, n)
    p.barrier()
    p.cur_stage = "m1"

    wu = p.wv(cx.s_pe, cx.s_dv, cx.s_ac, cx.s_st, cx.s_ld)
    for kk in range(8):
        gdma(p, cx, KB(MB_WIN + kk * 8192, 4096),
             d["inT"][kk * 128:(kk + 1) * 128, :], wait=wu)
    if h == 0:
        dve(p, cx, lambda e: e.memset(cx.small[:, SM_STATE:SM_STATE + 256], 0.0), wait=wu)
    for c in range(NCH):
        p.barrier()
        t0 = h * T_HALF + c * CH
        xch = [RG(MB_XCH + g * 2048, CH) for g in range(8)]
        xb = [KB(MB_XB + g * 1024, CH) for g in range(8)]
        wld = p.wv(cx.s_dv, cx.s_ac, cx.s_pe, cx.s_st)
        for g in range(8):
            ldma(p, cx, xch[g], d["Xbuf"][g * 128:(g + 1) * 128, t0:t0 + CH], wait=wld)
            gdma(p, cx, xb[g], d["Xbuf"][g * 128:(g + 1) * 128, t0:t0 + CH], wait=wld)
        ln_stats(p, cx, xb, MB_SQ)
        ln_rows(p, cx, mamba=True)
        wmr = p.wv(cx.s_pe, cx.s_st)
        for g in range(8):
            lnx = RG(MB_LNX + g * 2048, CH)
            lnm = KB(MB_LNM + g * 1024, CH)
            nwc = cx.small[:, SM_NW + g:SM_NW + g + 1]
            dve(p, cx, lambda e, x_=xch[g]: e.scalar_tensor_tensor(x_, cx.PS[:, 1024:1024 + CH],
                                                                   -1.0, x_, ALU.mult, ALU.add),
                wait=wmr if g == 0 else ())
            dve(p, cx, lambda e, o=lnx, x_=xch[g]: e.tensor_tensor(o, x_, cx.PS[:, 0:CH], ALU.mult))
            dve(p, cx, lambda e, x_=xch[g]: e.tensor_tensor(x_, x_, cx.PS[:, 1536:1536 + CH], ALU.mult))
            dve(p, cx, lambda e, o=lnm, x_=xch[g], w=nwc: e.tensor_scalar_mul(o, x_, w))
            gdma(p, cx, d["lnx_s"][g * 128:(g + 1) * 128, t0:t0 + CH], lnx, wait=p.wv(cx.s_dv))
        for m in range(32):
            bank = cx.PS[:, (m % 2) * 512:(m % 2) * 512 + 512]
            wmm = p.wv(cx.s_dv, cx.s_ac, cx.s_st)
            for kk in range(8):
                lhsT = KB(MB_WIN + kk * 8192 + m * 256, 128)
                pe(p, cx, lambda e, o=bank, lt=lhsT, r=KB(MB_LNM + kk * 1024, CH), kk=kk:
                   e.matmul(o, lt, r, start=(kk == 0), stop=(kk == 7)), wait=wmm if kk == 0 else ())
            slot = RG(MB_XZS + (m % 4) * 2048, CH)
            dve(p, cx, lambda e, o=slot, b=bank: e.tensor_copy(o, b),
                wait=p.wv(cx.s_pe, cx.s_st))
            dst = (d["xin_s"][(m % 16) * 128:(m % 16 + 1) * 128, t0:t0 + CH] if m < 16
                   else d["z_s"][(m - 16) * 128:(m - 15) * 128, t0:t0 + CH])
            gdma(p, cx, dst, slot, wait=p.wv(cx.s_dv))

    p.cur_stage = "m2"
    wu = p.wv(cx.s_pe, cx.s_dv, cx.s_ac, cx.s_st, cx.s_ld)
    p.emit("sync", lambda e: e.nop(), wait=p.wv(cx.s_st))
    ldma(p, cx, cx.small[:, SM_CW:SM_CW + 64], d["convw"][:, :], wait=wu)
    for kk in range(16):
        gdma(p, cx, KB(MB_WXP + kk * 224, 112), d["xpT"][kk * 128:(kk + 1) * 128, :], wait=wu)
    for c in range(NCH):
        p.barrier()
        t0 = h * T_HALF + c * CH
        wld = p.wv(cx.s_dv, cx.s_ac, cx.s_pe, cx.s_st)
        for g in range(16):
            tile = RG(MB_XIN + g * 2060, 515)
            if t0 == 0:
                dve(p, cx, lambda e, o=tile[:, 0:3]: e.memset(o, 0.0), wait=wld if g == 0 else ())
                ldma(p, cx, tile[:, 3:515], d["xin_s"][g * 128:(g + 1) * 128, 0:CH], wait=wld)
            else:
                ldma(p, cx, tile[:, 0:515], d["xin_s"][g * 128:(g + 1) * 128, t0 - 3:t0 + CH],
                     wait=wld)
        for g in range(16):
            tile = RG(MB_XIN + g * 2060, 515)
            xt = RG(MB_XT + g * 2048, CH)
            xtb = KB(MB_XTB + g * 1024, CH)
            wcv = p.wv(cx.s_ld, cx.s_dv, cx.s_ac, cx.s_pe, cx.s_st)
            dve(p, cx, lambda e, o=xt, i=tile, w=cx.small[:, SM_CW + g:SM_CW + g + 1]:
                e.tensor_scalar_mul(o, i[:, 0:CH], w), wait=wcv)
            for k in range(1, 4):
                dve(p, cx, lambda e, o=xt, i=tile, k=k,
                    w=cx.small[:, SM_CW + k * 16 + g:SM_CW + k * 16 + g + 1]:
                    e.scalar_tensor_tensor(o, i[:, k:k + CH], w, o, ALU.mult, ALU.add))
            act(p, cx, lambda e, o=xt, g=g: e.activation(
                out=o, in_=o, func=ACTF.Silu, bias=cx.small[:, SM_CB + g:SM_CB + g + 1]),
                wait=p.wv(cx.s_dv))
            dve(p, cx, lambda e, o=xtb, i=xt: e.tensor_copy(o, i), wait=p.wv(cx.s_ac))
            gdma(p, cx, d["xt_s"][g * 128:(g + 1) * 128, t0:t0 + CH], xt, wait=p.wv(cx.s_dv))
        bank = cx.PS[:, 1024:1536]
        wmm = p.wv(cx.s_dv, cx.s_ac)
        for kk in range(16):
            pe(p, cx, lambda e, lt=KB(MB_WXP + kk * 224, 112), r=KB(MB_XTB + kk * 1024, CH), kk=kk:
               e.matmul(cx.PS[0:112, 1024:1536], lt, r, start=(kk == 0), stop=(kk == 15)),
               wait=wmm if kk == 0 else ())
        dve(p, cx, lambda e, o=RG(MB_XP, 2048)[0:112, c * CH:(c + 1) * CH]:
            e.tensor_copy(o, cx.PS[0:112, 1024:1536]), wait=p.wv(cx.s_pe))

    p.cur_stage = "m3"
    wu = p.wv(cx.s_pe, cx.s_dv, cx.s_ac, cx.s_st, cx.s_ld)
    for kk in range(16):
        gdma(p, cx, KB(M3_WOUT + kk * 2048, 1024), d["outT"][kk * 128:(kk + 1) * 128, :], wait=wu)
    gdma(p, cx, KB(M3_WDT, 2048)[0:64, :], d["dtT"][0:64, :], wait=wu)
    dve(p, cx, lambda e: e.tensor_copy(KB(M3_DTRB, 2048)[0:64, :], RG(MB_XP, 2048)[0:64, :]),
        wait=wu)
    for c8 in range(NM3):
        p.barrier()
        t0 = h * T_HALF + c8 * TCM
        cs = c8 * TCM
        seln = RG(M3_DA, 2048)
        ldma(p, cx, seln, d["seln"][:, :], wait=p.wv(cx.s_dv, cx.s_ac, cx.s_pe, cx.s_st))
        for n in range(16):
            wmm = p.wv(cx.s_dv, cx.s_ac, cx.s_ld)
            pe(p, cx, lambda e, lt=seln[64:80, n * 128:(n + 1) * 128],
               r=RG(MB_XP, 2048)[64:80, cs:cs + TCM]:
               e.matmul(cx.PS[:, 2048:2048 + TCM], lt, r, start=True, stop=True), wait=wmm)
            dve(p, cx, lambda e, o=RG(M3_BBC + n * 1024, TCM): e.tensor_copy(o, cx.PS[:, 2048:2048 + TCM]),
                wait=p.wv(cx.s_pe))
            pe(p, cx, lambda e, lt=seln[96:112, n * 128:(n + 1) * 128],
               r=RG(MB_XP, 2048)[96:112, cs:cs + TCM]:
               e.matmul(cx.PS[:, 2560:2560 + TCM], lt, r, start=True, stop=True,
                        tile_position=(96, 0)))
            dve(p, cx, lambda e, o=RG(M3_CBC + n * 1024, TCM): e.tensor_copy(o, cx.PS[:, 2560:2560 + TCM]),
                wait=p.wv(cx.s_pe))
        for g in range(16):
            DT = RG(M3_DT, TCM)
            UM = RG(M3_UM, TCM)
            XTG = RG(M3_XTG, TCM)
            ZG = RG(M3_ZG, TCM)
            SZG = RG(M3_SZG, TCM)
            DA = RG(M3_DA, 16 * TCM)
            BH = RG(M3_BH, 16 * TCM)
            wld = p.wv(cx.s_dv, cx.s_ac, cx.s_st)
            ldma(p, cx, XTG, d["xt_s"][g * 128:(g + 1) * 128, t0:t0 + TCM], wait=wld)
            ldma(p, cx, ZG, d["z_s"][g * 128:(g + 1) * 128, t0:t0 + TCM], wait=wld)
            wmm = p.wv(cx.s_dv, cx.s_ac)
            pe(p, cx, lambda e, lt=KB(M3_WDT + g * 256, 128)[0:64, :], r=KB(M3_DTRB, 2048)[0:64, cs:cs + TCM]:
               e.matmul(cx.PS[:, 1024:1024 + TCM], lt, r, start=True, stop=True), wait=wmm)
            act(p, cx, lambda e, g=g: e.activation(out=RG(M3_SZG, TCM), in_=cx.PS[:, 1024:1024 + TCM],
                                                   func=ACTF.Exp,
                                                   bias=cx.small[:, SM_DTB + g:SM_DTB + g + 1]),
                wait=p.wv(cx.s_pe))
            act(p, cx, lambda e: e.activation(out=DT, in_=RG(M3_SZG, TCM), func=ACTF.Ln,
                                              bias=ccol(p, cx, 1.0)))
            dve(p, cx, lambda e, o=UM, i=XTG: e.tensor_tensor(o, DT, i, ALU.mult),
                wait=p.wv(cx.s_ac, cx.s_ld))
            for n in range(16):
                acol = cx.small[:, SM_ANEG + g * 16 + n:SM_ANEG + g * 16 + n + 1]
                act(p, cx, lambda e, o=DA[:, n * TCM:(n + 1) * TCM], a=acol:
                    e.activation(out=o, in_=DT, func=ACTF.Exp, scale=a),
                    wait=p.wv(cx.s_dv) if n == 0 else ())
            for n in range(16):
                dve(p, cx, lambda e, o=BH[:, n * TCM:(n + 1) * TCM], b=RG(M3_BBC + n * 1024, TCM):
                    e.tensor_tensor(o, UM, b, ALU.mult), wait=p.wv(cx.s_ac) if n == 0 else ())
            for n in range(16):
                st = cx.small[:, SM_STATE + g * 16 + n:SM_STATE + g * 16 + n + 1]
                sl = BH[:, n * TCM:(n + 1) * TCM]
                dve(p, cx, lambda e, o=sl, a=DA[:, n * TCM:(n + 1) * TCM], b=sl, s=st:
                    e.tensor_tensor_scan(o, a, b, s, ALU.mult, ALU.add))
            dve(p, cx, lambda e, o=cx.small[:, SM_STATE + g * 16:SM_STATE + (g + 1) * 16],
                i=BH[:, TCM - 1::TCM]: e.tensor_copy(o, i))
            dav = DA.rearrange("p (n t) -> p n t", n=16)
            da_t = DA.rearrange("p (t n) -> p t n", n=16)
            for n in range(16):
                dve(p, cx, lambda e, o=da_t[:, :, n], hh_=BH[:, n * TCM:(n + 1) * TCM],
                    cc=RG(M3_CBC + n * 1024, TCM): e.tensor_tensor(o, hh_, cc, ALU.mult))
            dve(p, cx, lambda e: e.tensor_reduce(UM, da_t, AX.X, ALU.add))
            dve(p, cx, lambda e, i=XTG, dc=cx.small[:, SM_DVEC + g:SM_DVEC + g + 1]:
                e.scalar_tensor_tensor(UM, i, dc, UM, ALU.mult, ALU.add))
            act(p, cx, lambda e, i=ZG: e.activation(out=SZG, in_=i, func=ACTF.Silu),
                wait=p.wv(cx.s_dv))
            dve(p, cx, lambda e, o=KB(M3_YSB + g * 512, TCM): e.tensor_tensor(o, UM, SZG, ALU.mult),
                wait=p.wv(cx.s_ac))
        for m in range(8):
            bank = cx.PS[:, (m % 2) * 512:(m % 2) * 512 + 512][:, 0:TCM]
            wmm = p.wv(cx.s_dv, cx.s_ac, cx.s_ld)
            for kk in range(16):
                pe(p, cx, lambda e, o=bank, lt=KB(M3_WOUT + kk * 2048 + m * 256, 128),
                   r=KB(M3_YSB + kk * 512, TCM), kk=kk:
                   e.matmul(o, lt, r, start=(kk == 0), stop=(kk == 15)),
                   wait=wmm if kk == 0 else ())
            xrm = RG(M3_DA + m * 1024, TCM)
            lnxc = RG(M3_DA + 8192 + m * 1024, TCM)
            wrd = p.wv(cx.s_dv, cx.s_ac, cx.s_st)
            ldma(p, cx, xrm, d["Xbuf"][m * 128:(m + 1) * 128, t0:t0 + TCM], wait=wrd)
            ldma(p, cx, lnxc, d["lnx_s"][m * 128:(m + 1) * 128, t0:t0 + TCM], wait=wrd)
            dve(p, cx, lambda e, o=lnxc, b=bank: e.tensor_tensor(o, b, o, ALU.add),
                wait=p.wv(cx.s_pe, cx.s_ld))
            dve(p, cx, lambda e, o=xrm, i=lnxc: e.tensor_tensor(o, i, o, ALU.add))
            gdma(p, cx, d["Xbuf"][m * 128:(m + 1) * 128, t0:t0 + TCM], xrm, wait=p.wv(cx.s_dv))
    p.emit("sync", lambda e: e.nop(), wait=p.wv(cx.s_st))


# version 10
# speedup vs baseline: 1.1332x; 1.0139x over previous
import contextlib
import math
import numpy as np
import concourse.bass as bass
import concourse.mybir as mybir
from concourse.bass_utils import run_bass_kernel_spmd

F32 = mybir.dt.float32
BF16 = mybir.dt.bfloat16
I32 = mybir.dt.int32
ALU = mybir.AluOpType
ACTF = mybir.ActivationFunctionType
AX = mybir.AxisListType

D_MODEL = 1024
D_INNER = 2048
D_STATE = 16
DT_RANK = 64
HEADS = 8
WINDOW = 512
HID = 2730
HIDP = 2816
QK_SCALE = 8.0
T_TOT = 4096
NEG_BIG = -1e38
DM_T = 8
DI_T = 16
N_HALF = 2
T_HALF = 2048
CH = 512
NCH = T_HALF // CH
TCM = 256
PI = math.pi
TWO_PI_HI = np.float32(2 * PI)
TWO_PI_LO = np.float32(2 * PI - float(np.float32(2 * PI)))
INV_2PI = np.float32(1.0 / (2 * PI))
PI_HALF = np.float32(PI / 2)


class _Sem:
    __slots__ = ("h", "name", "n", "emitted")

    def __init__(self, h, name):
        self.h, self.name, self.n, self.emitted = h, name, 0, 0


class Prog:
    ENGINES = ("sync", "scalar", "vector", "gpsimd", "tensor")

    def __init__(self, nc, stack):
        self.nc, self.stack = nc, stack
        self.prog = {e: [] for e in self.ENGINES}
        self._semid = 0

    def sem(self, name=None):
        name = name or f"s{self._semid}"
        self._semid += 1
        h = self.stack.enter_context(self.nc.semaphore(name))
        return _Sem(h, name)

    def sbuf(self, name, shape, dt):
        return self.stack.enter_context(self.nc.sbuf_tensor(name, shape, dt))

    def psum(self, name, shape, dt=F32):
        return self.stack.enter_context(self.nc.psum_tensor(name, shape, dt))

    SAFE = False
    BARRIERS = False
    SAFE_STAGES = set()
    cur_stage = ""

    def emit(self, eng, fn, wait=None, inc=None):
        if (Prog.SAFE or (self.cur_stage in Prog.SAFE_STAGES)) and getattr(self, "allsems", None):
            wait = [(s, s.n) for s in self.allsems]
        if inc is not None:
            sem, k = inc
            sem.n += k
            inc = (sem, sem.n)
        self.prog[eng].append((fn, list(wait or []), inc))

    def wv(self, *sems):
        return [(s, s.n) for s in sems]

    def barrier(self):
        if not Prog.BARRIERS:
            return
        w = [(s, s.n) for s in self.allsems]
        for eng in self.ENGINES:
            self.prog[eng].append(((lambda e: e.nop()), list(w), None))

    def build(self):
        with self.nc.Block() as block:
            for eng_name in self.ENGINES:
                seq = self.prog[eng_name]
                if not seq:
                    continue

                def body(eng, seq=seq):
                    for fn, waits, inc in seq:
                        for s, v in waits:
                            if v > 0:
                                eng.wait_ge(s.h, v)
                        inst = fn(eng)
                        if inc is not None:
                            s, v = inc
                            inst.then_inc(s.h, v - s.emitted)
                            s.emitted = v

                getattr(block, eng_name)(body)



def _colvec(v):
    v = np.asarray(v, np.float32).reshape(-1)
    return np.ascontiguousarray(v.reshape(-1, 128).T)


def _prep_inputs(ix):
    f = np.float32
    pre = {}
    for l in range(2):
        pre[f"wqkv{l}"] = np.ascontiguousarray(ix["L0_ffn_wqkv"][l], f)
        pre[f"wo{l}"] = np.ascontiguousarray(ix["L0_ffn_wo"][l], f)
        w1 = np.asarray(ix["L0_ffn_w1"][l], f)
        w1p = np.zeros((D_MODEL, 2 * HIDP), f)
        w1p[:, :HID] = w1[:, :HID]
        w1p[:, HIDP:HIDP + HID] = w1[:, HID:]
        pre[f"w1{l}"] = w1p
        w2 = np.asarray(ix["L0_ffn_w2"][l], f)
        w2p = np.zeros((HIDP, D_MODEL), f)
        w2p[:HID] = w2
        pre[f"w2{l}"] = w2p
        b1 = np.asarray(ix["L0_ffn_b1"][l], f)
        b1p = np.zeros((2 * HIDP,), f)
        b1p[:HID] = b1[:HID]
        b1p[HIDP:HIDP + HID] = b1[HID:]
        pre[f"b1{l}"] = _colvec(b1p)
        pre[f"b2{l}"] = _colvec(ix["L0_ffn_b2"][l])
        pre[f"lg{l}"] = _colvec(ix["L0_ffn_ln_g"][l])
        pre[f"lb{l}"] = _colvec(ix["L0_ffn_ln_b"][l])
        pre[f"qs{l}"] = np.ascontiguousarray(np.tile(np.asarray(ix["L0_ffn_q_scale"][l], f), 2).reshape(128, 1))
        pre[f"ks{l}"] = np.ascontiguousarray(np.tile(np.asarray(ix["L0_ffn_k_scale"][l], f), 2).reshape(128, 1))
    pre["inT"] = np.ascontiguousarray(np.asarray(ix["L0_attn_in_proj_w"], f).T)
    pre["outT"] = np.ascontiguousarray(np.asarray(ix["L0_attn_out_proj_w"], f).T)
    xpT = np.asarray(ix["L0_attn_x_proj_w"], f).T
    xpp = np.zeros((D_INNER, 112), f)
    xpp[:, 0:80] = xpT[:, 0:80]
    xpp[:, 96:112] = xpT[:, 80:96]
    pre["xpT"] = xpp
    pre["dtT"] = np.ascontiguousarray(np.asarray(ix["L0_attn_dt_proj_w"], f).T)
    cw = np.asarray(ix["L0_attn_conv_w"], f)[:, 0, :]
    pre["convw"] = np.ascontiguousarray(
        np.concatenate([_colvec(cw[:, k]) for k in range(4)], axis=1))
    pre["nw"] = _colvec(ix["L0_attn_norm_weight"])
    pre["cb"] = _colvec(ix["L0_attn_conv_b"])
    pre["dtb"] = _colvec(ix["L0_attn_dt_proj_b"])
    pre["Dvec"] = _colvec(ix["L0_attn_D"])
    al = np.asarray(ix["L0_attn_A_log"], f).reshape(DI_T, 128, D_STATE)
    pre["AlogT"] = np.ascontiguousarray(al.transpose(1, 0, 2).reshape(128, DI_T * D_STATE))
    pre["dimidx"] = np.ascontiguousarray((2.0 * (np.arange(128) % 32)).reshape(128, 1).astype(f))
    pre["ident"] = np.ascontiguousarray(np.eye(128, dtype=f))
    R = np.zeros((128, 128), f)
    for m in range(128):
        if m % 64 < 32:
            R[(m + 32), m] = -1.0
        else:
            R[(m - 32), m] = 1.0
    pre["rotm"] = R
    bm = np.zeros((128, 128), f)
    bm[:64, :64] = 1.0
    bm[64:, 64:] = 1.0
    pre["headmask"] = bm
    sel = np.zeros((128, 16 * 128), f)
    for n in range(16):
        sel[64 + n, n * 128:(n + 1) * 128] = 1.0
        sel[96 + n, n * 128:(n + 1) * 128] = 1.0
    pre["seln"] = sel
    pre["ten_thousand"] = np.full((1, 1), 10000.0, f)
    dimidx = (2.0 * (np.arange(128) % 32)).astype(np.float64)
    l10k = np.log(np.float64(10000.0))
    invf = np.exp(dimidx * np.float64(-1.0 / 64.0) * l10k)
    t_all = np.arange(T_TOT, dtype=np.float64)
    fr = invf[:, None] * t_all[None, :]
    c = np.float64(np.float32(1.0 / (2.0 * math.pi)))
    pre["rotk1"] = np.floor(fr * c + 0.5).astype(f)
    pre["rotk2"] = np.floor(fr * c + 0.75).astype(f)
    return pre


_IN_SHAPES = {
    "hs": [T_TOT, D_MODEL],
    "inT": [D_MODEL, 2 * D_INNER], "outT": [D_INNER, D_MODEL],
    "xpT": [D_INNER, 112], "dtT": [DT_RANK, D_INNER], "convw": [128, 64],
    "nw": [128, 8], "cb": [128, 16], "dtb": [128, 16], "Dvec": [128, 16],
    "AlogT": [128, DI_T * D_STATE], "dimidx": [128, 1], "ident": [128, 128],
    "rotm": [128, 128], "headmask": [128, 128], "seln": [128, 16 * 128],
    "ten_thousand": [1, 1],
    "rotk1": [128, T_TOT], "rotk2": [128, T_TOT],
}
for _l in range(2):
    _IN_SHAPES.update({
        f"wqkv{_l}": [D_MODEL, 1536], f"wo{_l}": [512, D_MODEL],
        f"w1{_l}": [D_MODEL, 2 * HIDP], f"w2{_l}": [HIDP, D_MODEL],
        f"b1{_l}": [128, 44], f"b2{_l}": [128, 8], f"lg{_l}": [128, 8],
        f"lb{_l}": [128, 8], f"qs{_l}": [128, 1], f"ks{_l}": [128, 1],
    })



class Ctx:
    pass


def _reg(cx, byte_off, n_f32):
    assert byte_off % 4 == 0
    c = byte_off // 4
    assert c + n_f32 <= cx.SLAB_COLS, (byte_off, n_f32)
    return cx.slab[:, c:c + n_f32]


def _regb(cx, byte_off, n_bf16):
    assert byte_off % 4 == 0
    c = byte_off // 4
    assert c * 4 + n_bf16 * 2 <= cx.SLAB_COLS * 4
    return cx.slab[:, c:c + (n_bf16 + 1) // 2].bitcast(BF16)[:, :n_bf16]


def setup(p, dram):
    cx = Ctx()
    cx.dram = dram
    cx.ident_f = p.sbuf("ident_f", [128, 128], F32)
    cx.ident_b = p.sbuf("ident_b", [128, 128], BF16)
    cx.rot_b = p.sbuf("rot_b", [128, 128], BF16)
    cx.hmask_b = p.sbuf("hmask_b", [128, 128], BF16)
    cx.triu_b = p.sbuf("triu_b", [128, 128], BF16)
    cx.tril_b = p.sbuf("tril_b", [128, 128], BF16)
    cx.negb_b = p.sbuf("negb_b", [128, 128], BF16)
    cx.ones_b = p.sbuf("ones_b", [128, 1], BF16)
    cx.small = p.sbuf("small", [128, 3072], F32)
    cx.SLAB_COLS = 48592
    cx.slab = p.sbuf("slab", [128, cx.SLAB_COLS], F32)
    cx.PS = p.psum("PS", [128, 3072], F32)
    cx.PSB = p.psum("PSB", [128, 2048], BF16)
    cx.s_ld = p.sem("ld")
    cx.s_st = p.sem("st")
    cx.s_ld2 = p.sem("ld2")
    cx.s_pe = p.sem("pe")
    cx.s_dv = p.sem("dv")
    cx.s_ac = p.sem("ac")
    cx.s_gp = p.sem("gp")
    cx.consts = {}
    cx.n_consts = 0
    p.allsems = [cx.s_ld, cx.s_st, cx.s_ld2, cx.s_pe, cx.s_dv, cx.s_ac, cx.s_gp]
    return cx


SM_B1 = {0: 0, 1: 44}
SM_B2 = {0: 88, 1: 96}
SM_LG = {0: 104, 1: 112}
SM_LB = {0: 120, 1: 128}
SM_QS = {0: 136, 1: 137}
SM_KS = {0: 138, 1: 139}
SM_NW = 140
SM_CB = 148
SM_DTB = 164
SM_DVEC = 180
SM_ANEG = 196
SM_DIMIDX = 452
SM_INVF = 453
SM_LNB = 454
SM_L10K = 455
SM_BASE = 456
SM_MX = 460
SM_NMX = 461
SM_RS = 462
SM_RC = 463
SM_STATE = 464
SM_CONVH = 736
SM_AHALO = 800
SM_ONESR = 840
SM_STAT = 1024
SM_STAT2 = 1536
SM_TMP = 2048


SLAB_BYTES = 48592 * 4

A_F = 0
FF_ABF = 65568
FF_QH = 73856
FF_KF = 90240
FF_VT = 110720
FF_SQB = 131200
FF_TBL = 135296
FF_TSC = 143488
FF_RRS = FF_TSC
FF_Q2 = FF_TSC + 2048
FF_UB = FF_TSC + 4096
FF_WSCR = FF_TSC + 6144
FF_WQ = 151680
FF_WO = 176256
FF_OB = 131200
FF_XCH = FF_QH
FF_XB = FF_VT
FF_PSC = 184448
FF_PT = 187008

FF_SHLN = 65568
FF_W1 = 98336
FF_W2 = 143392
FF_HP = 165920
FF_GLB = 177184
FF_ABF2 = 98336
FF_TMP45 = 106624
FF_SQ45 = 123008

MB_LNM = 65568
MB_SQ = 73760
MB_XCH = 77856
MB_XB = 94240
MB_WIN = 102432
MB_XZS = 167968
MB_LNX = 176160
MB_XIN = 65568
MB_XT = 98560
MB_XTB = 131328
MB_WXP = 147712
M3_DA = 65568
M3_BH = 81952
M3_DT = 98336
M3_UM = 99360
M3_XTG = 100384
M3_ZG = 101408
M3_SZG = 102432
M3_YG = M3_UM
M3_YSB = 103456
M3_WDT = 111648
M3_DTRB = 115744
M3_WOUT = 119840
M3_BBC = 152608
M3_CBC = 168992
MB_XP = 185376

assert FF_PT + 1280 <= SLAB_BYTES
assert FF_GLB + 1024 <= SLAB_BYTES
assert MB_XZS + 8192 <= SLAB_BYTES
assert MB_XP + 8192 <= SLAB_BYTES



def _allsems(cx):
    return (cx.s_ld, cx.s_st, cx.s_ld2, cx.s_pe, cx.s_dv, cx.s_ac, cx.s_gp)


def barrier_wait(p, cx):
    return p.wv(*_allsems(cx))


def dve(p, cx, fn, wait=()):
    p.emit("vector", fn, wait=wait, inc=(cx.s_dv, 1))


def vnop(p, cx):
    p.emit("vector", lambda e: e.nop())


def act(p, cx, fn, wait=()):
    p.emit("scalar", fn, wait=wait, inc=(cx.s_ac, 1))


def pe(p, cx, fn, wait=()):
    p.emit("tensor", fn, wait=wait, inc=(cx.s_pe, 1))


def ldma(p, cx, out, in_, wait=()):
    p.emit("sync", lambda e, o=out, i=in_: e.dma_start(out=o, in_=i), wait=wait, inc=(cx.s_ld, 16))


def gdma(p, cx, out, in_, wait=()):
    p.emit("gpsimd", lambda e, o=out, i=in_: e.dma_start(out=o, in_=i), wait=wait, inc=(cx.s_st, 16))


SM_CONST = 968
SM_CW = 2100


def ccol(p, cx, val):
    val = float(val)
    if val in cx.consts:
        return cx.consts[val]
    idx = SM_CONST + cx.n_consts
    cx.n_consts += 1
    assert idx < 3072
    ap = cx.small[:, idx:idx + 1]
    p.emit("vector", lambda e, a=ap, v=val: e.memset(a, v), inc=(cx.s_dv, 1))
    cx.consts[val] = ap
    return ap


def sm(cx, col, n=1, row0=False):
    t = cx.small[0:1, col:col + n] if row0 else cx.small[:, col:col + n]
    return t



def init_consts(p, cx):
    d = cx.dram
    for v in (1e-5, 1.0, -0.5, -1.0, 0.5):
        ccol(p, cx, v)
    B = lambda: barrier_wait(p, cx)
    ldma(p, cx, cx.ident_f[:, :], d["ident"][:, :])
    gdma(p, cx, cx.ident_b[:, :], d["ident"][:, :])
    gdma(p, cx, cx.rot_b[:, :], d["rotm"][:, :])
    gdma(p, cx, cx.hmask_b[:, :], d["headmask"][:, :])
    for l in range(2):
        ldma(p, cx, cx.small[:, SM_B1[l]:SM_B1[l] + 44], d[f"b1{l}"][:, :])
        ldma(p, cx, cx.small[:, SM_B2[l]:SM_B2[l] + 8], d[f"b2{l}"][:, :])
        ldma(p, cx, cx.small[:, SM_LG[l]:SM_LG[l] + 8], d[f"lg{l}"][:, :])
        ldma(p, cx, cx.small[:, SM_LB[l]:SM_LB[l] + 8], d[f"lb{l}"][:, :])
        ldma(p, cx, cx.small[:, SM_QS[l]:SM_QS[l] + 1], d[f"qs{l}"][:, :])
        ldma(p, cx, cx.small[:, SM_KS[l]:SM_KS[l] + 1], d[f"ks{l}"][:, :])
    ldma(p, cx, cx.small[:, SM_NW:SM_NW + 8], d["nw"][:, :])
    ldma(p, cx, cx.small[:, SM_CB:SM_CB + 16], d["cb"][:, :])
    ldma(p, cx, cx.small[:, SM_DTB:SM_DTB + 16], d["dtb"][:, :])
    ldma(p, cx, cx.small[:, SM_DVEC:SM_DVEC + 16], d["Dvec"][:, :])
    ldma(p, cx, cx.small[:, SM_DIMIDX:SM_DIMIDX + 1], d["dimidx"][:, :])
    ldma(p, cx, cx.small[:, SM_TMP:SM_TMP + 256], d["AlogT"][:, :])
    ldma(p, cx, cx.small[0:1, SM_TMP + 384:SM_TMP + 385], d["ten_thousand"][:, :])
    act(p, cx, lambda e: e.activation(out=sm(cx, SM_TMP, 256), in_=sm(cx, SM_TMP, 256),
                                      func=ACTF.Exp), wait=B())
    dve(p, cx, lambda e: e.tensor_scalar_mul(sm(cx, SM_ANEG, 256), sm(cx, SM_TMP, 256), -1.0),
        wait=B())
    dve(p, cx, lambda e: e.memset(cx.ones_b[:, :], 1.0), wait=B())
    dve(p, cx, lambda e: e.memset(cx.negb_b[:, :], NEG_BIG))
    dve(p, cx, lambda e: e.memset(cx.small[0:1, SM_ONESR:SM_ONESR + 128], 1.0))
    p.emit("gpsimd", lambda e: e.iota(sm(cx, SM_TMP, 128), [[1, 128]], channel_multiplier=0,
                                      allow_small_or_imprecise_dtypes=True), wait=B(), inc=(cx.s_gp, 1))
    p.emit("gpsimd", lambda e: e.iota(sm(cx, SM_TMP + 128, 128), [[0, 128]], channel_multiplier=1,
                                      allow_small_or_imprecise_dtypes=True), inc=(cx.s_gp, 1))
    dve(p, cx, lambda e: e.tensor_tensor(sm(cx, SM_TMP + 256, 128), sm(cx, SM_TMP, 128),
                                         sm(cx, SM_TMP + 128, 128), ALU.is_lt), wait=B())
    dve(p, cx, lambda e: e.tensor_scalar_mul(cx.triu_b[:, :], sm(cx, SM_TMP + 256, 128), NEG_BIG),
        wait=B())
    dve(p, cx, lambda e: e.tensor_tensor(sm(cx, SM_TMP + 256, 128), sm(cx, SM_TMP, 128),
                                         sm(cx, SM_TMP + 128, 128), ALU.is_gt), wait=B())
    dve(p, cx, lambda e: e.tensor_scalar_mul(cx.tril_b[:, :], sm(cx, SM_TMP + 256, 128), NEG_BIG),
        wait=B())
    act(p, cx, lambda e: e.activation(out=cx.small[0:1, SM_TMP + 385:SM_TMP + 386],
                                      in_=cx.small[0:1, SM_TMP + 384:SM_TMP + 385], func=ACTF.Ln),
        wait=B())
    pe(p, cx, lambda e: e.matmul(cx.PS[:, 0:1], cx.small[0:1, SM_ONESR:SM_ONESR + 128],
                                 cx.small[0:1, SM_TMP + 385:SM_TMP + 386], start=True, stop=True),
       wait=B())
    dve(p, cx, lambda e: e.tensor_copy(sm(cx, SM_L10K), cx.PS[:, 0:1]), wait=B())
    vnop(p, cx)
    dve(p, cx, lambda e: e.tensor_scalar_mul(sm(cx, SM_TMP), sm(cx, SM_DIMIDX), -1.0 / 64.0),
        wait=B())
    vnop(p, cx)
    dve(p, cx, lambda e: e.tensor_tensor(sm(cx, SM_TMP + 1, 1), sm(cx, SM_TMP), sm(cx, SM_L10K),
                                         ALU.mult))
    vnop(p, cx)
    act(p, cx, lambda e: e.activation(out=sm(cx, SM_INVF), in_=sm(cx, SM_TMP + 1, 1),
                                      func=ACTF.Exp), wait=B())
    dve(p, cx, lambda e: e.tensor_scalar(sm(cx, SM_BASE), sm(cx, SM_DIMIDX), 25.6, 1.0 / 89.6,
                                         ALU.add, ALU.mult), wait=B())
    act(p, cx, lambda e: e.activation(out=sm(cx, SM_LNB), in_=sm(cx, SM_BASE), func=ACTF.Ln),
        wait=B())
    p.emit("sync", lambda e: e.nop(), wait=B())



def ln_stats(p, cx, xb_tiles, sq_slot_off, shifted=False):
    for g in range(8):
        pe(p, cx, lambda e, g=g: e.matmul(cx.PS[0:1, 2048:2048 + CH], cx.ones_b[:, 0:1],
                                          xb_tiles[g], start=(g == 0), stop=(g == 7)),
           wait=p.wv(cx.s_st, cx.s_dv, cx.s_ld))
        slot = _regb(cx, sq_slot_off + (g % 4) * 1024, CH)
        act(p, cx, lambda e, s=slot, g=g: e.activation(out=s, in_=xb_tiles[g], func=ACTF.Square),
            wait=p.wv(cx.s_pe, cx.s_st, cx.s_ld))
        pe(p, cx, lambda e, s=slot, g=g: e.matmul(cx.PS[0:1, 2560:2560 + CH], cx.ones_b[:, 0:1],
                                                  s, start=(g == 0), stop=(g == 7)),
           wait=p.wv(cx.s_ac))


def ln_rows(p, cx, mamba=False):
    S = 1024
    m_row = cx.small[0:1, S:S + CH]
    r_row = cx.small[0:1, S + 512:S + 512 + CH]
    t1 = cx.small[0:1, S + 1024:S + 1024 + CH]
    w = p.wv(cx.s_pe)
    dve(p, cx, lambda e: e.tensor_scalar_mul(m_row, cx.PS[0:1, 2048:2048 + CH], 1.0 / 1024.0), wait=w)
    dve(p, cx, lambda e: e.tensor_scalar_mul(t1, cx.PS[0:1, 2560:2560 + CH], 1.0 / 1024.0))
    dve(p, cx, lambda e: e.tensor_tensor(r_row, m_row, m_row, ALU.mult))
    dve(p, cx, lambda e: e.tensor_tensor(t1, t1, r_row, ALU.subtract))
    w = p.wv(cx.s_dv)
    act(p, cx, lambda e: e.activation(out=t1, in_=t1, func=ACTF.Ln,
                                      bias=ccol(p, cx, 1e-5)[0:1, :]), wait=w)
    if not mamba:
        act(p, cx, lambda e: e.activation(out=r_row, in_=t1, func=ACTF.Exp,
                                          scale=ccol(p, cx, -0.5)[0:1, :]))
    else:
        r1_row = cx.small[0:1, S + 1536:S + 1536 + CH]
        act(p, cx, lambda e: e.activation(out=r1_row, in_=t1, func=ACTF.Exp,
                                          scale=ccol(p, cx, -0.5)[0:1, :]))
        act(p, cx, lambda e: e.activation(out=r_row, in_=t1, func=ACTF.Exp,
                                          scale=ccol(p, cx, -1.0)[0:1, :]))
        dve(p, cx, lambda e: e.tensor_scalar(r_row, r_row, -1e-5, 1.0, ALU.mult, ALU.add),
            wait=p.wv(cx.s_ac))
        act(p, cx, lambda e: e.activation(out=r_row, in_=r_row, func=ACTF.Ln,
                                          bias=ccol(p, cx, 1e-5)[0:1, :]), wait=p.wv(cx.s_dv))
        dve(p, cx, lambda e: e.tensor_tensor(t1, t1, r_row, ALU.add), wait=p.wv(cx.s_ac))
        act(p, cx, lambda e: e.activation(out=r_row, in_=t1, func=ACTF.Exp,
                                          scale=ccol(p, cx, -0.5)[0:1, :]), wait=p.wv(cx.s_dv))
    w = p.wv(cx.s_ac, cx.s_dv)
    pe(p, cx, lambda e: e.matmul(cx.PS[:, 1024:1024 + CH], cx.small[0:1, SM_ONESR:SM_ONESR + 128],
                                 m_row, start=True, stop=True), wait=w)
    pe(p, cx, lambda e: e.matmul(cx.PS[:, 1536:1536 + CH], cx.small[0:1, SM_ONESR:SM_ONESR + 128],
                                 r_row, start=True, stop=True))
    if mamba:
        r1_row = cx.small[0:1, S + 1536:S + 1536 + CH]
        pe(p, cx, lambda e: e.matmul(cx.PS[:, 0:CH], cx.small[0:1, SM_ONESR:SM_ONESR + 128],
                                     r1_row, start=True, stop=True))


def stage_ln1(p, cx, h, mamba=False, dst="a"):
    d = cx.dram
    p.cur_stage = "ln1"
    for c in range(NCH):
        p.barrier()
        t0 = h * T_HALF + c * CH
        xch = [_reg(cx, (FF_XCH if not mamba else MB_XCH) + g * 2048, CH) for g in range(8)]
        xb = [_regb(cx, (FF_XB if not mamba else MB_XB) + g * 1024, CH) for g in range(8)]
        wld = p.wv(cx.s_dv, cx.s_ac, cx.s_pe, cx.s_st)
        for g in range(8):
            ldma(p, cx, xch[g], d["Xbuf"][g * 128:(g + 1) * 128, t0:t0 + CH], wait=wld)
            gdma(p, cx, xb[g], d["Xbuf"][g * 128:(g + 1) * 128, t0:t0 + CH], wait=wld)
        ln_stats(p, cx, xb, (FF_SQB if not mamba else MB_LNM))
        ln_rows(p, cx, mamba=mamba)
        wmr = p.wv(cx.s_pe)
        for g in range(8):
            dve(p, cx, lambda e, g=g: e.scalar_tensor_tensor(xch[g], cx.PS[:, 1024:1024 + CH],
                                                             -1.0, xch[g], ALU.mult, ALU.add),
                wait=wmr)
            if not mamba:
                dst_ap = cx.slab[:, (A_F // 4) + g * 2049 + c * CH + 1: (A_F // 4) + g * 2049 + c * CH + 1 + CH]
                dve(p, cx, lambda e, g=g, o=dst_ap: e.tensor_tensor(o, xch[g], cx.PS[:, 1536:1536 + CH],
                                                                    ALU.mult))
            else:
                dve(p, cx, lambda e, g=g: e.tensor_tensor(xch[g], xch[g], cx.PS[:, 1536:1536 + CH],
                                                          ALU.mult))
                lnm = _regb(cx, MB_LNM + g * 1024, CH)
                dve(p, cx, lambda e, g=g, o=lnm: e.tensor_scalar_mul(o, xch[g],
                                                                     cx.small[:, SM_NW + g:SM_NW + g + 1]))



def tables_chunk(p, cx, t0):
    FR = _reg(cx, FF_TSC, CH)
    KF_ = _reg(cx, FF_TSC + 2048, CH)
    S1 = _reg(cx, FF_TSC + 4096, CH)
    S2 = _reg(cx, FF_TSC + 6144, CH)
    cs = _reg(cx, FF_TBL, CH)
    sn = _reg(cx, FF_TBL + 2048, CH)
    scq = _reg(cx, FF_TBL + 4096, CH)
    sck = _reg(cx, FF_TBL + 6144, CH)
    w = p.wv(cx.s_dv, cx.s_ac, cx.s_pe)
    p.emit("gpsimd", lambda e: e.iota(FR, [[1, CH]], base=t0, channel_multiplier=0,
                                      allow_small_or_imprecise_dtypes=True), wait=w, inc=(cx.s_gp, 1))
    wg = p.wv(cx.s_gp)
    dve(p, cx, lambda e: e.tensor_scalar_mul(FR, FR, sm(cx, SM_INVF)), wait=wg)
    ldma(p, cx, KF_, cx.dram["rotk1"][:, t0:t0 + CH], wait=p.wv(cx.s_dv, cx.s_ac))
    dve(p, cx, lambda e: e.scalar_tensor_tensor(S1, KF_, -float(TWO_PI_HI), FR, ALU.mult, ALU.add),
        wait=p.wv(cx.s_ld))
    dve(p, cx, lambda e: e.scalar_tensor_tensor(S1, KF_, -float(TWO_PI_LO), S1, ALU.mult, ALU.add))
    act(p, cx, lambda e: e.activation(out=sn, in_=S1, func=ACTF.Sin), wait=p.wv(cx.s_dv))
    ldma(p, cx, KF_, cx.dram["rotk2"][:, t0:t0 + CH], wait=p.wv(cx.s_dv, cx.s_ac))
    dve(p, cx, lambda e: e.scalar_tensor_tensor(S2, KF_, -float(TWO_PI_HI), FR, ALU.mult, ALU.add),
        wait=p.wv(cx.s_ld))
    dve(p, cx, lambda e: e.scalar_tensor_tensor(S2, KF_, -float(TWO_PI_LO), S2, ALU.mult, ALU.add))
    dve(p, cx, lambda e: e.tensor_scalar_add(S2, S2, float(PI_HALF)))
    act(p, cx, lambda e: e.activation(out=cs, in_=S2, func=ACTF.Sin), wait=p.wv(cx.s_dv))
    p.emit("gpsimd", lambda e: e.iota(S1, [[1, CH]], base=t0 - 2048, channel_multiplier=0,
                                      allow_small_or_imprecise_dtypes=True),
           wait=p.wv(cx.s_dv), inc=(cx.s_gp, 1))
    dve(p, cx, lambda e: e.tensor_scalar_mul(S1, S1, 1.0 / 256.0), wait=p.wv(cx.s_gp))
    dve(p, cx, lambda e: e.tensor_scalar_mul(S1, S1, sm(cx, SM_LNB)))
    act(p, cx, lambda e: e.activation(out=scq, in_=S1, func=ACTF.Exp), wait=p.wv(cx.s_dv))
    act(p, cx, lambda e: e.activation(out=sck, in_=S1, func=ACTF.Exp, scale=ccol(p, cx, -1.0)))
    dve(p, cx, lambda e: e.tensor_scalar_mul(scq, scq, 8.0), wait=p.wv(cx.s_ac))
    w = p.wv(cx.s_ac)
    csq = cs
    snq = sn
    csk = _reg(cx, FF_TSC, CH)
    snk = _reg(cx, FF_TSC + 2048, CH)
    dve(p, cx, lambda e: e.tensor_tensor(csk, cs, sck, ALU.mult), wait=w)
    dve(p, cx, lambda e: e.tensor_tensor(snk, sn, sck, ALU.mult))
    dve(p, cx, lambda e: e.tensor_tensor(csq, cs, scq, ALU.mult))
    dve(p, cx, lambda e: e.tensor_tensor(snq, sn, scq, ALU.mult))
    dve(p, cx, lambda e: e.tensor_copy(scq, csk))
    dve(p, cx, lambda e: e.tensor_copy(sck, snk))
    return dict(csq=csq, snq=snq, csk=scq, snk=sck)




def _af(cx, g, c0, n):
    return cx.slab[:, (A_F // 4) + g * 2049 + c0:(A_F // 4) + g * 2049 + c0 + n]


def stage_ffn(p, cx, h, only=None):
    d = cx.dram
    p.cur_stage = "ffn"
    p.barrier()
    wdv = p.wv(cx.s_dv, cx.s_ac, cx.s_pe, cx.s_st, cx.s_ld)
    if h == 0:
        for g in range(8):
            dve(p, cx, lambda e, g=g: e.memset(_af(cx, g, 0, 1), 0.0), wait=wdv if g == 0 else ())
        for g in range(8):
            dve(p, cx, lambda e, o=cx.small[:, SM_AHALO + g:SM_AHALO + g + 1],
                i=_af(cx, g, T_HALF, 1): e.tensor_copy(o, i))
    else:
        for g in range(8):
            dve(p, cx, lambda e, o=_af(cx, g, 0, 1),
                i=cx.small[:, SM_AHALO + g:SM_AHALO + g + 1]: e.tensor_copy(o, i),
                wait=wdv if g == 0 else ())
    for l in range(2):
        stage_attn_layer(p, cx, h, l)
        if only == "attn0":
            return
        stage_mlp_layer(p, cx, h, l)
        if only == "l0":
            return


def stage_attn_layer(p, cx, h, l, p1_only=False):
    d = cx.dram
    p.cur_stage = "attn"
    KB = lambda off, n: _regb(cx, off, n)
    wub = p.wv(cx.s_pe, cx.s_dv, cx.s_ac, cx.s_st, cx.s_ld)
    for kk in range(8):
        gdma(p, cx, KB(FF_WQ + kk * 3072, 1536),
             d[f"wqkv{l}"][kk * 128:(kk + 1) * 128, :], wait=wub)
    for kk in range(4):
        gdma(p, cx, KB(FF_WO + kk * 2048, 1024),
             d[f"wo{l}"][kk * 128:(kk + 1) * 128, :], wait=wub)
    wv0 = p.wv(cx.s_pe, cx.s_ac, cx.s_dv, cx.s_st)
    if h == 0:
        dve(p, cx, lambda e: e.memset(KB(FF_KF, 512), 0.0), wait=wv0)
        dve(p, cx, lambda e: e.memset(KB(FF_KF + 5120, 512), 0.0))
        dve(p, cx, lambda e: e.memset(KB(FF_KF + 10240, 512), 0.0))
        dve(p, cx, lambda e: e.memset(KB(FF_KF + 15360, 512), 0.0))
        for t4 in range(4):
            dve(p, cx, lambda e, t4=t4: e.memset(KB(FF_VT + t4 * 1024, 512), 0.0))
    else:
        for g in range(4):
            ldma(p, cx, KB(FF_KF + g * 5120, 512),
                 d["kcar"][l, g * 128:(g + 1) * 128, :], wait=wv0)
        for t4 in range(4):
            ldma(p, cx, KB(FF_VT + t4 * 1024, 512),
                 d["vcar"][l, t4 * 128:(t4 + 1) * 128, :], wait=wv0)
    for c in range(NCH):
        p.barrier()
        t0g = h * T_HALF + c * CH
        tabs = tables_chunk(p, cx, t0g)
        wab = p.wv(cx.s_pe, cx.s_ac)
        for g in range(8):
            dve(p, cx, lambda e, o=KB(FF_ABF + g * 1032, 513), a=_af(cx, g, c * CH, 513):
                e.tensor_copy(o, a), wait=wab)
        wqb = {0: p.wv(cx.s_dv, cx.s_ac, cx.s_st), 1: p.wv(cx.s_dv, cx.s_ac, cx.s_st)}
        for m in range(12):
            bank = cx.PS[:, (m % 2) * 512:(m % 2) * 512 + 512]
            wmm = wqb[m % 2]
            for kk in range(8):
                rhs = (KB(FF_ABF + kk * 1032, 513)[:, 1:513] if kk < 4
                       else KB(FF_ABF + kk * 1032, 513)[:, 0:512])
                lhsT = KB(FF_WQ + kk * 3072 + m * 256, 128)
                pe(p, cx, lambda e, o=bank, lt=lhsT, r=rhs, kk=kk:
                   e.matmul(o, lt, r, start=(kk == 0), stop=(kk == 7)), wait=wmm if kk == 0 else ())
            if m < 8:
                isq = m < 4
                tbl_cs = tabs["csq" if isq else "csk"]
                tbl_sn = tabs["snq" if isq else "snk"]
                scol = cx.small[:, SM_QS[l]:SM_QS[l] + 1] if isq else cx.small[:, SM_KS[l]:SM_KS[l] + 1]
                slot = KB(FF_SQB + (m % 4) * 1024, 512)
                act(p, cx, lambda e, s=slot, b=bank: e.activation(out=s, in_=b, func=ACTF.Square),
                    wait=p.wv(cx.s_pe))
                pe(p, cx, lambda e, s=slot: e.matmul(cx.PS[:, 1024:1536], cx.hmask_b[:, :], s,
                                                     start=True, stop=True), wait=p.wv(cx.s_ac, cx.s_dv))
                RRS = _reg(cx, FF_RRS, 512)
                WSC = _reg(cx, FF_WSCR, 512)
                Q2 = _reg(cx, FF_Q2, 512)
                UB = KB(FF_UB, 512)
                dve(p, cx, lambda e: e.tensor_scalar_max(RRS, cx.PS[:, 1024:1536], 1e-24),
                    wait=p.wv(cx.s_pe))
                act(p, cx, lambda e: e.activation(out=WSC, in_=RRS, func=ACTF.Ln), wait=p.wv(cx.s_dv))
                act(p, cx, lambda e: e.activation(out=RRS, in_=WSC, func=ACTF.Exp,
                                                  scale=ccol(p, cx, -0.5)))
                dve(p, cx, lambda e, b=bank, sc=scol: e.scalar_tensor_tensor(Q2, b, sc, RRS,
                                                                             ALU.mult, ALU.mult),
                    wait=p.wv(cx.s_ac))
                dve(p, cx, lambda e, t=tbl_sn: e.tensor_tensor(UB, Q2, t, ALU.mult),
                    wait=p.wv(cx.s_pe))
                pe(p, cx, lambda e: e.matmul(cx.PS[:, 1536:2048], cx.rot_b[:, :], UB,
                                             start=True, stop=True), wait=p.wv(cx.s_dv))
                dve(p, cx, lambda e, t=tbl_cs: e.tensor_tensor(WSC, Q2, t, ALU.mult))
                if isq:
                    dst = KB(FF_QH + m * 4096, 2048)[:, c * CH:(c + 1) * CH]
                else:
                    dst = KB(FF_KF + (m - 4) * 5120, 2560)[:, 512 + c * CH:512 + (c + 1) * CH]
                dve(p, cx, lambda e, o=dst: e.tensor_tensor(o, WSC, cx.PS[:, 1536:2048], ALU.add),
                    wait=p.wv(cx.s_pe, cx.s_st))
                wqb[m % 2] = p.wv(cx.s_dv, cx.s_ac, cx.s_st)
            else:
                mv = m - 8
                slot = KB(FF_SQB + (mv % 4) * 1024, 512)
                dve(p, cx, lambda e, s=slot, b=bank: e.tensor_copy(s, b), wait=p.wv(cx.s_pe))
                for b4 in range(4):
                    pe(p, cx, lambda e, s=slot, b4=b4,
                       o=cx.PSB[:, (m % 2) * 512 + b4 * 128:(m % 2) * 512 + (b4 + 1) * 128]:
                       e.transpose(o, s[:, b4 * 128:(b4 + 1) * 128], cx.ident_b[:, :]),
                       wait=p.wv(cx.s_dv))
                for b4 in range(4):
                    dst = KB(FF_VT + (4 + c * 4 + b4) * 1024, 512)[:, mv * 128:(mv + 1) * 128]
                    dve(p, cx, lambda e, o=dst,
                        i=cx.PSB[:, (m % 2) * 512 + b4 * 128:(m % 2) * 512 + (b4 + 1) * 128]:
                        e.tensor_copy(o, i), wait=p.wv(cx.s_pe, cx.s_st))
                wqb[m % 2] = p.wv(cx.s_dv, cx.s_ac, cx.s_st)
    if p1_only:
        return
    if h == 0:
        wkv = p.wv(cx.s_dv, cx.s_ac, cx.s_pe)
        for g in range(4):
            gdma(p, cx, d["kcar"][l, g * 128:(g + 1) * 128, :],
                 KB(FF_KF + g * 5120, 2560)[:, T_HALF:T_HALF + 512], wait=wkv)
        for t4 in range(4):
            gdma(p, cx, d["vcar"][l, t4 * 128:(t4 + 1) * 128, :],
                 KB(FF_VT + (16 + t4) * 1024, 512), wait=wkv)
    p.cur_stage = "p2"
    for c in range(NCH):
        p.barrier()
        for s4 in range(4):
            qt = c * 4 + s4
            for hp in range(4):
                qh_t = KB(FF_QH + hp * 4096, 2048)
                kf_t = KB(FF_KF + hp * 5120, 2560)
                wS = {}
                for hh in range(2):
                    sb = hh * 1024
                    lhsT = qh_t[hh * 64:(hh + 1) * 64, qt * 128:qt * 128 + 128]
                    wmm = p.wv(cx.s_ac, cx.s_dv)
                    pe(p, cx, lambda e, lt=lhsT, r=kf_t[hh * 64:(hh + 1) * 64, qt * 128:qt * 128 + 512], sb=sb:
                       e.matmul(cx.PS[:, sb:sb + 512], lt, r, start=True, stop=False), wait=wmm)
                    if h == 0 and qt < 4:
                        for bb in range(4 - qt):
                            pe(p, cx, lambda e, sb=sb, bb=bb:
                               e.matmul(cx.PS[:, sb + bb * 128:sb + (bb + 1) * 128], cx.ident_b[:, :],
                                        cx.negb_b[:, :], start=False, stop=False))
                    pe(p, cx, lambda e, sb=sb: e.matmul(cx.PS[:, sb:sb + 128], cx.ident_b[:, :],
                                                        cx.triu_b[:, :], start=False, stop=True))
                    pe(p, cx, lambda e, lt=lhsT, r=kf_t[hh * 64:(hh + 1) * 64, qt * 128 + 512:qt * 128 + 640], sb=sb:
                       e.matmul(cx.PS[:, sb + 512:sb + 640], lt, r, start=True, stop=False))
                    pe(p, cx, lambda e, sb=sb: e.matmul(cx.PS[:, sb + 512:sb + 640], cx.ident_b[:, :],
                                                        cx.tril_b[:, :], start=False, stop=True))
                    wS[hh] = p.wv(cx.s_pe)
                for hh in range(2):
                    sb = hh * 1024
                    sl = (qt * 8 + hp * 2 + hh) % 4
                    NMX = cx.small[:, 2052 + sl:2053 + sl]
                    RS = cx.small[:, 2056 + sl:2057 + sl]
                    RC = cx.small[:, 2060 + sl:2061 + sl]
                    dve(p, cx, lambda e, sb=sb, o=NMX: e.tensor_reduce(o, cx.PS[:, sb:sb + 640],
                                                                       AX.X, ALU.max, negate=True),
                        wait=wS[hh])
                    PSC = KB(FF_PSC + hh * 1280, 640)
                    act(p, cx, lambda e, sb=sb, o=PSC, b=NMX, a=RS:
                        e.activation(out=o, in_=cx.PS[:, sb:sb + 640], func=ACTF.Exp, bias=b,
                                     accum_out=a), wait=p.wv(cx.s_dv) + wS[hh])
                    dve(p, cx, lambda e, o=RC, i=RS: e.reciprocal(o, i), wait=p.wv(cx.s_ac))
                    vnop(p, cx)
                    dve(p, cx, lambda e, o=PSC, i=RC: e.tensor_scalar_mul(o, o, i))
                    for b5 in range(5):
                        pe(p, cx, lambda e, i=PSC, b5=b5, hh=hh:
                           e.transpose(cx.PSB[:, hh * 1024 + b5 * 128:hh * 1024 + (b5 + 1) * 128],
                                       i[:, b5 * 128:(b5 + 1) * 128], cx.ident_b[:, :]),
                           wait=p.wv(cx.s_dv))
                    PT = KB(FF_PT + hh * 1280, 640)
                    dve(p, cx, lambda e, o=PT, hh=hh: e.tensor_copy(o, cx.PSB[:, hh * 1024:hh * 1024 + 640]),
                        wait=p.wv(cx.s_pe))
                    for b5 in range(5):
                        vt_t = KB(FF_VT + (qt + b5) * 1024, 512)
                        lhsT_v = vt_t[:, hp * 128 + hh * 64:hp * 128 + hh * 64 + 64]
                        pe(p, cx, lambda e, lt=lhsT_v, r=PT[:, b5 * 128:(b5 + 1) * 128], hh=hh, b5=b5:
                           e.matmul(cx.PS[hh * 64:(hh + 1) * 64, 2048:2176], lt, r,
                                    start=(b5 == 0), stop=(b5 == 4)),
                           wait=p.wv(cx.s_dv) if b5 == 0 else ())
                ob = KB(FF_OB + hp * 4096, 2048)
                dve(p, cx, lambda e, o=ob[:, qt * 128:(qt + 1) * 128]: e.tensor_copy(o, cx.PS[:, 2048:2176]),
                    wait=p.wv(cx.s_pe))
    p.cur_stage = "wo"
    p.barrier()
    for c in range(NCH):
        for m in range(8):
            bank = cx.PS[:, (m % 2) * 512:(m % 2) * 512 + 512]
            wmm = p.wv(cx.s_dv, cx.s_ac)
            for kk in range(4):
                lhsT = KB(FF_WO + kk * 2048 + m * 256, 128)
                rhs = KB(FF_OB + kk * 4096, 2048)[:, c * CH:(c + 1) * CH]
                pe(p, cx, lambda e, o=bank, lt=lhsT, r=rhs, kk=kk:
                   e.matmul(o, lt, r, start=(kk == 0), stop=(kk == 3)), wait=wmm if kk == 0 else ())
            asl = _af(cx, m, c * CH + 1, CH)
            dve(p, cx, lambda e, o=asl, b=bank: e.tensor_tensor(o, b, o, ALU.add), wait=p.wv(cx.s_pe))
    if h == 0:
        for g in range(8):
            s0 = SM_AHALO + (2 * l + 1) * 8 + g
            dve(p, cx, lambda e, o=cx.small[:, s0:s0 + 1], i=_af(cx, g, T_HALF, 1):
                e.tensor_copy(o, i), wait=p.wv(cx.s_dv) if g == 0 else ())


def stage_mlp_layer(p, cx, h, l):
    d = cx.dram
    p.cur_stage = "mlp"
    KB = lambda off, n: _regb(cx, off, n)
    if h == 1:
        for g in range(8):
            s0 = SM_AHALO + (2 * l + 1) * 8 + g
            dve(p, cx, lambda e, o=_af(cx, g, 0, 1), i=cx.small[:, s0:s0 + 1]:
                e.tensor_copy(o, i), wait=p.wv(cx.s_dv, cx.s_pe) if g == 0 else ())
    p.barrier()
    for cm in range(NCH):
        wab = p.wv(cx.s_pe, cx.s_ac, cx.s_st)
        ab2 = [KB(FF_ABF2 + g * 1032, 513) for g in range(8)]
        for g in range(8):
            dve(p, cx, lambda e, o=ab2[g], a=_af(cx, g, cm * CH, 513):
                e.tensor_copy(o, a), wait=wab)
        sh_ap = [ab2[g][:, 1:513] if g < 4 else ab2[g][:, 0:512] for g in range(8)]
        ln_stats(p, cx, sh_ap, FF_SQ45)
        ln_rows(p, cx)
        wmr = p.wv(cx.s_pe)
        for g in range(8):
            tmp = _reg(cx, FF_TMP45 + g * 2048, 512)
            src = (_af(cx, g, cm * CH + 1, CH) if g < 4 else _af(cx, g, cm * CH, CH))
            dve(p, cx, lambda e, t=tmp, s=src: e.scalar_tensor_tensor(t, cx.PS[:, 1024:1536],
                                                                      -1.0, s, ALU.mult, ALU.add),
                wait=wmr if g == 0 else ())
            dve(p, cx, lambda e, t=tmp: e.tensor_tensor(t, t, cx.PS[:, 1536:2048], ALU.mult))
            dst = KB(FF_SHLN + g * 4096, 2048)[:, cm * CH:(cm + 1) * CH]
            dve(p, cx, lambda e, o=dst, t=tmp, g=g: e.tensor_scalar(
                o, t, cx.small[:, SM_LG[l] + g:SM_LG[l] + g + 1],
                cx.small[:, SM_LB[l] + g:SM_LB[l] + g + 1], ALU.mult, ALU.add))
    for hp2 in range(2):
        wst = p.wv(cx.s_pe, cx.s_dv, cx.s_ac, cx.s_st, cx.s_ld)
        for kk in range(8):
            for side in range(2):
                gdma(p, cx, KB(FF_W1 + (kk * 22 + side * 11) * 256, 1408),
                     d[f"w1{l}"][kk * 128:(kk + 1) * 128,
                                 side * HIDP + hp2 * 1408:side * HIDP + hp2 * 1408 + 1408],
                     wait=wst)
        for kl in range(11):
            gdma(p, cx, KB(FF_W2 + kl * 2048, 1024),
                 d[f"w2{l}"][(hp2 * 11 + kl) * 128:(hp2 * 11 + kl) * 128 + 128, :], wait=wst)
        for cm in range(NCH):
            p.barrier()
            wbank = {0: p.wv(cx.s_dv, cx.s_ac, cx.s_st), 1: p.wv(cx.s_dv, cx.s_ac, cx.s_st)}
            for i in range(11):
                gi = hp2 * 11 + i
                ab_ = cx.PS[:, (i % 2) * 1024:(i % 2) * 1024 + 512]
                gb_ = cx.PS[:, (i % 2) * 1024 + 512:(i % 2) * 1024 + 1024]
                wmm = wbank[i % 2]
                for kk in range(8):
                    rhs = KB(FF_SHLN + kk * 4096, 2048)[:, cm * CH:(cm + 1) * CH]
                    pe(p, cx, lambda e, o=ab_, lt=KB(FF_W1 + (kk * 22 + i) * 256, 128), r=rhs, kk=kk:
                       e.matmul(o, lt, r, start=(kk == 0), stop=(kk == 7)), wait=wmm if kk == 0 else ())
                for kk in range(8):
                    rhs = KB(FF_SHLN + kk * 4096, 2048)[:, cm * CH:(cm + 1) * CH]
                    pe(p, cx, lambda e, o=gb_, lt=KB(FF_W1 + (kk * 22 + 11 + i) * 256, 128), r=rhs, kk=kk:
                       e.matmul(o, lt, r, start=(kk == 0), stop=(kk == 7)))
                GLB = KB(FF_GLB, 512)
                act(p, cx, lambda e, b=gb_, gi=gi: e.activation(
                    out=GLB, in_=b, func=ACTF.Gelu,
                    bias=cx.small[:, SM_B1[l] + 22 + gi:SM_B1[l] + 22 + gi + 1]),
                    wait=p.wv(cx.s_pe, cx.s_dv))
                hp_t = KB(FF_HP + i * 1024, 512)
                dve(p, cx, lambda e, o=hp_t, b=ab_, gi=gi: e.scalar_tensor_tensor(
                    o, b, cx.small[:, SM_B1[l] + gi:SM_B1[l] + gi + 1], GLB, ALU.add, ALU.mult),
                    wait=p.wv(cx.s_ac, cx.s_pe))
                wbank[i % 2] = p.wv(cx.s_dv, cx.s_ac, cx.s_st)
            for mp in range(2):
                for m4 in range(4):
                    m = mp * 4 + m4
                    bank = cx.PS[:, m4 * 512:(m4 + 1) * 512]
                    wmm = p.wv(cx.s_dv, cx.s_ac)
                    for kl in range(11):
                        pe(p, cx, lambda e, o=bank, lt=KB(FF_W2 + kl * 2048 + m * 256, 128),
                           r=KB(FF_HP + kl * 1024, 512), kl=kl:
                           e.matmul(o, lt, r, start=(kl == 0), stop=(kl == 10)),
                           wait=wmm if kl == 0 else ())
                for m4 in range(4):
                    m = mp * 4 + m4
                    bank = cx.PS[:, m4 * 512:(m4 + 1) * 512]
                    asl = _af(cx, m, cm * CH + 1, CH)
                    if hp2 == 0:
                        dve(p, cx, lambda e, o=asl, b=bank: e.tensor_tensor(o, b, o, ALU.add),
                            wait=p.wv(cx.s_pe))
                    else:
                        dve(p, cx, lambda e, o=asl, b=bank, m=m: e.scalar_tensor_tensor(
                            o, b, cx.small[:, SM_B2[l] + m:SM_B2[l] + m + 1], o, ALU.add, ALU.add),
                            wait=p.wv(cx.s_pe))
    if h == 0 and l == 0:
        for g in range(8):
            s0 = SM_AHALO + 16 + g
            dve(p, cx, lambda e, o=cx.small[:, s0:s0 + 1], i=_af(cx, g, T_HALF, 1):
                e.tensor_copy(o, i), wait=p.wv(cx.s_dv) if g == 0 else ())
    if h == 1 and l == 0:
        for g in range(8):
            s0 = SM_AHALO + 16 + g
            dve(p, cx, lambda e, o=_af(cx, g, 0, 1), i=cx.small[:, s0:s0 + 1]:
                e.tensor_copy(o, i), wait=p.wv(cx.s_dv, cx.s_pe) if g == 0 else ())


def stage_combine(p, cx, h):
    d = cx.dram
    p.cur_stage = "combine"
    p.barrier()
    for c in range(NCH):
        t0 = h * T_HALF + c * CH
        for g in range(8):
            xch = _reg(cx, FF_XCH + g * 2048, CH)
            ldma(p, cx, xch, d["Xbuf"][g * 128:(g + 1) * 128, t0:t0 + CH],
                 wait=p.wv(*p.allsems))
            dve(p, cx, lambda e, o=xch, a=_af(cx, g, c * CH + 1, CH): e.scalar_tensor_tensor(
                o, a, 0.5, o, ALU.mult, ALU.add), wait=p.wv(*p.allsems))
            gdma(p, cx, d["Xbuf"][g * 128:(g + 1) * 128, t0:t0 + CH], xch,
                 wait=p.wv(*p.allsems))
    p.emit("sync", lambda e: e.nop(), wait=p.wv(cx.s_st))



def stage_x_in(p, cx):
    d = cx.dram
    tm = FF_QH
    for c in range(T_TOT // 512):
        wld = p.wv(cx.s_pe)
        for s in range(4):
            ldma(p, cx, _reg(cx, tm + s * 4096, 1024),
                 d["hs"][c * 512 + s * 128: c * 512 + (s + 1) * 128, :], wait=wld)
        for g in range(8):
            bank = cx.PS[:, (g % 2) * 512:(g % 2) * 512 + 512]
            for s in range(4):
                src = _reg(cx, tm + s * 4096 + g * 512, 128)
                pe(p, cx, lambda e, o=bank[:, s * 128:(s + 1) * 128], i=src:
                   e.transpose(o, i, cx.ident_f[:, :]), wait=p.wv(cx.s_ld, cx.s_dv))
            dst = _reg(cx, FF_OB + g * 2048, 512)
            dve(p, cx, lambda e, o=dst, i=bank: e.tensor_copy(o, i),
                wait=p.wv(cx.s_pe, cx.s_st))
            gdma(p, cx, d["Xbuf"][g * 128:(g + 1) * 128, c * 512:(c + 1) * 512], dst,
                 wait=p.wv(cx.s_dv))
    p.emit("sync", lambda e: e.nop(), wait=p.wv(cx.s_st))


def stage_x_out(p, cx):
    d = cx.dram
    tm = FF_QH
    fm = FF_OB
    for c in range(T_TOT // 512):
        wg = p.wv(cx.s_st, cx.s_pe)
        for g in range(8):
            ldma(p, cx, _reg(cx, fm + g * 2048, 512),
                 d["Xbuf"][g * 128:(g + 1) * 128, c * 512:(c + 1) * 512], wait=wg)
        for s in range(4):
            bank = cx.PS[:, (s % 2) * 512:(s % 2) * 512 + 512]
            for g in range(8):
                src = _reg(cx, fm + g * 2048 + s * 512, 128)
                pe(p, cx, lambda e, o=bank[:, (g % 4) * 128:(g % 4 + 1) * 128], i=src:
                   e.transpose(o, i, cx.ident_f[:, :]), wait=p.wv(cx.s_ld, cx.s_dv))
                if g % 4 == 3:
                    dst = _reg(cx, tm + s * 4096 + (g - 3) * 512, 512)
                    dve(p, cx, lambda e, o=dst, i=bank: e.tensor_copy(o, i), wait=p.wv(cx.s_pe))
            gdma(p, cx, d["out"][c * 512 + s * 128: c * 512 + (s + 1) * 128, :],
                 _reg(cx, tm + s * 4096, 1024), wait=p.wv(cx.s_dv))
    p.emit("gpsimd", lambda e: e.nop(), wait=p.wv(cx.s_st))


def dump_af_to_xbuf(p, cx, h):
    d = cx.dram
    for g in range(8):
        src = cx.slab[:, (A_F // 4) + g * 2049 + 1:(A_F // 4) + g * 2049 + 1 + T_HALF]
        gdma(p, cx, d["Xbuf"][g * 128:(g + 1) * 128, h * T_HALF:(h + 1) * T_HALF], src,
             wait=p.wv(cx.s_dv, cx.s_pe, cx.s_ac))
    p.emit("sync", lambda e: e.nop(), wait=p.wv(cx.s_st))



def build_program(debug=None):
    nc = bass.Bass("TRN2", target_bir_lowering=False, debug=False, detect_race_conditions=False)
    stack = contextlib.ExitStack()
    with stack:
        p = Prog(nc, stack)
        dram = {}
        for nm, shp in _IN_SHAPES.items():
            dram[nm] = nc.dram_tensor(nm, shp, F32, kind="ExternalInput").ap()
        dram["out"] = nc.dram_tensor("out", [T_TOT, D_MODEL], F32, kind="ExternalOutput").ap()
        for nm, shp in [("Xbuf", [D_MODEL, T_TOT]), ("xin_s", [D_INNER, T_TOT]),
                        ("xt_s", [D_INNER, T_TOT]), ("z_s", [D_INNER, T_TOT])]:
            dram[nm] = nc.dram_tensor(nm, shp, F32).ap()
        dram["lnx_s"] = nc.dram_tensor("lnx_s", [D_MODEL, T_TOT], F32).ap()
        for nm in ("kcar", "vcar"):
            dram[nm] = nc.dram_tensor(nm, [2, 512, 512], BF16).ap()

        cx = setup(p, dram)
        init_consts(p, cx)
        stage_x_in(p, cx)
        if debug == "xcopy":
            pass
        elif debug == "ln1":
            for h in range(N_HALF):
                stage_ln1(p, cx, h)
                dump_af_to_xbuf(p, cx, h)
        elif debug == "consts":
            wd = p.wv(cx.s_dv, cx.s_ac, cx.s_pe, cx.s_gp, cx.s_st, cx.s_ld)
            gdma(p, cx, dram["Xbuf"][0:128, 0:5], cx.small[:, SM_DIMIDX:SM_DIMIDX + 5], wait=wd)
            gdma(p, cx, dram["Xbuf"][0:128, 5:9], cx.small[:, SM_TMP:SM_TMP + 4], wait=wd)
            p.emit("sync", lambda e: e.nop(), wait=p.wv(cx.s_st))
        elif debug and debug.startswith("tbl"):
            tabs = tables_chunk(p, cx, int(debug[3:] or 0))
            for i, k in enumerate(("csq", "snq", "csk", "snk")):
                gdma(p, cx, dram["Xbuf"][i * 128:(i + 1) * 128, 0:CH], tabs[k],
                     wait=p.wv(cx.s_dv, cx.s_ac, cx.s_pe, cx.s_gp))
            p.emit("sync", lambda e: e.nop(), wait=p.wv(cx.s_st))
        elif debug == "p1":
            stage_ln1(p, cx, 0)
            for g in range(8):
                p.emit("vector", lambda e, g=g: e.memset(_af(cx, g, 0, 1), 0.0),
                       wait=p.wv(cx.s_dv, cx.s_ac, cx.s_pe), inc=(cx.s_dv, 1))
            stage_attn_layer(p, cx, 0, 0, p1_only=True)
            wd = p.wv(cx.s_dv, cx.s_ac, cx.s_pe)
            for g in range(4):
                qh = _regb(cx, FF_QH + g * 4096, 2048)
                dve(p, cx, lambda e, g=g, i=qh: e.tensor_copy(_reg(cx, FF_TSC, 512), i[:, 0:512]), wait=wd)
                gdma(p, cx, dram["Xbuf"][g * 128:(g + 1) * 128, 0:512], _reg(cx, FF_TSC, 512),
                     wait=p.wv(cx.s_dv))
            for g in range(4):
                kf = _regb(cx, FF_KF + g * 5120, 2560)
                dve(p, cx, lambda e, g=g, i=kf: e.tensor_copy(_reg(cx, FF_TSC, 512), i[:, 512:1024]),
                    wait=p.wv(cx.s_st))
                gdma(p, cx, dram["Xbuf"][512 + g * 128:512 + (g + 1) * 128, 0:512], _reg(cx, FF_TSC, 512),
                     wait=p.wv(cx.s_dv))
            for s in range(4):
                vt = _regb(cx, FF_VT + (4 + s) * 1024, 512)
                dve(p, cx, lambda e, i=vt: e.tensor_copy(_reg(cx, FF_TSC, 512), i),
                    wait=p.wv(cx.s_st))
                gdma(p, cx, dram["Xbuf"][512 + s * 128:512 + (s + 1) * 128, 512:1024],
                     _reg(cx, FF_TSC, 512), wait=p.wv(cx.s_dv))
            p.emit("sync", lambda e: e.nop(), wait=p.wv(cx.s_st))
        elif debug == "attn0":
            for h in range(N_HALF):
                stage_ln1(p, cx, h)
                stage_ffn(p, cx, h, only="attn0")
                dump_af_to_xbuf(p, cx, h)
        elif debug == "l0":
            for h in range(N_HALF):
                stage_ln1(p, cx, h)
                stage_ffn(p, cx, h, only="l0")
                dump_af_to_xbuf(p, cx, h)
        elif debug == "mamba":
            for h in range(N_HALF):
                stage_mamba(p, cx, h)
        elif debug == "ffn1":
            for h in range(N_HALF):
                stage_ln1(p, cx, h)
                stage_ffn(p, cx, h)
                dump_af_to_xbuf(p, cx, h)
        else:
            full_chain(p, cx)
        stage_x_out(p, cx)
        p.build()
    return nc


def full_chain(p, cx):
    for h in range(N_HALF):
        stage_ln1(p, cx, h)
        stage_ffn(p, cx, h)
        stage_combine(p, cx, h)
    for h in range(N_HALF):
        stage_mamba(p, cx, h)
    for h in range(N_HALF):
        stage_ln1(p, cx, h)
        stage_ffn(p, cx, h)
        stage_combine(p, cx, h)


def kernel(**inputs):
    pre = _prep_inputs(inputs)
    hs_full = np.ascontiguousarray(np.asarray(inputs["L0_attn_hidden_states"], np.float32))
    nc = build_program()
    in_maps = []
    for c in range(8):
        m = dict(pre)
        m["hs"] = hs_full[c // 2]
        in_maps.append(m)
    res = run_bass_kernel_spmd(nc, in_maps, core_ids=list(range(8)))
    outs = [res.results[c]["out"] for c in (0, 2, 4, 6)]
    return np.stack(outs, axis=0)



NM3 = T_HALF // TCM


def stage_mamba(p, cx, h):
    d = cx.dram
    KB = lambda off, n: _regb(cx, off, n)
    RG = lambda off, n: _reg(cx, off, n)
    p.barrier()
    p.cur_stage = "m1"

    wu = p.wv(cx.s_pe, cx.s_dv, cx.s_ac, cx.s_st, cx.s_ld)
    for kk in range(8):
        gdma(p, cx, KB(MB_WIN + kk * 8192, 4096),
             d["inT"][kk * 128:(kk + 1) * 128, :], wait=wu)
    if h == 0:
        dve(p, cx, lambda e: e.memset(cx.small[:, SM_STATE:SM_STATE + 256], 0.0), wait=wu)
    for c in range(NCH):
        p.barrier()
        t0 = h * T_HALF + c * CH
        xch = [RG(MB_XCH + g * 2048, CH) for g in range(8)]
        xb = [KB(MB_XB + g * 1024, CH) for g in range(8)]
        wld = p.wv(cx.s_dv, cx.s_ac, cx.s_pe, cx.s_st)
        for g in range(8):
            ldma(p, cx, xch[g], d["Xbuf"][g * 128:(g + 1) * 128, t0:t0 + CH], wait=wld)
            gdma(p, cx, xb[g], d["Xbuf"][g * 128:(g + 1) * 128, t0:t0 + CH], wait=wld)
        ln_stats(p, cx, xb, MB_SQ)
        ln_rows(p, cx, mamba=True)
        wmr = p.wv(cx.s_pe, cx.s_st)
        for g in range(8):
            lnx = RG(MB_LNX + g * 2048, CH)
            lnm = KB(MB_LNM + g * 1024, CH)
            nwc = cx.small[:, SM_NW + g:SM_NW + g + 1]
            dve(p, cx, lambda e, x_=xch[g]: e.scalar_tensor_tensor(x_, cx.PS[:, 1024:1024 + CH],
                                                                   -1.0, x_, ALU.mult, ALU.add),
                wait=wmr if g == 0 else ())
            dve(p, cx, lambda e, o=lnx, x_=xch[g]: e.tensor_tensor(o, x_, cx.PS[:, 0:CH], ALU.mult))
            dve(p, cx, lambda e, x_=xch[g]: e.tensor_tensor(x_, x_, cx.PS[:, 1536:1536 + CH], ALU.mult))
            dve(p, cx, lambda e, o=lnm, x_=xch[g], w=nwc: e.tensor_scalar_mul(o, x_, w))
            gdma(p, cx, d["lnx_s"][g * 128:(g + 1) * 128, t0:t0 + CH], lnx, wait=p.wv(cx.s_dv))
        wib = {0: p.wv(cx.s_dv, cx.s_ac, cx.s_st), 1: p.wv(cx.s_dv, cx.s_ac, cx.s_st)}
        for m in range(32):
            bank = cx.PS[:, (m % 2) * 512:(m % 2) * 512 + 512]
            wmm = wib[m % 2]
            for kk in range(8):
                lhsT = KB(MB_WIN + kk * 8192 + m * 256, 128)
                pe(p, cx, lambda e, o=bank, lt=lhsT, r=KB(MB_LNM + kk * 1024, CH), kk=kk:
                   e.matmul(o, lt, r, start=(kk == 0), stop=(kk == 7)), wait=wmm if kk == 0 else ())
            slot = RG(MB_XZS + (m % 4) * 2048, CH)
            dve(p, cx, lambda e, o=slot, b=bank: e.tensor_copy(o, b),
                wait=p.wv(cx.s_pe, cx.s_st))
            wib[m % 2] = p.wv(cx.s_dv, cx.s_ac, cx.s_st)
            dst = (d["xin_s"][(m % 16) * 128:(m % 16 + 1) * 128, t0:t0 + CH] if m < 16
                   else d["z_s"][(m - 16) * 128:(m - 15) * 128, t0:t0 + CH])
            gdma(p, cx, dst, slot, wait=p.wv(cx.s_dv))

    p.cur_stage = "m2"
    wu = p.wv(cx.s_pe, cx.s_dv, cx.s_ac, cx.s_st, cx.s_ld)
    p.emit("sync", lambda e: e.nop(), wait=p.wv(cx.s_st))
    ldma(p, cx, cx.small[:, SM_CW:SM_CW + 64], d["convw"][:, :], wait=wu)
    for kk in range(16):
        gdma(p, cx, KB(MB_WXP + kk * 224, 112), d["xpT"][kk * 128:(kk + 1) * 128, :], wait=wu)
    for c in range(NCH):
        p.barrier()
        t0 = h * T_HALF + c * CH
        wld = p.wv(cx.s_dv, cx.s_ac, cx.s_pe, cx.s_st)
        for g in range(16):
            tile = RG(MB_XIN + g * 2060, 515)
            if t0 == 0:
                dve(p, cx, lambda e, o=tile[:, 0:3]: e.memset(o, 0.0), wait=wld if g == 0 else ())
                ldma(p, cx, tile[:, 3:515], d["xin_s"][g * 128:(g + 1) * 128, 0:CH], wait=wld)
            else:
                ldma(p, cx, tile[:, 0:515], d["xin_s"][g * 128:(g + 1) * 128, t0 - 3:t0 + CH],
                     wait=wld)
        for g in range(16):
            tile = RG(MB_XIN + g * 2060, 515)
            xt = RG(MB_XT + g * 2048, CH)
            xtb = KB(MB_XTB + g * 1024, CH)
            wcv = p.wv(cx.s_ld, cx.s_dv, cx.s_ac, cx.s_pe, cx.s_st)
            dve(p, cx, lambda e, o=xt, i=tile, w=cx.small[:, SM_CW + g:SM_CW + g + 1]:
                e.tensor_scalar_mul(o, i[:, 0:CH], w), wait=wcv)
            for k in range(1, 4):
                dve(p, cx, lambda e, o=xt, i=tile, k=k,
                    w=cx.small[:, SM_CW + k * 16 + g:SM_CW + k * 16 + g + 1]:
                    e.scalar_tensor_tensor(o, i[:, k:k + CH], w, o, ALU.mult, ALU.add))
            act(p, cx, lambda e, o=xt, g=g: e.activation(
                out=o, in_=o, func=ACTF.Silu, bias=cx.small[:, SM_CB + g:SM_CB + g + 1]),
                wait=p.wv(cx.s_dv))
            dve(p, cx, lambda e, o=xtb, i=xt: e.tensor_copy(o, i), wait=p.wv(cx.s_ac))
            gdma(p, cx, d["xt_s"][g * 128:(g + 1) * 128, t0:t0 + CH], xt, wait=p.wv(cx.s_dv))
        bank = cx.PS[:, 1024:1536]
        wmm = p.wv(cx.s_dv, cx.s_ac)
        for kk in range(16):
            pe(p, cx, lambda e, lt=KB(MB_WXP + kk * 224, 112), r=KB(MB_XTB + kk * 1024, CH), kk=kk:
               e.matmul(cx.PS[0:112, 1024:1536], lt, r, start=(kk == 0), stop=(kk == 15)),
               wait=wmm if kk == 0 else ())
        dve(p, cx, lambda e, o=RG(MB_XP, 2048)[0:112, c * CH:(c + 1) * CH]:
            e.tensor_copy(o, cx.PS[0:112, 1024:1536]), wait=p.wv(cx.s_pe))

    p.cur_stage = "m3"
    wu = p.wv(cx.s_pe, cx.s_dv, cx.s_ac, cx.s_st, cx.s_ld)
    for kk in range(16):
        gdma(p, cx, KB(M3_WOUT + kk * 2048, 1024), d["outT"][kk * 128:(kk + 1) * 128, :], wait=wu)
    gdma(p, cx, KB(M3_WDT, 2048)[0:64, :], d["dtT"][0:64, :], wait=wu)
    dve(p, cx, lambda e: e.tensor_copy(KB(M3_DTRB, 2048)[0:64, :], RG(MB_XP, 2048)[0:64, :]),
        wait=wu)
    for c8 in range(NM3):
        p.barrier()
        t0 = h * T_HALF + c8 * TCM
        cs = c8 * TCM
        seln = RG(M3_DA, 2048)
        ldma(p, cx, seln, d["seln"][:, :], wait=p.wv(cx.s_dv, cx.s_ac, cx.s_pe, cx.s_st))
        for n in range(16):
            wmm = p.wv(cx.s_dv, cx.s_ac, cx.s_ld)
            pe(p, cx, lambda e, lt=seln[64:80, n * 128:(n + 1) * 128],
               r=RG(MB_XP, 2048)[64:80, cs:cs + TCM]:
               e.matmul(cx.PS[:, 2048:2048 + TCM], lt, r, start=True, stop=True), wait=wmm)
            dve(p, cx, lambda e, o=RG(M3_BBC + n * 1024, TCM): e.tensor_copy(o, cx.PS[:, 2048:2048 + TCM]),
                wait=p.wv(cx.s_pe))
            pe(p, cx, lambda e, lt=seln[96:112, n * 128:(n + 1) * 128],
               r=RG(MB_XP, 2048)[96:112, cs:cs + TCM]:
               e.matmul(cx.PS[:, 2560:2560 + TCM], lt, r, start=True, stop=True,
                        tile_position=(96, 0)))
            dve(p, cx, lambda e, o=RG(M3_CBC + n * 1024, TCM): e.tensor_copy(o, cx.PS[:, 2560:2560 + TCM]),
                wait=p.wv(cx.s_pe))
        for g in range(16):
            DT = RG(M3_DT, TCM)
            UM = RG(M3_UM, TCM)
            XTG = RG(M3_XTG, TCM)
            ZG = RG(M3_ZG, TCM)
            SZG = RG(M3_SZG, TCM)
            DA = RG(M3_DA, 16 * TCM)
            BH = RG(M3_BH, 16 * TCM)
            wld = p.wv(cx.s_dv, cx.s_ac, cx.s_st)
            ldma(p, cx, XTG, d["xt_s"][g * 128:(g + 1) * 128, t0:t0 + TCM], wait=wld)
            ldma(p, cx, ZG, d["z_s"][g * 128:(g + 1) * 128, t0:t0 + TCM], wait=wld)
            wmm = p.wv(cx.s_dv, cx.s_ac)
            pe(p, cx, lambda e, lt=KB(M3_WDT + g * 256, 128)[0:64, :], r=KB(M3_DTRB, 2048)[0:64, cs:cs + TCM]:
               e.matmul(cx.PS[:, 1024:1024 + TCM], lt, r, start=True, stop=True), wait=wmm)
            act(p, cx, lambda e, g=g: e.activation(out=RG(M3_SZG, TCM), in_=cx.PS[:, 1024:1024 + TCM],
                                                   func=ACTF.Exp,
                                                   bias=cx.small[:, SM_DTB + g:SM_DTB + g + 1]),
                wait=p.wv(cx.s_pe))
            act(p, cx, lambda e: e.activation(out=DT, in_=RG(M3_SZG, TCM), func=ACTF.Ln,
                                              bias=ccol(p, cx, 1.0)))
            dve(p, cx, lambda e, o=UM, i=XTG: e.tensor_tensor(o, DT, i, ALU.mult),
                wait=p.wv(cx.s_ac, cx.s_ld))
            for n in range(16):
                acol = cx.small[:, SM_ANEG + g * 16 + n:SM_ANEG + g * 16 + n + 1]
                act(p, cx, lambda e, o=DA[:, n * TCM:(n + 1) * TCM], a=acol:
                    e.activation(out=o, in_=DT, func=ACTF.Exp, scale=a),
                    wait=p.wv(cx.s_dv) if n == 0 else ())
            for n in range(16):
                dve(p, cx, lambda e, o=BH[:, n * TCM:(n + 1) * TCM], b=RG(M3_BBC + n * 1024, TCM):
                    e.tensor_tensor(o, UM, b, ALU.mult), wait=p.wv(cx.s_ac) if n == 0 else ())
            for n in range(16):
                st = cx.small[:, SM_STATE + g * 16 + n:SM_STATE + g * 16 + n + 1]
                sl = BH[:, n * TCM:(n + 1) * TCM]
                dve(p, cx, lambda e, o=sl, a=DA[:, n * TCM:(n + 1) * TCM], b=sl, s=st:
                    e.tensor_tensor_scan(o, a, b, s, ALU.mult, ALU.add))
            dve(p, cx, lambda e, o=cx.small[:, SM_STATE + g * 16:SM_STATE + (g + 1) * 16],
                i=BH[:, TCM - 1::TCM]: e.tensor_copy(o, i))
            dav = DA.rearrange("p (n t) -> p n t", n=16)
            da_t = DA.rearrange("p (t n) -> p t n", n=16)
            for n in range(16):
                dve(p, cx, lambda e, o=da_t[:, :, n], hh_=BH[:, n * TCM:(n + 1) * TCM],
                    cc=RG(M3_CBC + n * 1024, TCM): e.tensor_tensor(o, hh_, cc, ALU.mult))
            dve(p, cx, lambda e: e.tensor_reduce(UM, da_t, AX.X, ALU.add))
            dve(p, cx, lambda e, i=XTG, dc=cx.small[:, SM_DVEC + g:SM_DVEC + g + 1]:
                e.scalar_tensor_tensor(UM, i, dc, UM, ALU.mult, ALU.add))
            act(p, cx, lambda e, i=ZG: e.activation(out=SZG, in_=i, func=ACTF.Silu),
                wait=p.wv(cx.s_dv))
            dve(p, cx, lambda e, o=KB(M3_YSB + g * 512, TCM): e.tensor_tensor(o, UM, SZG, ALU.mult),
                wait=p.wv(cx.s_ac))
        for m in range(8):
            bank = cx.PS[:, (m % 2) * 512:(m % 2) * 512 + 512][:, 0:TCM]
            wmm = p.wv(cx.s_dv, cx.s_ac, cx.s_ld)
            for kk in range(16):
                pe(p, cx, lambda e, o=bank, lt=KB(M3_WOUT + kk * 2048 + m * 256, 128),
                   r=KB(M3_YSB + kk * 512, TCM), kk=kk:
                   e.matmul(o, lt, r, start=(kk == 0), stop=(kk == 15)),
                   wait=wmm if kk == 0 else ())
            xrm = RG(M3_DA + m * 1024, TCM)
            lnxc = RG(M3_DA + 8192 + m * 1024, TCM)
            wrd = p.wv(cx.s_dv, cx.s_ac, cx.s_st)
            ldma(p, cx, xrm, d["Xbuf"][m * 128:(m + 1) * 128, t0:t0 + TCM], wait=wrd)
            ldma(p, cx, lnxc, d["lnx_s"][m * 128:(m + 1) * 128, t0:t0 + TCM], wait=wrd)
            dve(p, cx, lambda e, o=lnxc, b=bank: e.tensor_tensor(o, b, o, ALU.add),
                wait=p.wv(cx.s_pe, cx.s_ld))
            dve(p, cx, lambda e, o=xrm, i=lnxc: e.tensor_tensor(o, i, o, ALU.add))
            gdma(p, cx, d["Xbuf"][m * 128:(m + 1) * 128, t0:t0 + TCM], xrm, wait=p.wv(cx.s_dv))
    p.emit("sync", lambda e: e.nop(), wait=p.wv(cx.s_st))
